# version 04
# baseline (speedup 1.0000x reference)
import numpy as np
import ml_dtypes
from contextlib import ExitStack

import concourse.bass as bass
import concourse.mybir as mybir
from concourse.bass_utils import run_bass_kernel_spmd

F32 = mybir.dt.float32
BF16 = mybir.dt.bfloat16
AF = mybir.ActivationFunctionType
ALU = mybir.AluOpType
BF = ml_dtypes.bfloat16

NCORES = 8
B = 2
T = 4096
D = 2048
TPC = 1024
HALO = 3
W = TPC + HALO
NKT = D // 128

LN_EPS = 1e-5
RMS_EPS = 1e-5
L2_EPS = 1e-6
H = 32
DBG = False



class SemW:
    def __init__(self, h):
        self.h = h
        self.n = 0

    def inc(self, k=1):
        self.n += k
        return self.n


class P:

    def __init__(self, name):
        self.name = name
        self.nc = bass.Bass("TRN2", target_bir_lowering=False, debug=False,
                            detect_race_conditions=False)
        self.q = {k: [] for k in ("t", "v", "s", "g", "y")}
        self.ctx = ExitStack()
        self.input_names = []
        self.output_names = []
        self._auto = {}
        self._last_sync = {}
        self._prefix = ""
        self._scope = None
        self._sems = {}
        self._psums = {}
        self._inps = {}
        self._dma_sems = set()
        self._bar = None
        self._barwait = []
        self._pending = {}

    def sem(self, name):
        if name not in self._sems:
            self._sems[name] = SemW(self.ctx.enter_context(self.nc.semaphore(name)))
        return self._sems[name]

    def sbuf(self, name, shape, dt=F32):
        ctx = self._scope if self._scope is not None else self.ctx
        return ctx.enter_context(
            self.nc.sbuf_tensor(self._prefix + name, list(shape), dt))

    def psum(self, name, shape=(128, 512), dt=F32):
        if name not in self._psums:
            self._psums[name] = self.ctx.enter_context(
                self.nc.psum_tensor(name, list(shape), dt))
        return self._psums[name]

    def inp(self, name, shape, dt=F32):
        if name not in self._inps:
            h = self.nc.dram_tensor(name, list(shape), dt, kind="ExternalInput")
            self.input_names.append(name)
            self._inps[name] = h.ap()
        return self._inps[name]

    def outp(self, name, shape, dt=F32):
        h = self.nc.dram_tensor(name, list(shape), dt, kind="ExternalOutput")
        self.output_names.append(name)
        return h.ap()

    def dram(self, name, shape, dt=F32):
        return self.nc.dram_tensor(self._prefix + name, list(shape), dt,
                                   kind="Internal").ap()

    def io_inp(self, io, name, shape, dt=F32):
        if io is not None and name in io:
            return io[name]
        return self.inp(name, shape, dt)

    def io_out(self, io, name, shape, dt=F32):
        if io is not None and name in io:
            return io[name]
        return self.outp(name, shape, dt)

    def init_barrier(self):
        self._bar = self.sem("sbar")
        self._barc = self.ctx.enter_context(
            self.nc.sbuf_tensor("barc", [1, 2], F32))
        self.psum("ps0")
        self.E("v", lambda e: e.memzero(self._barc[:, :]))

    def stage(self, prefix):
        self._prefix = prefix
        self._scope = ExitStack()

    def barrier(self):
        waits = []
        for qn in ("v", "s"):
            ls = self._last_sync.get(qn)
            if ls is not None and ls[1] > 0:
                waits.append((ls[0], ls[1]))
        waits += [(s, s.n) for s in self._dma_sems if s.n > 0]
        assert self._bar is not None, "call init_barrier() before stages"
        bar = self._bar
        ps0 = self.psum("ps0")
        self.E("t", lambda e: e.matmul(
            ps0[0:1, 0:1], self._barc[0:1, 0:1], self._barc[0:1, 1:2],
            start=True, stop=True), wait=waits, inc=bar)
        self._barwait = [(bar, bar.n)]
        self._pending = {q: True for q in self.q}
        self._dma_sems = set()
        if self._scope is not None:
            self._scope.close()
            self._scope = None
        self._prefix = ""

    def E(self, qn, mk, wait=(), inc=None, inck=1):
        wait = tuple(wait)
        if self._pending.get(qn):
            wait = tuple(self._barwait) + wait
            self._pending[qn] = False
        if inc is not None:
            inc.inc(inck)
            if qn in ("y", "g"):
                self._dma_sems.add(inc)
        chain = None
        eff_inc, eff_inck = inc, inck
        if qn in ("v", "s"):
            chain = self._last_sync.get(qn)
            if inc is None:
                if qn not in self._auto:
                    self._auto[qn] = self.sem(f"auto_{qn}")
                eff_inc, eff_inck = self._auto[qn], 1
                eff_inc.inc(1)
            self._last_sync[qn] = (eff_inc, eff_inc.n)

        def f(eng, mk=mk, wait=wait, eff_inc=eff_inc, eff_inck=eff_inck, chain=chain):
            if chain is not None and chain[1] > 0:
                eng.wait_ge(chain[0].h, chain[1])
            for (s, v) in wait:
                if v > 0:
                    eng.wait_ge(s.h, v)
            ins = mk(eng)
            if eff_inc is not None:
                ins.then_inc(eff_inc.h, eff_inck)
        self.q[qn].append(f)

    def finish(self):
        if self._scope is not None:
            self._scope.close()
            self._scope = None
        blk = self.ctx.enter_context(self.nc.Block())
        qs = self.q
        if qs["t"]:
            blk.tensor(lambda e: [f(e) for f in qs["t"]])
        if qs["v"]:
            blk.vector(lambda e: [f(e) for f in qs["v"]])
        if qs["s"]:
            blk.scalar(lambda e: [f(e) for f in qs["s"]])
        if qs["g"]:
            blk.gpsimd(lambda e: [f(e) for f in qs["g"]])
        if qs["y"]:
            blk.sync(lambda e: [f(e) for f in qs["y"]])
        self.ctx.close()
        return self.nc


def col_groups(width, step=512):
    return [(c, min(width, c + step)) for c in range(0, width, step)]



def emit_ln_rows2(p, xs, width, ones_c, ones_r, rows, RSTD, R2, TMP, x2bufs, ps,
                  sq, smm, srow, eps_ap, extra_wait=()):
    cgs = col_groups(width)
    assert 2 * len(cgs) <= len(ps)
    mm_done_by_kt = {}
    for kt in range(NKT):
        x2 = x2bufs[kt % 2]
        w = list(extra_wait) if kt == 0 else []
        if kt >= 2:
            w.append((smm, mm_done_by_kt[kt - 2]))
        p.E("s", lambda e, kt=kt, x2=x2: e.activation(x2[:], xs[kt][:, 0:width], AF.Square),
            wait=w, inc=sq)
        sq_t = sq.n
        for cg, (c0, c1) in enumerate(cgs):
            p.E("t", lambda e, kt=kt, c0=c0, c1=c1, cg=cg: e.matmul(
                ps[2 * cg][0:1, 0:c1 - c0], ones_c[:, 0:1], xs[kt][:, c0:c1],
                start=(kt == 0), stop=(kt == NKT - 1)),
                wait=list(extra_wait) if (kt == 0 and cg == 0) else [])
            p.E("t", lambda e, kt=kt, c0=c0, c1=c1, cg=cg, x2=x2, sq_t=sq_t: e.matmul(
                ps[2 * cg + 1][0:1, 0:c1 - c0], ones_c[:, 0:1], x2[:, c0:c1],
                start=(kt == 0), stop=(kt == NKT - 1)),
                wait=[(sq, sq_t)],
                inc=smm)
        mm_done_by_kt[kt] = smm.n
    stats_done = smm.n
    MU = R2
    for cg, (c0, c1) in enumerate(cgs):
        n = c1 - c0
        p.E("v", lambda e, cg=cg, c0=c0, n=n: e.tensor_scalar(
            rows[0:1, MU + c0:MU + c0 + n], ps[2 * cg][0:1, 0:n], 1.0 / D, None, ALU.mult),
            wait=[(smm, stats_done)] if cg == 0 else [])
        p.E("v", lambda e, cg=cg, c0=c0, n=n: e.tensor_scalar(
            rows[0:1, RSTD + c0:RSTD + c0 + n], ps[2 * cg + 1][0:1, 0:n], 1.0 / D, None, ALU.mult))
    p.E("v", lambda e: e.tensor_tensor(
        rows[0:1, TMP:TMP + width], rows[0:1, MU:MU + width],
        rows[0:1, MU:MU + width], ALU.mult))
    p.E("v", lambda e: e.tensor_tensor(
        rows[0:1, RSTD:RSTD + width], rows[0:1, RSTD:RSTD + width],
        rows[0:1, TMP:TMP + width], ALU.subtract), inc=smm)
    var_t = smm.n
    p.E("s", lambda e, var_t=var_t: e.activation(
        rows[0:1, RSTD:RSTD + width], rows[0:1, RSTD:RSTD + width], AF.Ln,
        bias=eps_ap), wait=[(smm, var_t)], inc=sq)
    p.E("s", lambda e: e.activation(
        rows[0:1, RSTD:RSTD + width], rows[0:1, RSTD:RSTD + width], AF.Exp,
        scale=-0.5), inc=sq)
    p.E("v", lambda e, v=sq.n: e.engine_nop(), wait=[(sq, sq.n)])
    p.E("v", lambda e: e.scalar_tensor_tensor(
        rows[0:1, R2:R2 + width], rows[0:1, MU:MU + width], -1.0,
        rows[0:1, RSTD:RSTD + width], ALU.mult, ALU.mult), inc=srow)
    return srow.n



def build_sign_ffn(width, p=None, io=None, prefix=""):
    if p is None:
        p = P("signffn")
    p.stage(prefix)

    x_in = p.io_inp(io, "x", (D, width))
    wT_in = p.inp("wT", (D, D))
    par_in = p.inp("pars_in", (128, 4 * NKT))
    ones_in = p.inp("cONESC", (128, 1))
    onesr_in = p.inp("cONESR", (1, 128))
    cmisc_in = p.inp("cMISC", (128, 8))
    xo = p.io_out(io, "xo", (D, width))

    xs = [p.sbuf(f"x{i}", (128, width)) for i in range(NKT)]
    wtile = [p.sbuf(f"w{i}", (128, D)) for i in range(2)]
    swT = [p.sbuf(f"sw{i}", (128, D), BF16) for i in range(NKT)]
    a_b = [p.sbuf(f"a{i}", (128, width), BF16) for i in range(NKT)]
    x2 = [p.sbuf(f"x2_{i}", (128, width)) for i in range(2)]
    tn = [p.sbuf(f"tn{i}", (128, 512)) for i in range(2)]
    xot = [p.sbuf(f"xot{i}", (128, 512)) for i in range(2)]
    rows = p.sbuf("rows", (1, 3300))
    pars = p.sbuf("pars", (128, 4 * NKT))
    ones_c = p.sbuf("onesc", (128, 1))
    ones_r = p.sbuf("onesr", (1, 128))
    cmisc = p.sbuf("cmisc", (128, 8))

    ps = [p.psum(f"ps{i}") for i in range(8)]

    sx = p.sem("sx")
    swl2 = [p.sem("swl0"), p.sem("swl1")]
    scons = p.sem("scons")
    ssq = p.sem("ssq")
    smm = p.sem("smm")
    srow = p.sem("srow")
    sa = p.sem("sa")
    sev = p.sem("sev")
    sst2 = [p.sem("sst0"), p.sem("sst1")]

    for i in range(NKT):
        p.E("y", lambda e, i=i: e.dma_start(out=xs[i][:], in_=x_in[i * 128:(i + 1) * 128, :]),
            inc=sx, inck=16)
    p.E("y", lambda e: e.dma_start(out=ones_c[:], in_=ones_in[:, :]), inc=sx, inck=16)
    p.E("y", lambda e: e.dma_start(out=ones_r[:], in_=onesr_in[:, :]), inc=sx, inck=16)
    p.E("y", lambda e: e.dma_start(out=pars[:], in_=par_in[:, :]), inc=sx, inck=16)
    p.E("y", lambda e: e.dma_start(out=cmisc[:], in_=cmisc_in[:, :]), inc=sx, inck=16)
    all_ld = sx.n

    GCOL = 0
    BCOL = NKT
    BACOL = 2 * NKT
    AACOL = 3 * NKT

    scons_b = scons.n
    for i in range(NKT):
        buf = wtile[i % 2]
        sw = swl2[i % 2]
        w = [(scons, scons_b + 4 * (i - 1))] if i >= 2 else []
        p.E("y", lambda e, buf=buf, i=i: e.dma_start(
            out=buf[:], in_=wT_in[i * 128:(i + 1) * 128, :]),
            wait=w, inc=sw, inck=16)
        ld_t = sw.n
        for oh in range(4):
            p.E("t", lambda e, buf=buf, i=i, oh=oh: e.matmul(
                ps[oh][0:1, 0:512], ones_c[:, 0:1], buf[:, oh * 512:(oh + 1) * 512],
                start=(i == 0), stop=(i == NKT - 1)),
                wait=[(sw, ld_t), (sx, all_ld)] if oh == 0 else [],
                inc=scons)
    mean_done = scons.n

    for oh in range(4):
        p.E("v", lambda e, oh=oh: e.tensor_scalar(
            rows[0:1, oh * 512:(oh + 1) * 512], ps[oh][0:1, 0:512],
            -1.0 / D, None, ALU.mult),
            wait=[(scons, mean_done)] if oh == 0 else [],
            inc=srow if oh == 3 else None)
    nmean_done = srow.n

    for oh in range(4):
        p.E("t", lambda e, oh=oh: e.matmul(
            ps[4 + oh][:, 0:512], ones_r[0:1, :], rows[0:1, oh * 512:(oh + 1) * 512],
            start=True, stop=True),
            wait=[(srow, nmean_done)] if oh == 0 else [],
            inc=smm if oh == 3 else None)
    bc4_t = smm.n
    tt_done = {}
    tile_tt = {}
    for i in range(NKT):
        buf = wtile[i % 2]
        sw = swl2[i % 2]
        w = [(sa, tile_tt[i - 2])] if i >= 2 else [(scons, mean_done)]
        p.E("y", lambda e, buf=buf, i=i: e.dma_start(
            out=buf[:], in_=wT_in[i * 128:(i + 1) * 128, :]),
            wait=w, inc=sw, inck=16)
        ld_t = sw.n
        for oh in range(4):
            tb = tn[(i * 4 + oh) % 2]
            w2 = [(smm, bc4_t), (sw, ld_t)]
            k = (i * 4 + oh)
            if k >= 2:
                w2.append((ssq, tt_done[k - 2]))
            p.E("v", lambda e, buf=buf, oh=oh, tb=tb: e.tensor_tensor(
                tb[:, 0:512], buf[:, oh * 512:(oh + 1) * 512], ps[4 + oh][:, 0:512],
                ALU.add), wait=w2, inc=sa)
            tt_t = sa.n
            p.E("s", lambda e, i=i, oh=oh, tb=tb, tt_t=tt_t: e.activation(
                swT[i][:, oh * 512:(oh + 1) * 512], tb[:, 0:512], AF.Sign),
                wait=[(sa, tt_t)],
                inc=ssq)
            tt_done[k] = ssq.n
        tile_tt[i] = sa.n
    sign_done = ssq.n
    prep_done = sign_done

    RSTD, R2, TMP = 0, 1100, 2200
    ln_t = emit_ln_rows2(p, xs, width, ones_c, ones_r, rows, RSTD, R2, TMP, x2, ps,
                         ssq, smm, srow, cmisc[0:1, 0:1],
                         extra_wait=[(sx, all_ld), (ssq, prep_done)])

    cgs = col_groups(width)
    a_done = {}
    prev_cg_tt = 0
    for cg, (c0, c1) in enumerate(cgs):
        n = c1 - c0
        pC = ps[4] if cg % 2 == 0 else ps[2]
        pD = ps[5] if cg % 2 == 0 else ps[3]
        p.E("t", lambda e, c0=c0, n=n, pC=pC: e.matmul(
            pC[:, 0:n], ones_r[0:1, :], rows[0:1, RSTD + c0:RSTD + c0 + n],
            start=True, stop=True),
            wait=[(srow, ln_t), (ssq, sign_done), (sa, prev_cg_tt)])
        p.E("t", lambda e, c0=c0, n=n, pD=pD: e.matmul(
            pD[:, 0:n], ones_r[0:1, :], rows[0:1, R2 + c0:R2 + c0 + n],
            start=True, stop=True), inc=smm)
        bc_t = smm.n
        for kt in range(NKT):
            tb = tn[kt % 2]
            w = [(smm, bc_t)]
            if kt >= 2:
                w.append((ssq, a_done[(cg, kt - 2)]))
            elif cg > 0:
                w.append((ssq, a_done[(cg - 1, NKT - 1)]))
            p.E("v", lambda e, kt=kt, c0=c0, c1=c1, n=n, tb=tb, pC=pC: e.tensor_tensor(
                tb[:, 0:n], xs[kt][:, c0:c1], pC[:, 0:n], ALU.mult), wait=w)
            p.E("v", lambda e, n=n, tb=tb, pD=pD: e.tensor_tensor(
                tb[:, 0:n], tb[:, 0:n], pD[:, 0:n], ALU.add))
            p.E("v", lambda e, kt=kt, n=n, tb=tb: e.tensor_scalar(
                tb[:, 0:n], tb[:, 0:n], pars[:, GCOL + kt:GCOL + kt + 1], None, ALU.mult),
                inc=sa)
            ts_t = sa.n
            p.E("s", lambda e, kt=kt, c0=c0, n=n, tb=tb, ts_t=ts_t: e.activation(
                a_b[kt][:, c0:c0 + n], tb[:, 0:n], AF.Sign,
                bias=pars[:, BCOL + kt:BCOL + kt + 1]),
                wait=[(sa, ts_t)], inc=ssq)
            a_done[(cg, kt)] = ssq.n
        prev_cg_tt = sa.n
    a_all = ssq.n

    p.E("v", lambda e: e.tensor_tensor(
        pars[:, BACOL:BACOL + NKT], pars[:, BACOL:BACOL + NKT],
        pars[:, AACOL:AACOL + NKT], ALU.mult), wait=[(sx, all_ld)])
    p.E("v", lambda e: e.tensor_scalar(
        pars[:, BACOL:BACOL + NKT], pars[:, BACOL:BACOL + NKT], 0.5, None, ALU.mult))
    p.E("v", lambda e: e.tensor_scalar(
        pars[:, AACOL:AACOL + NKT], pars[:, AACOL:AACOL + NKT], 0.5, None, ALU.mult),
        inc=srow)
    cols_done = srow.n

    ev_done = {}
    st_tgt = {}
    for cg, (c0, c1) in enumerate(cgs):
        n = c1 - c0
        for mt in range(NKT):
            pb = ps[6 + mt % 2]
            gi = cg * NKT + mt
            for kt in range(NKT):
                w = []
                if kt == 0:
                    w = [(ssq, a_all)]
                    if gi >= 2:
                        w.append((sev, ev_done[gi - 2]))
                p.E("t", lambda e, mt=mt, kt=kt, n=n, pb=pb, c0=c0, c1=c1: e.matmul(
                    pb[:, 0:n], swT[kt][:, mt * 128:(mt + 1) * 128], a_b[kt][:, c0:c1],
                    start=(kt == 0), stop=(kt == NKT - 1)),
                    wait=w, inc=smm if kt == NKT - 1 else None)
            mm_t = smm.n
            xob = xot[gi % 2]
            w = [(smm, mm_t), (srow, cols_done)]
            if gi >= 2:
                w.append((sst2[gi % 2], st_tgt[gi - 2]))
            p.E("v", lambda e, mt=mt, c0=c0, c1=c1, n=n, pb=pb, xob=xob: e.scalar_tensor_tensor(
                xob[:, 0:n], pb[:, 0:n], pars[:, AACOL + mt:AACOL + mt + 1],
                xs[mt][:, c0:c1], ALU.mult, ALU.add),
                wait=w, inc=sa)
            ev1 = sa.n
            p.E("s", lambda e, mt=mt, n=n, xob=xob, ev1=ev1: e.activation(
                xob[:, 0:n], xob[:, 0:n], AF.Identity, bias=pars[:, BACOL + mt:BACOL + mt + 1]),
                wait=[(sa, ev1)], inc=sev)
            ev_done[gi] = sev.n
            p.E("y", lambda e, mt=mt, c0=c0, n=n, xob=xob, v=sev.n: e.dma_start(
                out=xo[mt * 128:(mt + 1) * 128, c0:c0 + n], in_=xob[:, 0:n]),
                wait=[(sev, sev.n)], inc=sst2[gi % 2], inck=16)
            st_tgt[gi] = sst2[gi % 2].n
    if not DBG:
        st0, st1 = sst2[0].n, sst2[1].n
        p.q["y"].append(lambda e: e.wait_ge(sst2[0].h, st0))
        p.q["y"].append(lambda e: e.wait_ge(sst2[1].h, st1))
        return p
    dbg_sw = p.outp("dbg_sw", (128, D), BF16)
    dbg_a = p.outp("dbg_a", (128, width), BF16)
    dbg_af = p.outp("dbg_af", (D, width), BF16)
    dbg_tn = p.outp("dbg_tn", (128, 1024))
    p.E("y", lambda e: e.dma_start(out=dbg_tn[:, 0:512], in_=tn[0][:]),
        wait=[(ssq, a_all)], inc=sst2[0], inck=16)
    p.E("y", lambda e: e.dma_start(out=dbg_tn[:, 512:1024], in_=tn[1][:]),
        wait=[(ssq, a_all)], inc=sst2[0], inck=16)
    for kt in range(NKT):
        p.E("y", lambda e, kt=kt: e.dma_start(out=dbg_af[kt * 128:(kt + 1) * 128, :], in_=a_b[kt][:]),
            wait=[(ssq, a_all)], inc=sst2[0], inck=16)
    dbg_rows = p.outp("dbg_rows", (1, 3300))
    p.E("y", lambda e: e.dma_start(out=dbg_sw[:, :], in_=swT[0][:]),
        wait=[(ssq, sign_done)], inc=sst2[0], inck=16)
    p.E("y", lambda e: e.dma_start(out=dbg_a[:, :], in_=a_b[0][:]),
        wait=[(ssq, a_all)], inc=sst2[0], inck=16)
    p.E("y", lambda e: e.dma_start(out=dbg_rows[0:1, 0:width], in_=rows[0:1, 0:width]),
        wait=[(srow, ln_t)], inc=sst2[0], inck=16)
    p.E("y", lambda e: e.dma_start(out=dbg_rows[0:1, 1100:1100 + width],
                                   in_=rows[0:1, 1100:1100 + width]),
        wait=[(srow, ln_t)], inc=sst2[0], inck=16)
    st0, st1 = sst2[0].n, sst2[1].n
    p.q["y"].append(lambda e: e.wait_ge(sst2[0].h, st0))
    p.q["y"].append(lambda e: e.wait_ge(sst2[1].h, st1))
    return p




def build_moe(trunc=99, use_w=True, p=None, io=None, prefix=""):
    width = TPC
    FF = 1408
    NFF = FF // 128
    if p is None:
        p = P("moe")
    p.stage(prefix)

    x_in = p.io_inp(io, "x", (D, width))
    gateT_in = p.inp("gateT", (128, 16 * NKT))
    guT_in = p.inp("guT", (16, D, 2 * FF))
    dnT_in = p.inp("dnT", (16, FF, D))
    sguT_in = p.inp("sguT", (D, 4 * FF))
    sdT_in = p.inp("sdT", (2 * FF, D))
    ones_in = p.inp("cONESC", (128, 1))
    onesr_in = p.inp("cONESR", (1, 128))
    cmisc_in = p.inp("cMISC", (128, 8))
    eye16_in = p.inp("cEYE16", (16, 16))
    idf_in = p.inp("cIDF", (128, 128))
    xo = p.io_out(io, "xo", (D, width))

    xn = [p.sbuf(f"xn{i}", (128, width), BF16) for i in range(NKT)]
    ysb = [p.sbuf(f"y{i}", (128, width)) for i in range(NKT)]
    act = [p.sbuf(f"act{i}", (128, width), BF16) for i in range(NFF)]
    guw = p.sbuf("guw", (128, 16 * 256), BF16)
    guw2 = p.sbuf("guw2", (128, 16 * 256), BF16)
    dnw = p.sbuf("dnw", (128, 6 * D), BF16)
    xtr = [p.sbuf(f"xtr{i}", (128, width)) for i in range(2)]
    x2b = [p.sbuf("x2b0", (128, width))] * 2
    gateT = p.sbuf("gateT_sb", (128, 16 * NKT))
    rows = p.sbuf("rows", (1, 3250))
    gsum_sb = p.sbuf("gsum", (1, 16))
    lg = p.sbuf("lg", (16, width))
    lgt = p.sbuf("lgt", (128, 16))
    scs = [p.sbuf(f"sc{i}", (128, 16)) for i in range(8)]
    scw = p.sbuf("scw", (128, 16))
    msk = p.sbuf("msk", (128, 16))
    mcol = p.sbuf("mcol", (128, 8))
    wrows = p.sbuf("wrows", (16, width))
    wrow1 = p.sbuf("wrow1", (1, 512))
    wbc = [p.sbuf(f"wbc{i}", (128, 512)) for i in range(2)]
    sg2 = [p.sbuf(f"sgb{i}", (128, 512), BF16) for i in range(2)]
    tu2 = [p.sbuf(f"tub{i}", (128, 512), BF16) for i in range(2)]
    rtb = p.sbuf("rtb", (16, 512))
    tmpv = p.sbuf("tmpv", (128, 512))
    ones_c = p.sbuf("onesc", (128, 1))
    ones_r = p.sbuf("onesr", (1, 128))
    cmisc = p.sbuf("cmisc", (128, 8))
    eye16 = p.sbuf("eye16", (16, 16))
    idf = p.sbuf("idf", (128, 128))
    xob = wbc

    ps = [p.psum(f"ps{i}") for i in range(8)]

    sx = p.sem("sx")
    sxa = [p.sem("sxa0"), p.sem("sxa1")]
    swg = [p.sem("swg0"), p.sem("swg1")]
    swd = p.sem("swd")
    smm = p.sem("smm")
    srow = p.sem("srow")
    svx = p.sem("svx")
    sev = p.sem("sev")
    sst2 = [p.sem("sst0"), p.sem("sst1")]

    cgs = col_groups(width)

    for dst, src_ in ((ones_c, ones_in), (ones_r, onesr_in), (cmisc, cmisc_in),
                      (eye16, eye16_in), (idf, idf_in), (gateT, gateT_in)):
        p.E("y", lambda e, dst=dst, src_=src_: e.dma_start(out=dst[:], in_=src_[:, :]),
            inc=sx, inck=16)
    const_ld = sx.n

    xa_t = {}
    for kt in range(NKT):
        xb = xtr[kt % 2]
        sxp = sxa[kt % 2]
        w = [(smm, xa_t[kt - 2])] if kt >= 2 else []
        p.E("y", lambda e, kt=kt, xb=xb: e.dma_start(
            out=xb[:], in_=x_in[kt * 128:(kt + 1) * 128, :]),
            wait=w, inc=sxp, inck=16)
        ld_t = sxp.n
        x2 = x2b[kt % 2]
        wsq = [(sxp, ld_t)]
        if kt >= 1:
            wsq.append((smm, xa_t[kt - 1]))
        p.E("s", lambda e, xb=xb, x2=x2, wsq=0, ld_t=ld_t, sxp=sxp: e.activation(
            x2[:], xb[:], AF.Square), wait=wsq, inc=srow)
        sq_t = srow.n
        for cg, (c0, c1) in enumerate(cgs):
            p.E("t", lambda e, kt=kt, c0=c0, c1=c1, cg=cg, xb=xb, ld_t=ld_t, sxp=sxp: e.matmul(
                ps[cg][0:1, 0:c1 - c0], ones_c[:, 0:1], xb[:, c0:c1],
                start=(kt == 0), stop=(kt == NKT - 1)),
                wait=[(sxp, ld_t), (sx, const_ld)] if cg == 0 else [])
            p.E("t", lambda e, kt=kt, c0=c0, c1=c1, cg=cg, x2=x2, sq_t=sq_t: e.matmul(
                ps[2 + cg][0:1, 0:c1 - c0], ones_c[:, 0:1], x2[:, c0:c1],
                start=(kt == 0), stop=(kt == NKT - 1)),
                wait=[(srow, sq_t)])
            p.E("t", lambda e, kt=kt, c0=c0, c1=c1, cg=cg, xb=xb: e.matmul(
                ps[6 + cg][0:16, 0:c1 - c0], gateT[:, kt * 16:(kt + 1) * 16], xb[:, c0:c1],
                start=(kt == 0), stop=False))
        p.E("t", lambda e, kt=kt: e.matmul(
            ps[5][0:1, 0:16], ones_c[:, 0:1], gateT[:, kt * 16:(kt + 1) * 16],
            start=(kt == 0), stop=(kt == NKT - 1)),
            inc=smm)
        xa_t[kt] = smm.n
    stats_done = smm.n

    p.E("s", lambda e: e.activation(gsum_sb[0:1, :], ps[5][0:1, 0:16], AF.Identity),
        wait=[(smm, stats_done)], inc=srow)
    gsum_t = srow.n

    for cg, (c0, c1) in enumerate(cgs):
        n = c1 - c0
        p.E("v", lambda e, cg=cg, c0=c0, n=n: e.tensor_scalar(
            rows[0:1, 1100 + c0:1100 + c0 + n], ps[cg][0:1, 0:n], -1.0 / D, None, ALU.mult),
            wait=[(smm, stats_done)] if cg == 0 else [])
        p.E("v", lambda e, cg=cg, c0=c0, n=n: e.tensor_scalar(
            rows[0:1, c0:c0 + n], ps[2 + cg][0:1, 0:n], 1.0 / D, None, ALU.mult))
    for h2 in range(2):
        p.E("v", lambda e, h2=h2: e.tensor_tensor(
            tmpv[0:1, 0:512], rows[0:1, 1100 + h2 * 512:1100 + h2 * 512 + 512],
            rows[0:1, 1100 + h2 * 512:1100 + h2 * 512 + 512], ALU.mult))
        p.E("v", lambda e, h2=h2: e.tensor_tensor(
            rows[0:1, h2 * 512:h2 * 512 + 512], rows[0:1, h2 * 512:h2 * 512 + 512],
            tmpv[0:1, 0:512], ALU.subtract), inc=svx if h2 == 1 else None)
    var_t = svx.n
    p.E("s", lambda e, var_t=var_t: e.activation(
        rows[0:1, 0:width], rows[0:1, 0:width], AF.Ln,
        bias=cmisc[0:1, 0:1]), wait=[(svx, var_t)], inc=srow)
    p.E("s", lambda e: e.activation(
        rows[0:1, 0:width], rows[0:1, 0:width], AF.Exp,
        scale=-0.5), inc=srow)
    p.E("v", lambda e, v=srow.n: e.engine_nop(), wait=[(srow, srow.n)])
    p.E("v", lambda e: e.tensor_tensor(
        rows[0:1, 2200:2200 + width], rows[0:1, 1100:1100 + width],
        rows[0:1, 0:width], ALU.mult), inc=svx)
    rows_done = svx.n

    for cg, (c0, c1) in enumerate(cgs):
        n = c1 - c0
        p.E("t", lambda e, cg=cg, c0=c0, n=n: e.matmul(
            ps[6 + cg][0:16, 0:n], gsum_sb[0:1, :], rows[0:1, 1100 + c0:1100 + c0 + n],
            start=False, stop=True),
            wait=[(svx, rows_done), (srow, gsum_t)] if cg == 0 else [])
        p.E("t", lambda e, cg=cg, c0=c0, n=n: e.matmul(
            ps[4][0:16, 0:n], ones_r[0:1, 0:16], rows[0:1, c0:c0 + n],
            start=True, stop=True),
            wait=[(sev, sev.n), (svx, svx.n)], inc=smm)
        bc_t = smm.n
        p.E("s", lambda e, c0=c0, n=n, bc_t=bc_t: e.activation(
            rtb[0:16, 0:n], ps[4][0:16, 0:n], AF.Identity),
            wait=[(smm, bc_t)], inc=sev)
        rtb_t = sev.n
        p.E("v", lambda e, cg=cg, c0=c0, n=n, rtb_t=rtb_t: e.tensor_tensor(
            lg[0:16, c0:c0 + n], ps[6 + cg][0:16, 0:n], rtb[0:16, 0:n], ALU.mult),
            wait=[(sev, rtb_t)], inc=svx)
    lg_done = svx.n

    BIG = 1.0e30
    for blk in range(8):
        p.E("t", lambda e, blk=blk: e.transpose(
            ps[6][0:128, 0:16], lg[0:16, blk * 128:(blk + 1) * 128], idf[0:16, 0:16]),
            wait=[(svx, lg_done)] if blk == 0 else [(srow, srow.n)], inc=smm)
        tr_t = smm.n
        p.E("s", lambda e, tr_t=tr_t: e.activation(
            lgt[:, :], ps[6][0:128, 0:16], AF.Identity),
            wait=[(smm, tr_t)], inc=srow)
        lgt_t = srow.n
        p.E("v", lambda e, lgt_t=lgt_t: e.reduce_max(
            mcol[:, 0:1], lgt[:, :], axis=mybir.AxisListType.X, negate=True),
            wait=[(srow, lgt_t)], inc=svx)
        p.E("s", lambda e, blk=blk, v=svx.n: e.activation(
            scs[blk][:, :], lgt[:, :], AF.Exp, bias=mcol[:, 0:1]),
            wait=[(svx, svx.n)], inc=srow)
        p.E("v", lambda e, blk=blk, v=srow.n: e.reduce_sum(
            mcol[:, 1:2], scs[blk][:, :], axis=mybir.AxisListType.X),
            wait=[(srow, srow.n)])
        p.E("v", lambda e: e.reciprocal(mcol[:, 2:3], mcol[:, 1:2]))
        p.E("v", lambda e, blk=blk: e.tensor_scalar(
            scs[blk][:, :], scs[blk][:, :], mcol[:, 2:3], None, ALU.mult))
        p.E("v", lambda e, blk=blk: e.tensor_copy(scw[:, :], scs[blk][:, :]))
        for r in range(4):
            p.E("v", lambda e: e.reduce_max(mcol[:, 3:4], scw[:, :], axis=mybir.AxisListType.X))
            if r < 3:
                p.E("v", lambda e: e.tensor_scalar(
                    msk[:, :], scw[:, :], mcol[:, 3:4], None, ALU.is_ge))
                p.E("v", lambda e: e.scalar_tensor_tensor(
                    scw[:, :], msk[:, :], -BIG, scw[:, :], ALU.mult, ALU.add))
        p.E("v", lambda e, blk=blk: e.tensor_scalar(
            msk[:, :], scs[blk][:, :], mcol[:, 3:4], None, ALU.is_ge))
        p.E("v", lambda e, blk=blk: e.tensor_tensor(
            scs[blk][:, :], scs[blk][:, :], msk[:, :], ALU.mult))
        p.E("v", lambda e, blk=blk: e.tensor_scalar(
            scs[blk][:, :], scs[blk][:, :], 0.5, None, ALU.mult), inc=svx)
        w_t = svx.n
        p.E("t", lambda e, blk=blk, w_t=w_t: e.transpose(
            ps[7][0:16, 0:128], scs[blk][:, :], idf[:, :]),
            wait=[(svx, w_t), (srow, srow.n)], inc=smm)
        tr2_t = smm.n
        p.E("s", lambda e, blk=blk, tr2_t=tr2_t: e.activation(
            wrows[0:16, blk * 128:(blk + 1) * 128], ps[7][0:16, 0:128], AF.Identity),
            wait=[(smm, tr2_t)], inc=srow)
    routing_done = srow.n

    for cg, (c0, c1) in enumerate(cgs):
        n = c1 - c0
        p.E("t", lambda e, c0=c0, n=n, cg=cg: e.matmul(
            ps[4 + 2 * cg][:, 0:n], ones_r[0:1, :], rows[0:1, c0:c0 + n],
            start=True, stop=True),
            wait=[(srow, routing_done), (svx, svx.n)] if cg == 0 else [])
        p.E("t", lambda e, c0=c0, n=n, cg=cg: e.matmul(
            ps[5 + 2 * cg][:, 0:n], ones_r[0:1, :], rows[0:1, 2200 + c0:2200 + c0 + n],
            start=True, stop=True), inc=smm)
    bcB_t = smm.n
    xnv = {}
    for kt in range(NKT):
        xb = xtr[kt % 2]
        sxp = sxa[kt % 2]
        w = [(svx, xnv[kt - 2])] if kt >= 2 else [(svx, svx.n), (srow, routing_done)]
        p.E("y", lambda e, kt=kt, xb=xb: e.dma_start(
            out=xb[:], in_=x_in[kt * 128:(kt + 1) * 128, :]),
            wait=w, inc=sxp, inck=16)
        ld_t = sxp.n
        for cg, (c0, c1) in enumerate(cgs):
            n = c1 - c0
            p.E("v", lambda e, xb=xb, c0=c0, c1=c1, n=n, cg=cg, ld_t=ld_t, sxp=sxp: e.tensor_tensor(
                tmpv[:, 0:n], xb[:, c0:c1], ps[4 + 2 * cg][:, 0:n], ALU.mult),
                wait=[(sxp, ld_t), (smm, bcB_t)])
            p.E("v", lambda e, kt=kt, c0=c0, n=n, cg=cg: e.tensor_tensor(
                xn[kt][:, c0:c0 + n], tmpv[:, 0:n], ps[5 + 2 * cg][:, 0:n], ALU.add),
                inc=svx)
        xnv[kt] = svx.n
    xn_done = svx.n

    st = {"gu_ev": {}, "dn_ev": {}, "guw_free": {0: 0, 1: 0}, "dn_free": 0,
          "gu_gi": 0, "dn_gi": 0, "wbc_t": 0}

    def emit_glu(guT_ap, dnT_ap, inter, with_w, first):
        n_half = inter // FF
        for half in range(n_half):
            g_base = half * FF
            u_base = inter + half * FF
            chs = [(0, 256), (256, 512), (512, 768), (768, 1024), (1024, 1280), (1280, 1408)]
            for ich, (i0, i1) in enumerate(chs):
                ni = i1 - i0
                nmt = (ni + 127) // 128
                sw = swg[ich % 2]
                wld = [(smm, st["guw_free"][ich % 2])]
                p.E("g", lambda e, guT_ap=guT_ap, g_base=g_base, i0=i0, ni=ni: e.dma_start(
                    out=guw[:, 0:16 * ni].rearrange("p (k c) -> p k c", k=16),
                    in_=guT_ap[:, g_base + i0:g_base + i0 + ni].rearrange(
                        "(k p) c -> p k c", p=128)),
                    wait=wld, inc=sw, inck=16)
                p.E("g", lambda e, guT_ap=guT_ap, u_base=u_base, i0=i0, ni=ni: e.dma_start(
                    out=guw2[:, 0:16 * ni].rearrange("p (k c) -> p k c", k=16),
                    in_=guT_ap[:, u_base + i0:u_base + i0 + ni].rearrange(
                        "(k p) c -> p k c", p=128)),
                    inc=sw, inck=16)
                ld_t = sw.n
                for th in range(2):
                    t0 = th * 512
                    for mt in range(nmt):
                        m0 = i0 + mt * 128
                        nm = min(128, ni - mt * 128)
                        gi = st["gu_gi"]
                        st["gu_gi"] += 1
                        pg = ps[gi % 2]
                        pu = ps[2 + gi % 2]
                        sgb = sg2[gi % 2]
                        tub = tu2[gi % 2]
                        for kt in range(NKT):
                            w = []
                            if kt == 0:
                                w = [(sw, ld_t), (svx, xn_done),
                                     (svx, st["gu_ev"].get(gi - 2, 0))]
                            p.E("t", lambda e, kt=kt, mt=mt, nm=nm, t0=t0, pg=pg, ni=ni: e.matmul(
                                pg[0:nm, 0:512],
                                guw[:, kt * ni + mt * 128:kt * ni + mt * 128 + nm],
                                xn[kt][:, t0:t0 + 512],
                                start=(kt == 0), stop=(kt == NKT - 1)),
                                wait=w)
                            p.E("t", lambda e, kt=kt, mt=mt, nm=nm, t0=t0, pu=pu, ni=ni: e.matmul(
                                pu[0:nm, 0:512],
                                guw2[:, kt * ni + mt * 128:kt * ni + mt * 128 + nm],
                                xn[kt][:, t0:t0 + 512],
                                start=(kt == 0), stop=(kt == NKT - 1)),
                                inc=smm if kt == NKT - 1 else None)
                        mm_t = smm.n
                        p.E("s", lambda e, nm=nm, pg=pg, sgb=sgb, mm_t=mm_t: e.activation(
                            sgb[0:nm, 0:512], pg[0:nm, 0:512], AF.Silu),
                            wait=[(smm, mm_t)], inc=srow)
                        sil_t = srow.n
                        if with_w:
                            wbt = st["wbc_t"]
                            p.E("v", lambda e, nm=nm, pu=pu, tub=tub, th=th: e.tensor_tensor(
                                tub[0:nm, 0:512], pu[0:nm, 0:512], wbc[th][0:nm, 0:512],
                                ALU.mult),
                                wait=[(smm, mm_t), (sev, wbt)])
                        else:
                            p.E("v", lambda e, nm=nm, pu=pu, tub=tub, mm_t=mm_t: e.tensor_scalar(
                                tub[0:nm, 0:512], pu[0:nm, 0:512], 0.5, None, ALU.mult),
                                wait=[(smm, mm_t)])
                        mtile = m0 // 128
                        dnf = st["dn_free"]
                        p.E("v", lambda e, nm=nm, sgb=sgb, tub=tub, mtile=mtile, t0=t0: e.tensor_tensor(
                            act[mtile][0:nm, t0:t0 + 512], sgb[0:nm, 0:512],
                            tub[0:nm, 0:512], ALU.mult),
                            wait=[(srow, sil_t), (smm, dnf)], inc=svx)
                        st["gu_ev"][gi] = svx.n
                st["guw_free"][ich % 2] = smm.n
            act_done = svx.n
            for kseg, (k0, k1) in enumerate(((0, 6), (6, NFF))):
                nk = k1 - k0
                p.E("g", lambda e, dnT_ap=dnT_ap, half=half, k0=k0, nk=nk: e.dma_start(
                    out=dnw[:, 0:nk * D].rearrange("p (k c) -> p k c", k=nk),
                    in_=dnT_ap[half * FF + k0 * 128:half * FF + (k0 + nk) * 128, :].rearrange(
                        "(k p) c -> p k c", p=128)),
                    wait=[(smm, st["dn_free"])], inc=swd, inck=16)
                dn_t = swd.n
                for th in range(2):
                    t0 = th * 512
                    for mt in range(NKT):
                        gi = st["dn_gi"]
                        st["dn_gi"] += 1
                        pd = ps[4 + gi % 3]
                        for kt in range(nk):
                            w = []
                            if kt == 0:
                                w = [(swd, dn_t), (svx, act_done),
                                     (svx, st["dn_ev"].get(gi - 3, 0))]
                            p.E("t", lambda e, kt=kt, k0=k0, mt=mt, pd=pd, t0=t0, nk=nk: e.matmul(
                                pd[:, 0:512], dnw[:, kt * D + mt * 128:kt * D + mt * 128 + 128],
                                act[k0 + kt][:, t0:t0 + 512],
                                start=(kt == 0), stop=(kt == nk - 1)),
                                wait=w, inc=smm if kt == nk - 1 else None)
                        dmm_t = smm.n
                        if first and half == 0 and kseg == 0:
                            p.E("v", lambda e, mt=mt, pd=pd, t0=t0, dmm_t=dmm_t: e.tensor_copy(
                                ysb[mt][:, t0:t0 + 512], pd[:, 0:512]),
                                wait=[(smm, dmm_t)], inc=svx)
                        else:
                            p.E("v", lambda e, mt=mt, pd=pd, t0=t0, dmm_t=dmm_t: e.tensor_tensor(
                                ysb[mt][:, t0:t0 + 512], ysb[mt][:, t0:t0 + 512], pd[:, 0:512],
                                ALU.add),
                                wait=[(smm, dmm_t)], inc=svx)
                        st["dn_ev"][gi] = svx.n
                st["dn_free"] = smm.n

    if trunc >= 1:
        emit_glu(sguT_in, sdT_in, 2 * FF, with_w=False, first=True)

    for ei in range(min(16, max(0, trunc - 1))):
        if not use_w:
            emit_glu(guT_in[ei], dnT_in[ei], FF, with_w=False, first=False)
            continue
        for th in range(2):
            t0 = th * 512
            p.E("t", lambda e, ei=ei, t0=t0: e.matmul(
                ps[7][0:1, 0:512], eye16[:, ei:ei + 1], wrows[0:16, t0:t0 + 512],
                start=True, stop=True),
                wait=[(srow, routing_done), (sev, sev.n)], inc=smm)
            sel_t = smm.n
            p.E("s", lambda e, sel_t=sel_t: e.activation(
                wrow1[0:1, :], ps[7][0:1, 0:512], AF.Identity),
                wait=[(smm, sel_t)], inc=srow)
            wr_t = srow.n
            p.E("t", lambda e, wr_t=wr_t: e.matmul(
                ps[7][:, 0:512], ones_r[0:1, :], wrow1[0:1, :],
                start=True, stop=True),
                wait=[(srow, wr_t)], inc=smm)
            bc2_t = smm.n
            p.E("s", lambda e, th=th, bc2_t=bc2_t: e.activation(
                wbc[th][:, :], ps[7][:, 0:512], AF.Identity),
                wait=[(smm, bc2_t)], inc=sev)
        st["wbc_t"] = sev.n
        emit_glu(guT_in[ei], dnT_in[ei], FF, with_w=True, first=False)
    y_done = svx.n

    stv = {}
    sst_b = (sst2[0].n, sst2[1].n)
    for kt in range(NKT):
        xb = xtr[kt % 2]
        sxp = sxa[kt % 2]
        w = [(svx, y_done)]
        if kt >= 2:
            w.append((svx, stv[kt - 2]))
        p.E("y", lambda e, kt=kt, xb=xb: e.dma_start(
            out=xb[:], in_=x_in[kt * 128:(kt + 1) * 128, :]),
            wait=w, inc=sxp, inck=16)
        ld_t = sxp.n
        for th in range(2):
            t0 = th * 512
            ob = xob[th]
            wst = [(sxp, ld_t)]
            if kt >= 1:
                wst.append((sst2[th], sst_b[th] + 16 * kt))
            if trunc >= 1:
                p.E("v", lambda e, kt=kt, xb=xb, t0=t0, ob=ob: e.tensor_tensor(
                    ob[:, :], xb[:, t0:t0 + 512], ysb[kt][:, t0:t0 + 512], ALU.add),
                    wait=wst, inc=svx)
            else:
                p.E("v", lambda e, kt=kt, xb=xb, t0=t0, ob=ob: e.tensor_copy(
                    ob[:, :], xb[:, t0:t0 + 512]),
                    wait=wst, inc=svx)
            ev_t = svx.n
            p.E("y", lambda e, kt=kt, t0=t0, ob=ob, ev_t=ev_t: e.dma_start(
                out=xo[kt * 128:(kt + 1) * 128, t0:t0 + 512], in_=ob[:, :]),
                wait=[(svx, ev_t)], inc=sst2[th], inck=16)
        stv[kt] = svx.n
    stA, stBv = sst2[0].n, sst2[1].n
    p.q["y"].append(lambda e: e.wait_ge(sst2[0].h, stA))
    p.q["y"].append(lambda e: e.wait_ge(sst2[1].h, stBv))
    return p


def run_moe(x_fm_slices, wts, trunc=99, use_w=True):
    p = build_moe(trunc, use_w)
    nc = p.finish()
    in_maps = []
    for c in range(NCORES):
        m = {"x": x_fm_slices[c]}
        m.update(wts)
        m.update(_CONSTS)
        in_maps.append(m)
    res = run_bass_kernel_spmd(nc, in_maps, core_ids=list(range(NCORES)))
    return [r["xo"] for r in res.results]


def moe_weights(inputs):
    gw = np.asarray(inputs["L1_ffn_gate_weight"], np.float32)
    gu = np.asarray(inputs["L1_ffn_gate_up_weights"], np.float32)
    dn = np.asarray(inputs["L1_ffn_down_weights"], np.float32)
    sgu = np.asarray(inputs["L1_ffn_shared_gate_up"], np.float32)
    sd = np.asarray(inputs["L1_ffn_shared_down"], np.float32)
    return {
        "gateT": np.ascontiguousarray(gw.T.reshape(NKT, 128, 16).transpose(1, 0, 2).reshape(128, NKT * 16)),
        "guT": np.ascontiguousarray(gu.transpose(0, 2, 1)),
        "dnT": np.ascontiguousarray(dn.transpose(0, 2, 1)),
        "sguT": np.ascontiguousarray(sgu.T),
        "sdT": np.ascontiguousarray(sd.T),
    }




def build_lru(p=None, io=None, prefix=""):
    width = TPC
    N2 = 2
    if p is None:
        p = P("lru")
    p.stage(prefix)

    x_in = p.io_inp(io, "x", (D, width))
    BTre_in = p.inp("BTre", (D, 256))
    BTim_in = p.inp("BTim", (D, 256))
    CTre_in = p.inp("CTre", (256, D))
    CTim_in = p.inp("CTim", (256, D))
    nu_in = p.inp("nu_log", (128, 2))
    th_in = p.inp("theta_log", (128, 2))
    dv_in = p.inp("Dvec", (128, NKT))
    cj_in = p.inp("cJ", (128, width))
    onesw_in = p.inp("cONESW", (128, width))
    msel_in = p.inp("mSEL", (128, 8))
    ones_in = p.inp("cONESC", (128, 1))
    onesr_in = p.inp("cONESR", (1, 128))
    cmisc_in = p.inp("cMISC", (128, 8))
    xo = p.io_out(io, "xo", (D, width))

    agin = p.dram("lru_agin", (128, 4))
    agout = p.dram("lru_agout", (8 * 128, 4))

    xn = [p.sbuf(f"xn{i}", (128, width), BF16) for i in range(NKT)]
    xtr = [p.sbuf(f"xtr{i}", (128, width)) for i in range(2)]
    x2b = [p.sbuf("x2b0", (128, width))] * 2
    rows = p.sbuf("rows", (1, 3250))
    BT = p.sbuf("BT", (128, NKT * 256), BF16)
    BT2 = p.sbuf("BT2", (128, NKT * 256), BF16)
    CT = p.sbuf("CT", (128, N2 * D), BF16)
    CT2 = p.sbuf("CT2", (128, N2 * D), BF16)
    cph = [p.sbuf(f"cph{i}", (128, width)) for i in range(N2)]
    sph = [p.sbuf(f"sph{i}", (128, width)) for i in range(N2)]
    cph1 = [p.sbuf(f"cph1_{i}", (128, width)) for i in range(N2)]
    sph1 = [p.sbuf(f"sph1_{i}", (128, width)) for i in range(N2)]
    magb = [p.sbuf(f"magb{i}", (128, width)) for i in range(N2)]
    magp = [p.sbuf(f"magp{i}", (128, width)) for i in range(N2)]
    bur = [p.sbuf(f"bur{i}", (128, width)) for i in range(N2)]
    bui = [p.sbuf(f"bui{i}", (128, width)) for i in range(N2)]
    wre = [p.sbuf(f"wre{i}", (128, width)) for i in range(N2)]
    wim = [p.sbuf(f"wim{i}", (128, width)) for i in range(N2)]
    hbr = [p.sbuf(f"hbr{i}", (128, width), BF16) for i in range(N2)]
    hbi = [p.sbuf(f"hbi{i}", (128, width), BF16) for i in range(N2)]
    cols = p.sbuf("cols", (128, 32))
    fins = p.sbuf("fins", (128, 32))
    cj = p.sbuf("cj", (128, width))
    onesw = p.sbuf("onesw", (128, width))
    msel = p.sbuf("msel", (128, 8))
    tq = p.sbuf("tq", (128, width))
    tq2 = p.sbuf("tq2", (128, width))
    ones_c = p.sbuf("onesc", (128, 1))
    ones_r = p.sbuf("onesr", (1, 128))
    cmisc = p.sbuf("cmisc", (128, 8))
    dcol = p.sbuf("dcol", (128, NKT))
    tmpv = p.sbuf("tmpv", (128, 512))
    xob = [p.sbuf(f"xob{i}", (128, 512)) for i in range(2)]

    ps = [p.psum(f"ps{i}") for i in range(8)]

    sx = p.sem("sx")
    sg = p.sem("sg")
    sxa = [p.sem("sxa0"), p.sem("sxa1")]
    smm = p.sem("smm")
    srow = p.sem("srow")
    svx = p.sem("svx")
    sev = p.sem("sev")
    scc = p.sem("scc")
    sst2 = [p.sem("sst0"), p.sem("sst1")]

    cgs = col_groups(width)
    PI = float(np.pi)
    TWOPI = float(2 * np.pi)

    for dst, src_ in ((ones_c, ones_in), (ones_r, onesr_in), (cmisc, cmisc_in),
                      (cj, cj_in), (onesw, onesw_in), (msel, msel_in),
                      (dv_in_sb := dcol, dv_in)):
        p.E("y", lambda e, dst=dst, src_=src_: e.dma_start(out=dst[:], in_=src_[:, :]),
            inc=sx, inck=16)
    p.E("y", lambda e: e.dma_start(out=cols[:, 0:2], in_=th_in[:, :]), inc=sx, inck=16)
    p.E("y", lambda e: e.dma_start(out=cols[:, 2:4], in_=nu_in[:, :]), inc=sx, inck=16)
    p.E("g", lambda e: e.dma_start(
        out=BT[:, :].rearrange("p (k c) -> p k c", k=NKT),
        in_=BTre_in[:, :].rearrange("(k p) c -> p k c", p=128)), inc=sg, inck=16)
    p.E("g", lambda e: e.dma_start(
        out=BT2[:, :].rearrange("p (k c) -> p k c", k=NKT),
        in_=BTim_in[:, :].rearrange("(k p) c -> p k c", p=128)), inc=sg, inck=16)
    p.E("g", lambda e: e.dma_start(
        out=CT[:, :].rearrange("p (k c) -> p k c", k=N2),
        in_=CTre_in[:, :].rearrange("(k p) c -> p k c", p=128)), inc=sg, inck=16)
    p.E("g", lambda e: e.dma_start(
        out=CT2[:, :].rearrange("p (k c) -> p k c", k=N2),
        in_=CTim_in[:, :].rearrange("(k p) c -> p k c", p=128)), inc=sg, inck=16)
    wload = sg.n
    all_ld = sx.n

    p.E("s", lambda e: e.activation(cols[:, 4:6], cols[:, 0:2], AF.Exp),
        wait=[(sx, all_ld)], inc=srow)
    p.E("v", lambda e, v=srow.n: e.tensor_scalar(
        cols[:, 16:18], cols[:, 2:4], -1.0, None, ALU.mult), wait=[(srow, srow.n)])
    p.E("v", lambda e: e.tensor_tensor(cols[:, 16:18], cols[:, 16:18], cols[:, 2:4], ALU.max))
    p.E("v", lambda e: e.tensor_scalar(cols[:, 18:20], cols[:, 2:4], 0.0, None, ALU.max),
        inc=svx)
    p.E("s", lambda e, v=svx.n: e.activation(cols[:, 16:18], cols[:, 16:18], AF.Exp,
                                             scale=-1.0), wait=[(svx, svx.n)], inc=srow)
    p.E("s", lambda e: e.activation(cols[:, 16:18], cols[:, 16:18], AF.Ln,
                                    bias=1.0), inc=srow)
    p.E("v", lambda e, v=srow.n: e.tensor_tensor(
        cols[:, 6:8], cols[:, 16:18], cols[:, 18:20], ALU.add),
        wait=[(srow, srow.n)], inc=svx)
    p.E("s", lambda e, v=svx.n: e.activation(cols[:, 6:8], cols[:, 6:8], AF.Exp),
        wait=[(svx, svx.n)], inc=srow)
    p.E("s", lambda e: e.activation(cols[:, 8:10], cols[:, 6:8], AF.Exp, scale=-1.0),
        inc=srow)
    p.E("v", lambda e, v=srow.n: e.tensor_tensor(
        cols[:, 16:18], cols[:, 8:10], cols[:, 8:10], ALU.mult),
        wait=[(srow, srow.n)])
    p.E("v", lambda e: e.tensor_scalar(cols[:, 16:18], cols[:, 16:18], -1.0, 1.0,
                                       ALU.mult, ALU.add))
    p.E("v", lambda e: e.tensor_scalar(cols[:, 16:18], cols[:, 16:18], 0.0, None,
                                       ALU.max), inc=svx)
    p.E("s", lambda e, v=svx.n: e.activation(cols[:, 10:12], cols[:, 16:18], AF.Ln,
                                             bias=cmisc[:, 3:4]),
        wait=[(svx, svx.n)], inc=srow)
    p.E("s", lambda e: e.activation(cols[:, 10:12], cols[:, 10:12], AF.Exp,
                                    scale=0.5), inc=srow)
    MAGIC = 8388608.0
    p.E("s", lambda e: e.activation(cols[:, 18:20], cols[:, 6:8], AF.Exp,
                                    scale=-1024.0), inc=srow)
    p.E("v", lambda e, v=srow.n: e.tensor_scalar(
        cols[:, 28:30], cols[:, 4:6], 1.0 / TWOPI, None, ALU.mult),
        wait=[(srow, srow.n)])
    p.E("v", lambda e: e.tensor_scalar(
        cols[:, 20:22], cols[:, 28:30], 1024.0, None, ALU.mult))
    p.E("v", lambda e: e.tensor_scalar(cols[:, 26:28], cols[:, 20:22], MAGIC, None,
                                       ALU.add))
    p.E("v", lambda e: e.tensor_scalar(cols[:, 26:28], cols[:, 26:28], -MAGIC, None,
                                       ALU.add))
    p.E("v", lambda e: e.tensor_tensor(cols[:, 20:22], cols[:, 20:22], cols[:, 26:28],
                                       ALU.subtract))
    p.E("v", lambda e: e.tensor_scalar(cols[:, 20:22], cols[:, 20:22], TWOPI, None,
                                       ALU.mult), inc=svx)
    p.E("s", lambda e, v=svx.n: e.activation(cols[:, 22:24], cols[:, 20:22], AF.Sin),
        wait=[(svx, svx.n)], inc=srow)
    p.E("s", lambda e: e.activation(cols[:, 24:26], cols[:, 20:22], AF.Sin, scale=0.5),
        inc=srow)
    p.E("v", lambda e, v=srow.n: e.tensor_tensor(
        cols[:, 24:26], cols[:, 24:26], cols[:, 24:26], ALU.mult),
        wait=[(srow, srow.n)])
    p.E("v", lambda e: e.tensor_scalar(cols[:, 24:26], cols[:, 24:26], -2.0, 1.0,
                                       ALU.mult, ALU.add))
    p.E("v", lambda e: e.tensor_tensor(cols[:, 12:14], cols[:, 18:20], cols[:, 24:26],
                                       ALU.mult))
    p.E("v", lambda e: e.tensor_tensor(cols[:, 14:16], cols[:, 18:20], cols[:, 22:24],
                                       ALU.mult), inc=svx)
    cols_done = svx.n

    def emit_phase(b, jshift, sdst, cdst, w0):
        t2p = cols[:, 28 + b:29 + b]
        p.E("v", lambda e, b=b, t2p=t2p: e.tensor_scalar(
            tq[:, :], cj[:, :], float(jshift), t2p, ALU.add, ALU.mult), wait=w0)
        p.E("v", lambda e: e.tensor_scalar(tq2[:, :], tq[:, :], MAGIC, None, ALU.add))
        p.E("v", lambda e: e.tensor_scalar(tq2[:, :], tq2[:, :], -MAGIC, None, ALU.add))
        p.E("v", lambda e: e.tensor_tensor(tq[:, :], tq[:, :], tq2[:, :], ALU.subtract))
        p.E("v", lambda e: e.tensor_scalar(tq[:, :], tq[:, :], TWOPI, None, ALU.mult),
            inc=svx)
        p.E("s", lambda e, sdst=sdst, v=svx.n: e.activation(sdst[:, :], tq[:, :], AF.Sin),
            wait=[(svx, svx.n)], inc=srow)
        p.E("s", lambda e, cdst=cdst: e.activation(cdst[:, :], tq[:, :], AF.Sin,
                                                   scale=0.5), inc=srow)
        p.E("v", lambda e, cdst=cdst, v=srow.n: e.tensor_tensor(
            cdst[:, :], cdst[:, :], cdst[:, :], ALU.mult), wait=[(srow, srow.n)])
        p.E("v", lambda e, cdst=cdst: e.tensor_scalar(
            cdst[:, :], cdst[:, :], -2.0, 1.0, ALU.mult, ALU.add))

    for b in range(N2):
        nu_c = cols[:, 6 + b:7 + b]
        emit_phase(b, 0, sph[b], cph[b], [(svx, cols_done)])
        emit_phase(b, 1, sph1[b], cph1[b], [])
        p.E("v", lambda e, b=b, nu_c=nu_c: e.tensor_scalar(
            tq2[:, :], cj[:, :], 1.0, nu_c, ALU.add, ALU.mult), inc=svx)
        p.E("s", lambda e, b=b, v=svx.n: e.activation(
            magp[b][:, :], tq2[:, :], AF.Exp, scale=-1.0),
            wait=[(svx, svx.n)], inc=srow)
        p.E("v", lambda e, b=b, mg=cols[:, 8 + b:9 + b], v=srow.n: e.tensor_scalar(
            magb[b][:, :], onesw[:, :], mg, None, ALU.mult),
            wait=[(srow, srow.n)], inc=svx)
    tables_done = svx.n

    RSTD, MU, R2 = 0, 1100, 2200
    xa_t = {}
    for kt in range(NKT):
        xb = xtr[kt % 2]
        sxp = sxa[kt % 2]
        w = [(smm, xa_t[kt - 2])] if kt >= 2 else []
        p.E("y", lambda e, kt=kt, xb=xb: e.dma_start(
            out=xb[:], in_=x_in[kt * 128:(kt + 1) * 128, :]),
            wait=w, inc=sxp, inck=16)
        ld_t = sxp.n
        x2 = x2b[0]
        wsq = [(sxp, ld_t)]
        if kt >= 1:
            wsq.append((smm, xa_t[kt - 1]))
        p.E("s", lambda e, xb=xb, x2=x2: e.activation(x2[:], xb[:], AF.Square),
            wait=wsq, inc=srow)
        sq_t = srow.n
        for cg, (c0, c1) in enumerate(cgs):
            p.E("t", lambda e, kt=kt, c0=c0, c1=c1, cg=cg, xb=xb, ld_t=ld_t, sxp=sxp: e.matmul(
                ps[cg][0:1, 0:c1 - c0], ones_c[:, 0:1], xb[:, c0:c1],
                start=(kt == 0), stop=(kt == NKT - 1)),
                wait=[(sxp, ld_t), (sx, all_ld)] if cg == 0 else [])
            p.E("t", lambda e, kt=kt, c0=c0, c1=c1, cg=cg, x2=x2, sq_t=sq_t: e.matmul(
                ps[2 + cg][0:1, 0:c1 - c0], ones_c[:, 0:1], x2[:, c0:c1],
                start=(kt == 0), stop=(kt == NKT - 1)),
                wait=[(srow, sq_t)],
                inc=smm if cg == len(cgs) - 1 else None)
        xa_t[kt] = smm.n
    stats_done = smm.n
    for cg, (c0, c1) in enumerate(cgs):
        n = c1 - c0
        p.E("v", lambda e, cg=cg, c0=c0, n=n: e.tensor_scalar(
            rows[0:1, MU + c0:MU + c0 + n], ps[cg][0:1, 0:n], -1.0 / D, None, ALU.mult),
            wait=[(smm, stats_done)] if cg == 0 else [])
        p.E("v", lambda e, cg=cg, c0=c0, n=n: e.tensor_scalar(
            rows[0:1, RSTD + c0:RSTD + c0 + n], ps[2 + cg][0:1, 0:n], 1.0 / D, None, ALU.mult))
    for h2 in range(2):
        p.E("v", lambda e, h2=h2: e.tensor_tensor(
            tmpv[0:1, 0:512], rows[0:1, MU + h2 * 512:MU + h2 * 512 + 512],
            rows[0:1, MU + h2 * 512:MU + h2 * 512 + 512], ALU.mult))
        p.E("v", lambda e, h2=h2: e.tensor_tensor(
            rows[0:1, RSTD + h2 * 512:RSTD + h2 * 512 + 512],
            rows[0:1, RSTD + h2 * 512:RSTD + h2 * 512 + 512],
            tmpv[0:1, 0:512], ALU.subtract), inc=svx if h2 == 1 else None)
    var_t = svx.n
    p.E("s", lambda e, var_t=var_t: e.activation(
        rows[0:1, RSTD:RSTD + width], rows[0:1, RSTD:RSTD + width], AF.Ln,
        bias=cmisc[0:1, 0:1]), wait=[(svx, var_t)], inc=srow)
    p.E("s", lambda e: e.activation(
        rows[0:1, RSTD:RSTD + width], rows[0:1, RSTD:RSTD + width], AF.Exp,
        scale=-0.5), inc=srow)
    p.E("v", lambda e, v=srow.n: e.engine_nop(), wait=[(srow, srow.n)])
    p.E("v", lambda e: e.tensor_tensor(
        rows[0:1, R2:R2 + width], rows[0:1, MU:MU + width],
        rows[0:1, RSTD:RSTD + width], ALU.mult), inc=svx)
    rows_done = svx.n

    for cg, (c0, c1) in enumerate(cgs):
        n = c1 - c0
        p.E("t", lambda e, c0=c0, n=n, cg=cg: e.matmul(
            ps[4 + 2 * cg][:, 0:n], ones_r[0:1, :], rows[0:1, RSTD + c0:RSTD + c0 + n],
            start=True, stop=True),
            wait=[(svx, rows_done)] if cg == 0 else [])
        p.E("t", lambda e, c0=c0, n=n, cg=cg: e.matmul(
            ps[5 + 2 * cg][:, 0:n], ones_r[0:1, :], rows[0:1, R2 + c0:R2 + c0 + n],
            start=True, stop=True), inc=smm)
    bcB_t = smm.n
    xnv = {}
    for kt in range(NKT):
        xb = xtr[kt % 2]
        sxp = sxa[kt % 2]
        w = [(svx, xnv[kt - 2])] if kt >= 2 else [(svx, svx.n)]
        p.E("y", lambda e, kt=kt, xb=xb: e.dma_start(
            out=xb[:], in_=x_in[kt * 128:(kt + 1) * 128, :]),
            wait=w, inc=sxp, inck=16)
        ld_t = sxp.n
        for cg, (c0, c1) in enumerate(cgs):
            n = c1 - c0
            p.E("v", lambda e, xb=xb, c0=c0, c1=c1, n=n, cg=cg, ld_t=ld_t, sxp=sxp: e.tensor_tensor(
                tmpv[:, 0:n], xb[:, c0:c1], ps[4 + 2 * cg][:, 0:n], ALU.mult),
                wait=[(sxp, ld_t), (smm, bcB_t)])
            p.E("v", lambda e, kt=kt, c0=c0, n=n, cg=cg: e.tensor_tensor(
                xn[kt][:, c0:c0 + n], tmpv[:, 0:n], ps[5 + 2 * cg][:, 0:n], ALU.add),
                inc=svx)
        xnv[kt] = svx.n
    xn_done = svx.n

    bu_ev = {}
    gi = 0
    for b in range(N2):
        for comp, (bt, dst) in enumerate(((BT, bur), (BT2, bui))):
            for th in range(2):
                t0 = th * 512
                pb = ps[gi % 2]
                for kt in range(NKT):
                    w = []
                    if kt == 0:
                        w = [(svx, xn_done), (sg, wload), (svx, bu_ev.get(gi - 2, 0))]
                    p.E("t", lambda e, kt=kt, b=b, t0=t0, pb=pb, bt=bt: e.matmul(
                        pb[:, 0:512], bt[:, kt * 256 + b * 128:kt * 256 + b * 128 + 128],
                        xn[kt][:, t0:t0 + 512],
                        start=(kt == 0), stop=(kt == NKT - 1)),
                        wait=w, inc=smm if kt == NKT - 1 else None)
                mm_t = smm.n
                p.E("v", lambda e, b=b, t0=t0, pb=pb, dst=dst, mm_t=mm_t,
                    gc=cols[:, 10 + b:11 + b]: e.tensor_scalar(
                    dst[b][:, t0:t0 + 512], pb[:, 0:512], gc, None, ALU.mult),
                    wait=[(smm, mm_t), (svx, tables_done)], inc=svx)
                bu_ev[gi] = svx.n
                gi += 1
    bu_done = svx.n

    scan_done = {}
    for b in range(N2):
        p.E("v", lambda e, b=b: e.tensor_tensor(tq[:, :], cph[b][:, :], bur[b][:, :],
                                                ALU.mult), wait=[(svx, bu_done)])
        p.E("v", lambda e, b=b: e.tensor_tensor(tq2[:, :], sph[b][:, :], bui[b][:, :],
                                                ALU.mult))
        p.E("v", lambda e, b=b: e.tensor_tensor(wre[b][:, :], tq[:, :], tq2[:, :],
                                                ALU.add))
        p.E("v", lambda e, b=b: e.tensor_tensor(tq[:, :], cph[b][:, :], bui[b][:, :],
                                                ALU.mult))
        p.E("v", lambda e, b=b: e.tensor_tensor(tq2[:, :], sph[b][:, :], bur[b][:, :],
                                                ALU.mult))
        p.E("v", lambda e, b=b: e.tensor_tensor(wim[b][:, :], tq[:, :], tq2[:, :],
                                                ALU.subtract))
        p.E("v", lambda e, b=b: e.tensor_tensor_scan(
            bur[b][:, :], magb[b][:, :], wre[b][:, :], 0.0, ALU.mult, ALU.add))
        p.E("v", lambda e, b=b: e.tensor_tensor_scan(
            bui[b][:, :], magb[b][:, :], wim[b][:, :], 0.0, ALU.mult, ALU.add))
        p.E("v", lambda e, b=b: e.tensor_tensor(tq[:, :], cph[b][:, :], bur[b][:, :],
                                                ALU.mult))
        p.E("v", lambda e, b=b: e.tensor_tensor(tq2[:, :], sph[b][:, :], bui[b][:, :],
                                                ALU.mult))
        p.E("v", lambda e, b=b: e.tensor_tensor(wre[b][:, :], tq[:, :], tq2[:, :],
                                                ALU.subtract))
        p.E("v", lambda e, b=b: e.tensor_tensor(tq[:, :], cph[b][:, :], bui[b][:, :],
                                                ALU.mult))
        p.E("v", lambda e, b=b: e.tensor_tensor(tq2[:, :], sph[b][:, :], bur[b][:, :],
                                                ALU.mult))
        p.E("v", lambda e, b=b: e.tensor_tensor(wim[b][:, :], tq[:, :], tq2[:, :],
                                                ALU.add), inc=svx)
        scan_done[b] = svx.n
    h_local_done = svx.n

    for b in range(N2):
        p.E("v", lambda e, b=b: e.tensor_copy(
            fins[:, 2 * b:2 * b + 1], wre[b][:, width - 1:width]),
            wait=[(svx, h_local_done)])
        p.E("v", lambda e, b=b: e.tensor_copy(
            fins[:, 2 * b + 1:2 * b + 2], wim[b][:, width - 1:width]),
            inc=svx if b == N2 - 1 else None)
    fin_t = svx.n
    p.E("y", lambda e, v=fin_t: e.dma_start(out=agin[:, :], in_=fins[:, 0:4]),
        wait=[(svx, fin_t)], inc=sx, inck=16)
    ag_ld = sx.n
    p.E("g", lambda e, v=ag_ld: e.collective_compute(
        "AllGather", ALU.bypass, replica_groups=[list(range(NCORES))],
        ins=[agin[:, :].opt()], outs=[agout[:, :].opt()]),
        wait=[(sx, ag_ld)], inc=scc)
    p.E("g", lambda e, v=scc.n: e.dma_start(
        out=fins[:, 0:32].rearrange("p (r c) -> p r c", r=8),
        in_=agout[:, :].rearrange("(r p) c -> p r c", p=128)),
        wait=[(scc, scc.n)], inc=sg, inck=16)
    fins_ld = sg.n

    p.E("v", lambda e, v=fins_ld: e.memzero(cols[:, 24:28]), wait=[(sg, fins_ld)])
    for r in range(8):
        fof = 4 * r
        if r % 4 == 0:
            p.E("v", lambda e, fof=fof: e.tensor_copy(cols[:, 16:20], fins[:, fof:fof + 4]))
        else:
            for b in range(N2):
                are = cols[:, 12 + b:13 + b]
                aim = cols[:, 14 + b:15 + b]
                pre = cols[:, 16 + 2 * b:17 + 2 * b]
                pim = cols[:, 17 + 2 * b:18 + 2 * b]
                p.E("v", lambda e, pre=pre, are=are: e.tensor_scalar(
                    cols[:, 20:21], pre, are, None, ALU.mult))
                p.E("v", lambda e, pim=pim, aim=aim: e.tensor_scalar(
                    cols[:, 21:22], pim, aim, None, ALU.mult))
                p.E("v", lambda e, pre=pre, aim=aim: e.tensor_scalar(
                    cols[:, 22:23], pre, aim, None, ALU.mult))
                p.E("v", lambda e, pim=pim, are=are: e.tensor_scalar(
                    cols[:, 23:24], pim, are, None, ALU.mult))
                p.E("v", lambda e: e.tensor_tensor(
                    cols[:, 20:21], cols[:, 20:21], cols[:, 21:22], ALU.subtract))
                p.E("v", lambda e: e.tensor_tensor(
                    cols[:, 22:23], cols[:, 22:23], cols[:, 23:24], ALU.add))
                p.E("v", lambda e, b=b, fof=fof, pre=pre: e.tensor_tensor(
                    pre, cols[:, 20:21], fins[:, fof + 2 * b:fof + 2 * b + 1], ALU.add))
                p.E("v", lambda e, b=b, fof=fof, pim=pim: e.tensor_tensor(
                    pim, cols[:, 22:23], fins[:, fof + 2 * b + 1:fof + 2 * b + 2], ALU.add))
        for q in range(4):
            p.E("v", lambda e, q=q, r=r: e.scalar_tensor_tensor(
                cols[:, 24 + q:25 + q], cols[:, 16 + q:17 + q], msel[:, r:r + 1],
                cols[:, 24 + q:25 + q], ALU.mult, ALU.add),
                inc=svx if (q == 3 and r == 7) else None)
    carry_done = svx.n

    for b in range(N2):
        cre = cols[:, 24 + 2 * b:25 + 2 * b]
        cim = cols[:, 25 + 2 * b:26 + 2 * b]
        p.E("v", lambda e, b=b, cre=cre: e.tensor_scalar(
            tq[:, :], cph1[b][:, :], cre, None, ALU.mult), wait=[(svx, carry_done)])
        p.E("v", lambda e, b=b, cim=cim: e.tensor_scalar(
            tq2[:, :], sph1[b][:, :], cim, None, ALU.mult))
        p.E("v", lambda e: e.tensor_tensor(tq[:, :], tq[:, :], tq2[:, :], ALU.subtract))
        p.E("v", lambda e, b=b: e.tensor_tensor(tq[:, :], tq[:, :], magp[b][:, :],
                                                ALU.mult))
        p.E("v", lambda e, b=b: e.tensor_tensor(wre[b][:, :], wre[b][:, :], tq[:, :],
                                                ALU.add))
        p.E("v", lambda e, b=b, cim=cim: e.tensor_scalar(
            tq[:, :], cph1[b][:, :], cim, None, ALU.mult))
        p.E("v", lambda e, b=b, cre=cre: e.tensor_scalar(
            tq2[:, :], sph1[b][:, :], cre, None, ALU.mult))
        p.E("v", lambda e: e.tensor_tensor(tq[:, :], tq[:, :], tq2[:, :], ALU.add))
        p.E("v", lambda e, b=b: e.tensor_tensor(tq[:, :], tq[:, :], magp[b][:, :],
                                                ALU.mult))
        p.E("v", lambda e, b=b: e.tensor_tensor(wim[b][:, :], wim[b][:, :], tq[:, :],
                                                ALU.add))
        p.E("v", lambda e, b=b: e.tensor_copy(hbr[b][:, :], wre[b][:, :]))
        p.E("v", lambda e, b=b: e.tensor_scalar(hbi[b][:, :], wim[b][:, :], -1.0, None,
                                                ALU.mult), inc=svx)
    h_done = svx.n

    yst = {}
    sst_b = (sst2[0].n, sst2[1].n)
    for kt in range(NKT):
        xb = xtr[kt % 2]
        sxp = sxa[kt % 2]
        w = [(svx, h_done)]
        if kt >= 2:
            w.append((svx, yst[kt - 2]))
        p.E("y", lambda e, kt=kt, xb=xb: e.dma_start(
            out=xb[:], in_=x_in[kt * 128:(kt + 1) * 128, :]),
            wait=w, inc=sxp, inck=16)
        ld_t = sxp.n
        for th in range(2):
            t0 = th * 512
            pb = ps[6 + (kt * 2 + th) % 2]
            for b in range(N2):
                p.E("t", lambda e, kt=kt, b=b, t0=t0, pb=pb: e.matmul(
                    pb[:, 0:512], CT[:, b * D + kt * 128:b * D + kt * 128 + 128],
                    hbr[b][:, t0:t0 + 512],
                    start=(b == 0), stop=False),
                    wait=[(svx, h_done), (svx, yst.get(kt - 1, 0))] if b == 0 and th == 0 else [])
                p.E("t", lambda e, kt=kt, b=b, t0=t0, pb=pb: e.matmul(
                    pb[:, 0:512], CT2[:, b * D + kt * 128:b * D + kt * 128 + 128],
                    hbi[b][:, t0:t0 + 512],
                    start=False, stop=(b == N2 - 1)),
                    inc=smm if b == N2 - 1 else None)
            mm_t = smm.n
            ob = xob[th]
            wst = [(smm, mm_t), (sxp, ld_t)]
            if kt >= 1:
                wst.append((sst2[th], sst_b[th] + 16 * kt))
            p.E("v", lambda e, kt=kt, xb=xb, t0=t0, ob=ob,
                dc=dcol[:, kt:kt + 1]: e.scalar_tensor_tensor(
                ob[:, :], xn[kt][:, t0:t0 + 512], dc, xb[:, t0:t0 + 512],
                ALU.mult, ALU.add), wait=wst)
            p.E("v", lambda e, ob=ob, pb=pb: e.tensor_tensor(
                ob[:, :], ob[:, :], pb[:, 0:512], ALU.add), inc=svx)
            ev_t = svx.n
            p.E("y", lambda e, kt=kt, t0=t0, ob=ob, ev_t=ev_t: e.dma_start(
                out=xo[kt * 128:(kt + 1) * 128, t0:t0 + 512], in_=ob[:, :]),
                wait=[(svx, ev_t)], inc=sst2[th], inck=16)
        yst[kt] = svx.n
    stA, stBv = sst2[0].n, sst2[1].n
    p.q["y"].append(lambda e: e.wait_ge(sst2[0].h, stA))
    p.q["y"].append(lambda e: e.wait_ge(sst2[1].h, stBv))
    return p


def run_lru(x_fm_slices, wts):
    p = build_lru()
    nc = p.finish()
    in_maps = []
    for c in range(NCORES):
        m = {"x": x_fm_slices[c], "mSEL": _lru_msel(c)}
        m.update(wts)
        m.update(_CONSTS)
        in_maps.append(m)
    res = run_bass_kernel_spmd(nc, in_maps, core_ids=list(range(NCORES)))
    return [r["xo"] for r in res.results]


def _lru_msel(c):
    b, k = c // 4, c % 4
    m = np.zeros((128, 8), np.float32)
    if k > 0:
        m[:, 4 * b + k - 1] = 1.0
    return m


def lru_weights(inputs):
    return {
        "BTre": np.ascontiguousarray(np.asarray(inputs["L1_attn_B_re"], np.float32).T),
        "BTim": np.ascontiguousarray(np.asarray(inputs["L1_attn_B_im"], np.float32).T),
        "CTre": np.ascontiguousarray(np.asarray(inputs["L1_attn_C_re"], np.float32).T),
        "CTim": np.ascontiguousarray(np.asarray(inputs["L1_attn_C_im"], np.float32).T),
        "nu_log": np.ascontiguousarray(np.asarray(inputs["L1_attn_nu_log"], np.float32).reshape(2, 128).T),
        "theta_log": np.ascontiguousarray(np.asarray(inputs["L1_attn_theta_log"], np.float32).reshape(2, 128).T),
        "Dvec": np.ascontiguousarray(np.asarray(inputs["L1_attn_D"], np.float32).reshape(NKT, 128).T),
    }




def build_attn_pre(p=None, io=None, prefix=""):
    width = W
    if p is None:
        p = P("attnpre")
    p.stage(prefix)

    x_in = p.io_inp(io, "x", (D, width))
    wq_in = p.inp("WqP", (D, 2048))
    wk_in = p.inp("WkP", (D, 2048))
    wv_in = p.inp("Wv", (D, 3072))
    wg_in = p.inp("Wg", (D, 3072))
    wab_in = p.inp("Wab", (D, 64))
    cqw_in = p.inp("convQ", (128, 16 * 4))
    ckw_in = p.inp("convK", (128, 16 * 4))
    cvw_in = p.inp("convV", (128, 24 * 4))
    rms1_in = p.inp("rms1P", (128, NKT))
    adt_in = p.inp("adt", (128, 4))
    hmask_in = p.inp("hmask", (128, 4))
    sel48_in = p.inp("cSEL48", (128, 2))
    bcq_in = p.inp("cBCQ", (2, 128))
    bck_in = p.inp("cBCK", (2, 128))
    ones_in = p.inp("cONESC", (128, 1))
    onesr_in = p.inp("cONESR", (1, 128))
    cmisc_in = p.inp("cMISC", (128, 8))
    qf_o = p.io_out(io, "qf", (2048, TPC), BF16)
    kf_o = p.io_out(io, "kf", (2048, TPC), BF16)
    vf_o = p.io_out(io, "vf", (3072, TPC), BF16)
    gate_o = p.io_out(io, "gate", (3072, TPC), BF16)
    g_o = p.io_out(io, "gout", (32, TPC))
    b_o = p.io_out(io, "bout", (32, TPC))

    xtr = [p.sbuf(f"xtr{i}", (128, width)) for i in range(2)]
    x2b = p.sbuf("x2b0", (128, width))
    rows = p.sbuf("rows", (1, 3250))
    hb = [p.sbuf(f"h{i}", (128, width), BF16) for i in range(NKT)]
    hf = p.sbuf("hf", (128, 512))
    wabf = p.sbuf("wabf", (128, NKT * 64))
    wbuf = [p.sbuf(f"wb{i}", (128, NKT * 128), BF16) for i in range(2)]
    ph = p.sbuf("ph", (128, width))
    pc = p.sbuf("pc", (128, width))
    cw = p.sbuf("cw", (128, (16 + 16 + 24) * 4))
    rms1 = p.sbuf("rms1_sb", (128, NKT))
    adt = p.sbuf("adt_sb", (128, 4))
    hmask = p.sbuf("hmask_sb", (128, 4))
    sel48 = p.sbuf("sel48", (128, 2))
    bcq = p.sbuf("bcq", (2, 128))
    bck = p.sbuf("bck", (2, 128))
    rsq = p.sbuf("rsq", (2, 1024))
    gb = p.sbuf("gb", (64, width))
    gb2 = p.sbuf("gb2", (64, width))
    tmpv = p.sbuf("tmpv", (128, 512))
    obuf = [p.sbuf(f"ob{i}", (128, TPC), BF16) for i in range(2)]
    ones_c = p.sbuf("onesc", (128, 1))
    ones_r = p.sbuf("onesr", (1, 128))
    cmisc = p.sbuf("cmisc", (128, 8))

    ps = [p.psum(f"ps{i}") for i in range(8)]

    sx = p.sem("sx")
    sg = p.sem("sg")
    sxa = [p.sem("sxa0"), p.sem("sxa1")]
    smm = p.sem("smm")
    srow = p.sem("srow")
    svx = p.sem("svx")
    sev = p.sem("sev")
    sst = p.sem("sst")

    RSTD, MU, R2 = 0, 1100, 2200

    for dst, src_ in ((ones_c, ones_in), (ones_r, onesr_in), (cmisc, cmisc_in),
                      (rms1, rms1_in), (adt, adt_in), (hmask, hmask_in),
                      (sel48, sel48_in), (bcq, bcq_in), (bck, bck_in)):
        p.E("y", lambda e, dst=dst, src_=src_: e.dma_start(out=dst[:], in_=src_[:, :]),
            inc=sx, inck=16)
    p.E("y", lambda e: e.dma_start(out=cw[:, 0:64], in_=cqw_in[:, :]), inc=sx, inck=16)
    p.E("y", lambda e: e.dma_start(out=cw[:, 64:128], in_=ckw_in[:, :]), inc=sx, inck=16)
    p.E("y", lambda e: e.dma_start(out=cw[:, 128:224], in_=cvw_in[:, :]), inc=sx, inck=16)
    p.E("y", lambda e: e.dma_start(
        out=wabf[:, :].rearrange("p (k c) -> p k c", k=NKT),
        in_=wab_in[:, :].rearrange("(k p) c -> p k c", p=128)), inc=sx, inck=16)
    all_ld = sx.n

    cg3 = [(0, 512), (512, 1024), (1024, width)]

    xa_t = {}
    for kt in range(NKT):
        xb = xtr[kt % 2]
        sxp = sxa[kt % 2]
        w = [(smm, xa_t[kt - 2])] if kt >= 2 else []
        p.E("y", lambda e, kt=kt, xb=xb: e.dma_start(
            out=xb[:], in_=x_in[kt * 128:(kt + 1) * 128, :]),
            wait=w, inc=sxp, inck=16)
        ld_t = sxp.n
        wsq = [(sxp, ld_t)]
        if kt >= 1:
            wsq.append((smm, xa_t[kt - 1]))
        p.E("s", lambda e, xb=xb: e.activation(x2b[:], xb[:], AF.Square),
            wait=wsq, inc=srow)
        sq_t = srow.n
        for cg, (c0, c1) in enumerate(cg3):
            pa = ps[0] if cg == 0 else (ps[1] if cg == 1 else ps[2])
            pq = ps[4] if cg == 0 else (ps[5] if cg == 1 else ps[6])
            p.E("t", lambda e, kt=kt, c0=c0, c1=c1, xb=xb, ld_t=ld_t, sxp=sxp, pa=pa: e.matmul(
                pa[0:1, 0:c1 - c0], ones_c[:, 0:1], xb[:, c0:c1],
                start=(kt == 0), stop=(kt == NKT - 1)),
                wait=[(sxp, ld_t), (sx, all_ld)] if cg == 0 else [])
            p.E("t", lambda e, kt=kt, c0=c0, c1=c1, sq_t=sq_t, pq=pq: e.matmul(
                pq[0:1, 0:c1 - c0], ones_c[:, 0:1], x2b[:, c0:c1],
                start=(kt == 0), stop=(kt == NKT - 1)),
                wait=[(srow, sq_t)],
                inc=smm if cg == 2 else None)
        xa_t[kt] = smm.n
    stats_done = smm.n
    for cg, (c0, c1) in enumerate(cg3):
        n = c1 - c0
        pa = ps[cg]
        pq = ps[4 + cg]
        p.E("v", lambda e, c0=c0, n=n, pa=pa: e.tensor_scalar(
            rows[0:1, MU + c0:MU + c0 + n], pa[0:1, 0:n], -1.0 / D, None, ALU.mult),
            wait=[(smm, stats_done)] if cg == 0 else [])
        p.E("v", lambda e, c0=c0, n=n, pq=pq: e.tensor_scalar(
            rows[0:1, RSTD + c0:RSTD + c0 + n], pq[0:1, 0:n], 1.0 / D, None, ALU.mult))
    for h2 in range(3):
        c0 = h2 * 512
        n = min(512, width - c0)
        p.E("v", lambda e, c0=c0, n=n: e.tensor_tensor(
            tmpv[0:1, 0:n], rows[0:1, MU + c0:MU + c0 + n],
            rows[0:1, MU + c0:MU + c0 + n], ALU.mult))
        p.E("v", lambda e, c0=c0, n=n: e.tensor_tensor(
            rows[0:1, RSTD + c0:RSTD + c0 + n], rows[0:1, RSTD + c0:RSTD + c0 + n],
            tmpv[0:1, 0:n], ALU.subtract), inc=svx if h2 == 2 else None)
    var_t = svx.n
    p.E("v", lambda e, var_t=var_t: e.tensor_scalar(
        rows[0:1, R2:R2 + width], rows[0:1, RSTD:RSTD + width], float(LN_EPS), None,
        ALU.add), wait=[(svx, var_t)])
    p.E("v", lambda e: e.reciprocal(rows[0:1, R2:R2 + width], rows[0:1, R2:R2 + width]))
    p.E("v", lambda e: e.tensor_tensor(
        rows[0:1, RSTD:RSTD + width], rows[0:1, RSTD:RSTD + width],
        rows[0:1, R2:R2 + width], ALU.mult))
    p.E("v", lambda e: e.tensor_scalar(
        rows[0:1, RSTD:RSTD + width], rows[0:1, RSTD:RSTD + width], float(RMS_EPS),
        None, ALU.add), inc=svx)
    q_t = svx.n
    p.E("s", lambda e, q_t=q_t: e.activation(
        rows[0:1, RSTD:RSTD + width], rows[0:1, RSTD:RSTD + width], AF.Ln),
        wait=[(svx, q_t)], inc=srow)
    p.E("s", lambda e: e.activation(
        rows[0:1, RSTD:RSTD + width], rows[0:1, RSTD:RSTD + width], AF.Exp,
        scale=-0.5), inc=srow)
    p.E("s", lambda e: e.activation(
        rows[0:1, R2:R2 + width], rows[0:1, R2:R2 + width], AF.Ln), inc=srow)
    p.E("s", lambda e: e.activation(
        rows[0:1, R2:R2 + width], rows[0:1, R2:R2 + width], AF.Exp,
        scale=0.5), inc=srow)
    p.E("v", lambda e, v=srow.n: e.tensor_tensor(
        rows[0:1, RSTD:RSTD + width], rows[0:1, RSTD:RSTD + width],
        rows[0:1, R2:R2 + width], ALU.mult),
        wait=[(srow, srow.n)])
    p.E("v", lambda e: e.tensor_tensor(
        rows[0:1, R2:R2 + width], rows[0:1, MU:MU + width],
        rows[0:1, RSTD:RSTD + width], ALU.mult), inc=svx)
    rows_done = svx.n

    hv = {}
    bc_by_cg = {}
    for cg, (c0, c1) in enumerate(cg3):
        n = c1 - c0
        pA = ps[2 + cg]
        pB = ps[5 + cg]
        wv0 = [(svx, rows_done)] if cg == 0 else [(svx, svx.n)]
        p.E("t", lambda e, c0=c0, n=n, pA=pA: e.matmul(
            pA[:, 0:n], ones_r[0:1, :], rows[0:1, RSTD + c0:RSTD + c0 + n],
            start=True, stop=True), wait=wv0)
        p.E("t", lambda e, c0=c0, n=n, pB=pB: e.matmul(
            pB[:, 0:n], ones_r[0:1, :], rows[0:1, R2 + c0:R2 + c0 + n],
            start=True, stop=True), inc=smm)
        bc_by_cg[cg] = (pA, pB, smm.n)
    abv = 0
    for kt in range(NKT):
        xb = xtr[kt % 2]
        sxp = sxa[kt % 2]
        w = [(svx, hv[kt - 2])] if kt >= 2 else [(svx, svx.n)]
        p.E("y", lambda e, kt=kt, xb=xb: e.dma_start(
            out=xb[:], in_=x_in[kt * 128:(kt + 1) * 128, :]),
            wait=w, inc=sxp, inck=16)
        ld_t = sxp.n
        for cg, (c0, c1) in enumerate(cg3):
            n = c1 - c0
            pA, pB, bt = bc_by_cg[cg]
            p.E("v", lambda e, xb=xb, c0=c0, n=n, pA=pA, ld_t=ld_t, sxp=sxp, bt=bt: e.tensor_tensor(
                tmpv[:, 0:n], xb[:, c0:c0 + n], pA[:, 0:n], ALU.mult),
                wait=[(sxp, ld_t), (smm, bt)])
            p.E("v", lambda e, c0=c0, n=n, pB=pB: e.tensor_tensor(
                tmpv[:, 0:n], tmpv[:, 0:n], pB[:, 0:n], ALU.add))
            p.E("v", lambda e, kt=kt, n=n: e.tensor_scalar(
                hf[:, 0:n], tmpv[:, 0:n], rms1[:, kt:kt + 1], None, ALU.mult),
                inc=svx)
            hf_t = svx.n
            p.E("v", lambda e, kt=kt, c0=c0, n=n: e.tensor_copy(
                hb[kt][:, c0:c0 + n], hf[:, 0:n]), inc=svx)
            p.E("t", lambda e, kt=kt, n=n, hf_t=hf_t: e.matmul(
                ps[0][0:64, 0:n], wabf[:, kt * 64:(kt + 1) * 64], hf[:, 0:n],
                start=True, stop=True),
                wait=[(svx, hf_t), (svx, abv)], inc=smm)
            ab_mt = smm.n
            if kt == 0:
                p.E("v", lambda e, c0=c0, n=n, ab_mt=ab_mt: e.tensor_copy(
                    gb[0:64, c0:c0 + n], ps[0][0:64, 0:n]),
                    wait=[(smm, ab_mt)], inc=svx)
            else:
                p.E("v", lambda e, c0=c0, n=n, ab_mt=ab_mt: e.tensor_tensor(
                    gb[0:64, c0:c0 + n], gb[0:64, c0:c0 + n], ps[0][0:64, 0:n],
                    ALU.add),
                    wait=[(smm, ab_mt)], inc=svx)
            abv = svx.n
        hv[kt] = svx.n
    h_done = svx.n

    pev = {}
    wfree = {}
    ost = []
    sq_free = []
    gidx = 0

    def emit_tile(w_in, gmt, out_ap, kind, conv_base):
        nonlocal gidx
        wb = wbuf[gidx % 2]
        wld = [(smm, wfree.get(gidx - 2, 0))]
        p.E("g", lambda e, w_in=w_in, gmt=gmt, wb=wb: e.dma_start(
            out=wb[:, :].rearrange("p (k c) -> p k c", k=NKT),
            in_=w_in[:, gmt * 128:(gmt + 1) * 128].rearrange("(k p) c -> p k c", p=128)),
            wait=wld, inc=sg, inck=16)
        ld_t = sg.n
        pb = ps[gidx % 2]
        for cg, (c0, c1) in enumerate(cg3):
            n = c1 - c0
            for kt in range(NKT):
                wq = []
                if kt == 0:
                    if cg == 0:
                        wq = [(sg, ld_t), (svx, h_done), (svx, pev.get(gidx - 2, 0))]
                    else:
                        wq = [(svx, svx.n)]
                p.E("t", lambda e, kt=kt, c0=c0, n=n, pb=pb, wb=wb: e.matmul(
                    pb[:, 0:n],
                    wb[:, kt * 128:(kt + 1) * 128], hb[kt][:, c0:c0 + n],
                    start=(kt == 0), stop=(kt == NKT - 1)),
                    wait=wq, inc=smm if kt == NKT - 1 else None)
            mm_t = smm.n
            p.E("v", lambda e, c0=c0, n=n, pb=pb, mm_t=mm_t: e.tensor_copy(
                ph[:, c0:c0 + n], pb[:, 0:n]),
                wait=[(smm, mm_t)], inc=svx)
        pev[gidx] = svx.n
        wfree[gidx] = smm.n
        gi_here = gidx
        gidx += 1
        ev = pev[gi_here]
        if kind == "gate":
            ob = obuf[gi_here % 2]
            wob = [(svx, ev)]
            if len(ost) >= 2:
                wob.append((sst, ost[-2]))
            p.E("v", lambda e, ob=ob, ev=ev: e.tensor_copy(
                ob[:, :], ph[:, HALO:width]),
                wait=wob, inc=svx)
            ev2 = svx.n
            p.E("y", lambda e, gmt=gmt, ob=ob, ev2=ev2, out_ap=out_ap: e.dma_start(
                out=out_ap[gmt * 128:(gmt + 1) * 128, :], in_=ob[:, :]),
                wait=[(svx, ev2)], inc=sst, inck=16)
            ost.append(sst.n)
            return
        p.E("v", lambda e, ev=ev: e.tensor_tensor(
            ph[:, 0:HALO], ph[:, 0:HALO], hmask[:, 0:HALO], ALU.mult),
            wait=[(svx, ev)])
        cwc0 = conv_base + gmt * 4
        p.E("v", lambda e, cwc0=cwc0: e.tensor_scalar(
            pc[:, 0:TPC], ph[:, 0:TPC], cw[:, cwc0:cwc0 + 1], None, ALU.mult))
        for tap in range(1, 4):
            p.E("v", lambda e, cwc0=cwc0, tap=tap: e.scalar_tensor_tensor(
                pc[:, 0:TPC], ph[:, tap:TPC + tap], cw[:, cwc0 + tap:cwc0 + tap + 1],
                pc[:, 0:TPC], ALU.mult, ALU.add),
                inc=svx if tap == 3 else None)
        cv_t = svx.n
        p.E("s", lambda e, cv_t=cv_t: e.activation(pc[:, 0:TPC], pc[:, 0:TPC], AF.Silu),
            wait=[(svx, cv_t)], inc=srow)
        sil_t = srow.n
        if kind == "v":
            ob = obuf[gi_here % 2]
            wob = [(srow, sil_t)]
            if len(ost) >= 2:
                wob.append((sst, ost[-2]))
            p.E("v", lambda e, ob=ob: e.tensor_copy(
                ob[:, :], pc[:, 0:TPC]),
                wait=wob, inc=svx)
            ev2 = svx.n
            p.E("y", lambda e, gmt=gmt, ob=ob, ev2=ev2, out_ap=out_ap: e.dma_start(
                out=out_ap[gmt * 128:(gmt + 1) * 128, :], in_=ob[:, :]),
                wait=[(svx, ev2)], inc=sst, inck=16)
            ost.append(sst.n)
            return
        p.E("s", lambda e, sil_t=sil_t: e.activation(x2b[:, 0:TPC], pc[:, 0:TPC],
                                                     AF.Square), inc=srow)
        ssq_t = srow.n
        bcw = bcq if kind == "q" else bck
        ob = obuf[gi_here % 2]
        for th in range(2):
            t0 = th * 512
            wsum = [(srow, ssq_t)] + list(sq_free)
            p.E("t", lambda e, t0=t0: e.matmul(
                ps[4][0:2, 0:512], sel48[:, 0:2], x2b[:, t0:t0 + 512],
                start=True, stop=True),
                wait=wsum, inc=smm)
            smt = smm.n
            p.E("s", lambda e, t0=t0, smt=smt: e.activation(
                rsq[0:2, t0:t0 + 512], ps[4][0:2, 0:512], AF.Ln,
                bias=cmisc[0:2, 2:3]), wait=[(smm, smt)], inc=sev)
            p.E("s", lambda e, t0=t0: e.activation(
                rsq[0:2, t0:t0 + 512], rsq[0:2, t0:t0 + 512], AF.Exp,
                scale=-0.5), inc=sev)
            rq_t = sev.n
            p.E("v", lambda e, v=rq_t: e.engine_nop(), wait=[(sev, rq_t)], inc=svx)
            rr_t = svx.n
            p.E("t", lambda e, t0=t0, rr_t=rr_t, bcw=bcw: e.matmul(
                ps[5][:, 0:512], bcw[0:2, :], rsq[0:2, t0:t0 + 512],
                start=True, stop=True),
                wait=[(svx, rr_t)], inc=smm)
            bmt = smm.n
            wob = [(smm, bmt)]
            if th == 0 and len(ost) >= 2:
                wob.append((sst, ost[-2]))
            p.E("v", lambda e, t0=t0, ob=ob, bmt=bmt: e.tensor_tensor(
                ob[:, t0:t0 + 512], pc[:, t0:t0 + 512], ps[5][:, 0:512], ALU.mult),
                wait=wob, inc=svx)
            sq_free.clear()
            sq_free.append((svx, svx.n))
        ev2 = svx.n
        p.E("y", lambda e, gmt=gmt, ob=ob, ev2=ev2, out_ap=out_ap: e.dma_start(
            out=out_ap[gmt * 128:(gmt + 1) * 128, :], in_=ob[:, :]),
            wait=[(svx, ev2)], inc=sst, inck=16)
        ost.append(sst.n)

    for gmt in range(16):
        emit_tile(wq_in, gmt, qf_o, "q", 0)
    for gmt in range(16):
        emit_tile(wk_in, gmt, kf_o, "k", 64)
    for gmt in range(24):
        emit_tile(wv_in, gmt, vf_o, "v", 128)
    for gmt in range(24):
        emit_tile(wg_in, gmt, gate_o, "gate", 0)

    ab_t = svx.n
    p.E("v", lambda e, v=ab_t: e.tensor_scalar(
        gb2[0:32, :], gb[0:32, :], adt[0:32, 1:2], None, ALU.add),
        wait=[(svx, ab_t)])
    p.E("v", lambda e: e.tensor_scalar(
        gb[0:32, :], gb2[0:32, :], -1.0, None, ALU.mult))
    p.E("v", lambda e: e.tensor_tensor(
        gb[0:32, :], gb[0:32, :], gb2[0:32, :], ALU.max))
    p.E("v", lambda e: e.tensor_scalar(
        ph[0:32, :], gb2[0:32, :], 0.0, None, ALU.max), inc=svx)
    sp_t = svx.n
    p.E("s", lambda e, sp_t=sp_t: e.activation(
        gb[0:32, :], gb[0:32, :], AF.Exp, scale=-1.0),
        wait=[(svx, sp_t)], inc=srow)
    p.E("s", lambda e: e.activation(gb[0:32, :], gb[0:32, :], AF.Ln, bias=1.0),
        inc=srow)
    p.E("s", lambda e: e.activation(adt[0:32, 2:3], adt[0:32, 0:1], AF.Exp), inc=srow)
    sp2_t = srow.n
    p.E("v", lambda e, sp2_t=sp2_t: e.tensor_scalar(
        adt[0:32, 2:3], adt[0:32, 2:3], -1.0, None, ALU.mult), wait=[(srow, sp2_t)])
    p.E("v", lambda e: e.tensor_tensor(
        gb[0:32, :], gb[0:32, :], ph[0:32, :], ALU.add))
    p.E("v", lambda e: e.tensor_scalar(
        gb[0:32, :], gb[0:32, :], adt[0:32, 2:3], None, ALU.mult), inc=svx)
    g_t = svx.n
    p.E("s", lambda e, g_t=g_t: e.activation(
        gb[32:64, :], gb[32:64, :], AF.Sigmoid), wait=[(svx, g_t)], inc=srow)
    beta_t = srow.n
    p.E("y", lambda e, g_t=g_t: e.dma_start(out=g_o[:, :], in_=gb[0:32, HALO:width]),
        wait=[(svx, g_t)], inc=sst, inck=16)
    p.E("y", lambda e, beta_t=beta_t: e.dma_start(out=b_o[:, :], in_=gb[32:64, HALO:width]),
        wait=[(srow, beta_t)], inc=sst, inck=16)
    if DBG:
        dbg_a = p.outp("dbg_a2", (32, width))
        p.E("y", lambda e, g_t=g_t: e.dma_start(out=dbg_a[:, :], in_=gb2[0:32, :]),
            wait=[(svx, g_t)], inc=sst, inck=16)
    if DBG:
        dbg_h = p.outp("dbg_h", (D, width), BF16)
        for kt in range(NKT):
            p.E("y", lambda e, kt=kt: e.dma_start(
                out=dbg_h[kt * 128:(kt + 1) * 128, :], in_=hb[kt][:]),
                wait=[(svx, h_done)], inc=sst, inck=16)
    stf = sst.n
    p.q["y"].append(lambda e: e.wait_ge(sst.h, stf))
    return p


def attn_pre_weights(inputs):
    Wq = np.asarray(inputs["L0_attn_Wq"], np.float32)
    Wk = np.asarray(inputs["L0_attn_Wk"], np.float32)
    Wv = np.asarray(inputs["L0_attn_Wv"], np.float32)
    Wg = np.asarray(inputs["L0_attn_Wg"], np.float32)
    Wa = np.asarray(inputs["L0_attn_Wa"], np.float32)
    Wb = np.asarray(inputs["L0_attn_Wb"], np.float32)
    cq = np.asarray(inputs["L0_attn_conv_q_w"], np.float32)
    ck = np.asarray(inputs["L0_attn_conv_k_w"], np.float32)
    cv = np.asarray(inputs["L0_attn_conv_v_w"], np.float32)

    def padqk(Wx):
        o = np.zeros((D, 2048), np.float32)
        o3 = o.reshape(D, 32, 64)
        o3[:, :, 0:48] = Wx.reshape(D, 32, 48)
        return np.ascontiguousarray(o)

    def padconv(c):
        o = np.zeros((2048, 4), np.float32)
        o.reshape(32, 64, 4)[:, 0:48, :] = c.reshape(32, 48, 4)
        return o

    def coltile(v, nt):
        return np.ascontiguousarray(v.reshape(nt, 128, -1).transpose(1, 0, 2).reshape(128, -1))

    adt = np.zeros((128, 4), np.float32)
    adt[0:32, 0] = np.asarray(inputs["L0_attn_A_log"], np.float32)
    adt[0:32, 1] = np.asarray(inputs["L0_attn_dt_bias"], np.float32)
    sel48 = np.zeros((128, 2), np.float32)
    sel48[0:48, 0] = 1.0
    sel48[64:112, 1] = 1.0
    bcq = np.zeros((2, 128), np.float32)
    bcq[0, 0:48] = 48.0 ** -0.5
    bcq[1, 64:112] = 48.0 ** -0.5
    bck = np.zeros((2, 128), np.float32)
    bck[0, 0:48] = 1.0
    bck[1, 64:112] = 1.0
    return {
        "WqP": padqk(Wq), "WkP": padqk(Wk),
        "Wv": np.ascontiguousarray(Wv), "Wg": np.ascontiguousarray(Wg),
        "Wab": np.ascontiguousarray(np.concatenate([Wa, Wb], 1)),
        "convQ": coltile(padconv(cq), 16), "convK": coltile(padconv(ck), 16),
        "convV": coltile(cv, 24),
        "rms1P": coltile(np.asarray(inputs["L0_attn_rms1_w"], np.float32), NKT),
        "adt": adt, "cSEL48": sel48, "cBCQ": bcq, "cBCK": bck,
    }



def build_delta(NH=H, p=None, io=None, prefix=""):
    if p is None:
        p = P("delta")
    p.stage(prefix)
    NC = 8
    MBIG = -1.0e5

    qf_in = p.io_inp(io, "qf", (NH * 64, TPC), BF16)
    kf_in = p.io_inp(io, "kf", (NH * 64, TPC), BF16)
    vf_in = p.io_inp(io, "vf", (NH * 96, TPC), BF16)
    g_in = p.io_inp(io, "gout", (NH, TPC))
    b_in = p.io_inp(io, "bout", (NH, TPC))
    rst_in = p.inp("cRST", (NH, TPC))
    msel_in = p.inp("mSELD", (48, 8))
    mstr_in = p.inp("cMSTR", (128, 128))
    mstrT_in = p.inp("cMSTRT", (128, 128))
    minct_in = p.inp("cMINCT", (128, 128))
    idb_in = p.inp("cIDB", (128, 128), BF16)
    idf_in = p.inp("cIDF", (128, 128))
    onesr_in = p.inp("cONESR", (1, 128))
    cmisc_in = p.inp("cMISC", (128, 8))
    o_out = p.io_out(io, "oout", (TPC, NH * 96), BF16)

    oloc_d = p.dram("olocD", (TPC, NH * 96), BF16)
    gq_d = p.dram("gqD", (48, NH * NC * 128), BF16)
    tb_d = p.dram("tbD", (48, NH * NC * 144))
    cs_d = p.dram("csD", (NH, TPC))
    agin = p.dram("aginD", (48, NH * 144))
    agout = p.dram("agoutD", (48 * NCORES, NH * 144))

    qh = [p.sbuf(f"qh{i}", (48, TPC), BF16) for i in range(2)]
    kh = [p.sbuf(f"kh{i}", (48, TPC), BF16) for i in range(2)]
    vh = [p.sbuf(f"vh{i}", (96, TPC), BF16) for i in range(2)]
    gsb = p.sbuf("gsb", (NH, TPC))
    bsb = p.sbuf("bsb", (NH, TPC))
    rstm = p.sbuf("rstm", (NH, TPC))
    cssb = p.sbuf("cssb", (NH, TPC))
    csst = [p.sbuf(f"csst{i}", (1, TPC)) for i in range(2)]
    negcs = [p.sbuf(f"negcs{i}", (1, TPC)) for i in range(2)]
    bst = [p.sbuf(f"bst{i}", (1, TPC)) for i in range(2)]
    rowscr = p.sbuf("rowscr", (1, 128))
    dstf = p.sbuf("dstf", (128, 128))
    dstTf = p.sbuf("dstTf", (128, 128))
    dincTf = p.sbuf("dincTf", (128, 128))
    lb = p.sbuf("lb", (128, 128), BF16)
    pbuf = [p.sbuf(f"pbuf{i}", (128, 128), BF16) for i in range(2)]
    ptall = p.sbuf("ptall", (128, 7 * 128), BF16)
    aqt = p.sbuf("aqt", (128, 128), BF16)
    ybuf = p.sbuf("ybuf", (128, 144), BF16)
    ktil = p.sbuf("ktil", (48, 128), BF16)
    ksb = p.sbuf("ksb", (128, 48), BF16)
    bbc = p.sbuf("bbc", (48, 128))
    gtmp = p.sbuf("gtmp", (48, 128))
    kv32 = p.sbuf("kv32", (96, 256))
    scr = p.sbuf("scr", (128, 8))
    scr48 = p.sbuf("scr48", (48, 48))
    olocb = [p.sbuf(f"olocb{i}", (128, 96), BF16) for i in range(2)]
    gqb = [p.sbuf(f"gqb{i}", (48, 128), BF16) for i in range(2)]
    tbb = [p.sbuf(f"tbb{i}", (48, 144)) for i in range(2)]
    tcrun = p.sbuf("tcrun", (48, 48))
    bcrun = p.sbuf("bcrun", (48, 96))
    sumb = [p.sbuf(f"sumb{i}", (48, 144)) for i in range(2)]
    sall = p.sbuf("sall", (48, NH * 96))
    sbf = p.sbuf("sbf", (48, 96), BF16)
    agld = [p.sbuf(f"agld{i}", (48, 144)) for i in range(2)]
    gql = [p.sbuf(f"gql{i}", (48, 128), BF16) for i in range(2)]
    tbl = [p.sbuf(f"tbl{i}", (48, 144)) for i in range(2)]
    olb = [p.sbuf(f"olb{i}", (128, 96), BF16) for i in range(2)]
    oob = [p.sbuf(f"oob{i}", (128, 96), BF16) for i in range(2)]
    msel = p.sbuf("msel_sb", (48, 8))
    mstr = p.sbuf("mstr_sb", (128, 128))
    mstrT = p.sbuf("mstrT_sb", (128, 128))
    minct = p.sbuf("minct_sb", (128, 128))
    idb = p.sbuf("idb_sb", (128, 128), BF16)
    idf = p.sbuf("idf_sb", (128, 128))
    ones_r = p.sbuf("onesr_sb", (1, 128))
    cmisc = p.sbuf("cmisc_sb", (128, 8))

    ps = [p.psum(f"ps{i}") for i in range(8)]

    sx = p.sem("sx")
    sqk = [p.sem("sqk0"), p.sem("sqk1")]
    srl = [p.sem("srl0"), p.sem("srl1")]
    smm = p.sem("smm")
    svx = p.sem("svx")
    srw = p.sem("srw")
    scc = p.sem("scc")
    ssp = [p.sem("ssp0"), p.sem("ssp1")]
    ssum = p.sem("ssum")
    sbl = [p.sem("sbl0"), p.sem("sbl1")]
    sst = p.sem("sst")

    one11 = cmisc[0:1, 1:2]

    for dst, src_ in ((msel, msel_in), (mstr, mstr_in), (mstrT, mstrT_in),
                      (minct, minct_in), (idb, idb_in), (idf, idf_in),
                      (ones_r, onesr_in), (cmisc, cmisc_in),
                      (gsb, g_in), (bsb, b_in), (rstm, rst_in)):
        p.E("y", lambda e, dst=dst, src_=src_: e.dma_start(out=dst[:], in_=src_[:, :]),
            inc=sx, inck=16)
    all_ld = sx.n

    p.E("v", lambda e, v=all_ld: e.tensor_tensor_scan(
        cssb[:, :], rstm[:, :], gsb[:, :], 0.0, ALU.mult, ALU.add),
        wait=[(sx, all_ld)], inc=svx)
    cs_t = svx.n
    p.E("y", lambda e, cs_t=cs_t: e.dma_start(out=cs_d[:, :], in_=cssb[:, :]),
        wait=[(svx, cs_t)], inc=sx, inck=16)
    csd_t = sx.n

    last_qkv_reader = {0: 0, 1: 0}
    last_row_reader = {0: 0, 1: 0}
    last_ps7_read = 0
    last_sp = {0: 0, 1: 0}
    dst_read = 0
    last_tp = 0
    prev_trc = 0

    for h in range(NH):
        par = h % 2
        wq = [(smm, last_qkv_reader[par])]
        p.E("y", lambda e, h=h, par=par: e.dma_start(
            out=qh[par][:, :], in_=qf_in[h * 64:h * 64 + 48, :]),
            wait=wq, inc=sqk[par], inck=16)
        p.E("y", lambda e, h=h, par=par: e.dma_start(
            out=kh[par][:, :], in_=kf_in[h * 64:h * 64 + 48, :]),
            inc=sqk[par], inck=16)
        p.E("y", lambda e, h=h, par=par: e.dma_start(
            out=vh[par][:, :], in_=vf_in[h * 96:h * 96 + 96, :]),
            inc=sqk[par], inck=16)
        qkv_t = sqk[par].n
        wr = [(smm, last_row_reader[par]), (sx, csd_t)]
        p.E("y", lambda e, h=h, par=par: e.dma_start(
            out=csst[par][:, :], in_=cs_d[h:h + 1, :]),
            wait=wr, inc=srl[par], inck=16)
        p.E("y", lambda e, h=h, par=par: e.dma_start(
            out=bst[par][:, :], in_=b_in[h:h + 1, :]),
            inc=srl[par], inck=16)
        row_t = srl[par].n
        p.E("v", lambda e, par=par, row_t=row_t: e.tensor_scalar(
            negcs[par][:, :], csst[par][:, :], -1.0, None, ALU.mult),
            wait=[(srl[par], row_t)], inc=svx)
        neg_t = svx.n
        p.E("v", lambda e, v=prev_trc: e.tensor_copy(tcrun[:, :], idf[0:48, 0:48]),
            wait=[(sx, all_ld), (smm, prev_trc)])
        p.E("v", lambda e: e.memzero(bcrun[:, :]), inc=svx)
        chain_t = svx.n

        for c in range(NC):
            c0 = c * 128
            p.E("v", lambda e, par=par, c0=c0: e.tensor_scalar(
                rowscr[0:1, :], negcs[par][0:1, c0:c0 + 128],
                csst[par][0:1, c0 + 127:c0 + 128], None, ALU.add),
                wait=[(svx, neg_t)], inc=svx)
            dec_t = svx.n
            p.E("t", lambda e, par=par, c0=c0: e.matmul(
                ps[0][:, 0:128], csst[par][0:1, c0:c0 + 128], ones_r[0:1, :],
                start=True, stop=False),
                wait=[(srl[par], row_t), (svx, dst_read), (sx, all_ld)])
            p.E("t", lambda e, par=par, c0=c0: e.matmul(
                ps[0][:, 0:128], ones_r[0:1, :], negcs[par][0:1, c0:c0 + 128],
                start=False, stop=True),
                wait=[(svx, neg_t)], inc=smm)
            od_t = smm.n
            p.E("t", lambda e, par=par, c0=c0, v=last_ps7_read: e.transpose(
                ps[7][:, 272:273], bst[par][0:1, c0:c0 + 128], one11),
                wait=[(svx, last_ps7_read)])
            p.E("t", lambda e, par=par, c0=c0: e.transpose(
                ps[7][:, 273:274], csst[par][0:1, c0:c0 + 128], one11))
            p.E("t", lambda e, dec_t=dec_t: e.transpose(
                ps[7][:, 274:275], rowscr[0:1, :], one11),
                wait=[(svx, dec_t)])
            p.E("t", lambda e, par=par, c0=c0: e.matmul(
                ps[7][0:48, 275:276], ones_r[0:1, 0:48],
                csst[par][0:1, c0 + 127:c0 + 128], start=True, stop=True))
            p.E("t", lambda e, par=par, c0=c0: e.matmul(
                ps[7][0:48, 144:272], ones_r[0:1, 0:48],
                bst[par][0:1, c0:c0 + 128], start=True, stop=True))
            p.E("t", lambda e, par=par, c0=c0: e.matmul(
                ps[7][0:48, 276:404], ones_r[0:1, 0:48],
                csst[par][0:1, c0:c0 + 128], start=True, stop=True))
            p.E("v", lambda e, par=par, c0=c0, qkv_t=qkv_t, v=last_tp: e.tensor_copy(
                kv32[0:48, 0:128], kh[par][:, c0:c0 + 128]),
                wait=[(sqk[par], qkv_t), (smm, last_tp)])
            p.E("v", lambda e, par=par, c0=c0: e.tensor_copy(
                kv32[0:96, 128:256], vh[par][:, c0:c0 + 128]), inc=svx)
            kv32_t = svx.n
            p.E("t", lambda e, kv32_t=kv32_t: e.transpose(
                ps[7][:, 0:48], kv32[0:48, 0:128], idf[0:48, 0:48]),
                wait=[(svx, kv32_t)])
            p.E("t", lambda e: e.transpose(
                ps[7][:, 48:144], kv32[0:96, 128:256], idf[0:96, 0:96]),
                inc=smm)
            tp_t = smm.n
            last_tp = tp_t
            p.E("s", lambda e, tp_t=tp_t: e.activation(
                scr[:, 0:1], ps[7][:, 272:273], AF.Identity),
                wait=[(smm, tp_t)])
            p.E("s", lambda e: e.activation(scr[:, 1:2], ps[7][:, 273:274], AF.Exp))
            p.E("s", lambda e: e.activation(scr[:, 2:3], ps[7][:, 274:275], AF.Exp))
            p.E("s", lambda e: e.activation(
                scr[0:48, 4:5], ps[7][0:48, 275:276], AF.Exp))
            p.E("s", lambda e: e.activation(
                bbc[0:48, :], ps[7][0:48, 276:404], AF.Exp), inc=srw)
            ev7_t = srw.n
            p.E("v", lambda e, ev7_t=ev7_t: e.tensor_tensor(
                scr[:, 3:4], scr[:, 0:1], scr[:, 1:2], ALU.mult),
                wait=[(srw, ev7_t)])
            p.E("v", lambda e, par=par, c0=c0: e.tensor_tensor(
                ktil[0:48, :], ps[7][0:48, 144:272], kh[par][0:48, c0:c0 + 128],
                ALU.mult))
            p.E("v", lambda e: e.tensor_scalar(
                ybuf[:, 0:96], ps[7][:, 48:144], scr[:, 0:1], None, ALU.mult))
            p.E("v", lambda e: e.tensor_scalar(
                ybuf[:, 96:144], ps[7][:, 0:48], scr[:, 3:4], None, ALU.mult))
            p.E("v", lambda e: e.tensor_scalar(
                ksb[:, 0:48], ps[7][:, 0:48], scr[:, 2:3], None, ALU.mult),
                inc=svx)
            last_ps7_read = svx.n
            ktil_t = last_ps7_read
            p.E("t", lambda e, par=par, c0=c0, ktil_t=ktil_t: e.matmul(
                ps[1][:, 0:128], ktil[0:48, :], kh[par][0:48, c0:c0 + 128],
                start=True, stop=True),
                wait=[(svx, ktil_t)])
            p.E("t", lambda e, par=par, c0=c0: e.matmul(
                ps[2][:, 0:128], kh[par][0:48, c0:c0 + 128], ktil[0:48, :],
                start=True, stop=True))
            p.E("t", lambda e, par=par, c0=c0: e.matmul(
                ps[3][:, 0:128], kh[par][0:48, c0:c0 + 128],
                qh[par][0:48, c0:c0 + 128], start=True, stop=True), inc=smm)
            gram_t = smm.n
            p.E("v", lambda e, od_t=od_t: e.tensor_tensor(
                dstf[:, :], ps[0][:, 0:128], mstr[:, :], ALU.add),
                wait=[(smm, od_t)])
            p.E("v", lambda e: e.scalar_tensor_tensor(
                dstTf[:, :], ps[0][:, 0:128], -1.0, mstrT[:, :],
                ALU.mult, ALU.add))
            p.E("v", lambda e: e.scalar_tensor_tensor(
                dincTf[:, :], ps[0][:, 0:128], -1.0, minct[:, :],
                ALU.mult, ALU.add), inc=svx)
            mk_t = svx.n
            p.E("s", lambda e, mk_t=mk_t: e.activation(
                dstf[:, :], dstf[:, :], AF.Exp), wait=[(svx, mk_t)])
            p.E("s", lambda e: e.activation(dstTf[:, :], dstTf[:, :], AF.Exp))
            p.E("s", lambda e: e.activation(dincTf[:, :], dincTf[:, :], AF.Exp),
                inc=srw)
            exp_t = srw.n
            p.E("v", lambda e, exp_t=exp_t, gram_t=gram_t: e.tensor_tensor(
                lb[:, :], ps[1][:, 0:128], dstf[:, :], ALU.mult),
                wait=[(srw, exp_t), (smm, gram_t)])
            p.E("v", lambda e: e.tensor_tensor(
                ptall[:, 0:128], ps[2][:, 0:128], dstTf[:, :], ALU.mult))
            p.E("v", lambda e: e.tensor_tensor(
                aqt[:, :], ps[3][:, 0:128], dincTf[:, :], ALU.mult), inc=svx)
            dst_read = svx.n
            laq_t = dst_read
            pcur = lb
            sq_done = laq_t
            for lev in range(1, 7):
                p.E("t", lambda e, lev=lev, pcur=pcur, sq_done=sq_done: e.matmul(
                    ps[4][:, 0:128], ptall[:, (lev - 1) * 128:lev * 128],
                    pcur[:, :], start=True, stop=True),
                    wait=[(svx, sq_done), (srw, srw.n)])
                p.E("t", lambda e, lev=lev, pcur=pcur: e.matmul(
                    ps[5][:, 0:128], pcur[:, :],
                    ptall[:, (lev - 1) * 128:lev * 128], start=True, stop=True),
                    inc=smm)
                sqm_t = smm.n
                pnext = pbuf[lev % 2]
                p.E("s", lambda e, pnext=pnext, sqm_t=sqm_t: e.activation(
                    pnext[:, :], ps[4][:, 0:128], AF.Identity),
                    wait=[(smm, sqm_t)])
                p.E("s", lambda e, lev=lev: e.activation(
                    ptall[:, lev * 128:(lev + 1) * 128], ps[5][:, 0:128],
                    AF.Identity), inc=srw)
                sq_done = 0
                pcur = pnext
            sqev_t = srw.n
            y_t = last_ps7_read
            for j in range(6, -1, -1):
                p.E("t", lambda e, j=j, y_t=y_t: e.matmul(
                    ps[6][:, 0:144], ptall[:, j * 128:(j + 1) * 128],
                    ybuf[:, :], start=True, stop=True),
                    wait=[(svx, y_t), (srw, sqev_t if j == 6 else 0)], inc=smm)
                ap_t = smm.n
                if j > 0:
                    p.E("v", lambda e, ap_t=ap_t: e.tensor_tensor(
                        ybuf[:, :], ps[6][:, 0:144], ybuf[:, :], ALU.add),
                        wait=[(smm, ap_t)], inc=svx)
                else:
                    p.E("v", lambda e, ap_t=ap_t: e.scalar_tensor_tensor(
                        ybuf[:, :], ps[6][:, 0:144], -1.0, ybuf[:, :],
                        ALU.mult, ALU.add),
                        wait=[(smm, ap_t)], inc=svx)
                y_t = svx.n
            u_t = y_t
            p.E("t", lambda e, u_t=u_t: e.matmul(
                ps[0][:, 0:96], aqt[:, :], ybuf[:, 0:96],
                start=True, stop=True), wait=[(svx, u_t)])
            p.E("t", lambda e: e.matmul(
                ps[1][0:48, 0:128], ybuf[:, 96:144], aqt[:, :],
                start=True, stop=True))
            p.E("t", lambda e: e.matmul(
                ps[2][0:48, 0:96], ksb[:, 0:48], ybuf[:, 0:96],
                start=True, stop=True))
            p.E("t", lambda e: e.matmul(
                ps[3][0:48, 0:48], ybuf[:, 96:144], ksb[:, 0:48],
                start=True, stop=True), inc=smm)
            om_t = smm.n
            par2 = c % 2
            wsp = [(ssp[par2], last_sp[par2])]
            p.E("s", lambda e, par2=par2, om_t=om_t, wsp=wsp: e.activation(
                olocb[par2][:, :], ps[0][:, 0:96], AF.Identity),
                wait=[(smm, om_t)] + wsp, inc=srw)
            oloc_t = srw.n
            p.E("v", lambda e, par=par, c0=c0, wsp=wsp, om_t=om_t: e.tensor_tensor(
                gtmp[0:48, :], bbc[0:48, :], qh[par][0:48, c0:c0 + 128],
                ALU.mult), wait=wsp + [(smm, om_t)])
            p.E("v", lambda e, par2=par2: e.tensor_tensor(
                gqb[par2][0:48, :], gtmp[0:48, :], ps[1][0:48, 0:128],
                ALU.subtract))
            p.E("v", lambda e: e.tensor_scalar(
                scr48[0:48, 0:48], idf[0:48, 0:48], scr[0:48, 4:5], None,
                ALU.mult))
            p.E("v", lambda e, par2=par2: e.scalar_tensor_tensor(
                tbb[par2][:, 0:48], ps[3][0:48, 0:48], -1.0, scr48[0:48, 0:48],
                ALU.mult, ALU.add))
            p.E("v", lambda e, par2=par2: e.tensor_copy(
                tbb[par2][:, 48:144], ps[2][0:48, 0:96]), inc=svx)
            ev_t = svx.n
            p.E("y", lambda e, h=h, c=c, par2=par2, oloc_t=oloc_t: e.dma_start(
                out=oloc_d[c * 128:(c + 1) * 128, h * 96:(h + 1) * 96],
                in_=olocb[par2][:, :]),
                wait=[(srw, oloc_t)], inc=ssp[par2], inck=16)
            p.E("y", lambda e, h=h, c=c, par2=par2, ev_t=ev_t: e.dma_start(
                out=gq_d[:, (h * NC + c) * 128:(h * NC + c + 1) * 128],
                in_=gqb[par2][0:48, :]),
                wait=[(svx, ev_t)], inc=ssp[par2], inck=16)
            p.E("y", lambda e, h=h, c=c, par2=par2: e.dma_start(
                out=tb_d[:, (h * NC + c) * 144:(h * NC + c + 1) * 144],
                in_=tbb[par2][:, :]),
                inc=ssp[par2], inck=16)
            last_sp[par2] = ssp[par2].n
            p.E("t", lambda e, par2=par2, ev_t=ev_t: e.matmul(
                ps[4][0:48, 0:48], tbb[par2][:, 0:48], tcrun[0:48, 0:48],
                start=True, stop=True),
                wait=[(svx, ev_t if c > 0 else max(ev_t, chain_t))])
            p.E("t", lambda e, par2=par2: e.matmul(
                ps[5][0:48, 0:96], tbb[par2][:, 0:48], bcrun[0:48, 0:96],
                start=True, stop=True), inc=smm)
            ch_t = smm.n
            p.E("v", lambda e, ch_t=ch_t: e.tensor_copy(
                tcrun[0:48, 0:48], ps[4][0:48, 0:48]), wait=[(smm, ch_t)])
            p.E("v", lambda e, par2=par2: e.tensor_tensor(
                bcrun[0:48, 0:96], ps[5][0:48, 0:96], tbb[par2][:, 48:144],
                ALU.add), inc=svx)
            chain_t = svx.n
        last_qkv_reader[par] = smm.n
        last_row_reader[par] = smm.n
        hpar = h % 2
        p.E("t", lambda e, chain_t=chain_t: e.transpose(
            ps[4][0:48, 0:48], tcrun[0:48, 0:48], idf[0:48, 0:48]),
            wait=[(svx, chain_t)], inc=smm)
        trc_t = smm.n
        prev_trc = trc_t
        p.E("v", lambda e, hpar=hpar, trc_t=trc_t, v=ssum.n: e.tensor_copy(
            sumb[hpar][:, 0:48], ps[4][0:48, 0:48]),
            wait=[(smm, trc_t), (ssum, ssum.n if h >= 2 else 0)])
        p.E("v", lambda e, hpar=hpar: e.tensor_copy(
            sumb[hpar][:, 48:144], bcrun[0:48, 0:96]), inc=svx)
        sum_t = svx.n
        p.E("y", lambda e, h=h, hpar=hpar, sum_t=sum_t: e.dma_start(
            out=agin[:, h * 144:(h + 1) * 144], in_=sumb[hpar][:, :]),
            wait=[(svx, sum_t)], inc=ssum, inck=16)

    ag_need = ssum.n
    p.E("g", lambda e, ag_need=ag_need: e.collective_compute(
        "AllGather", ALU.bypass, replica_groups=[list(range(NCORES))],
        ins=[agin[:, :].opt()], outs=[agout[:, :].opt()]),
        wait=[(ssum, ag_need)], inc=scc)
    ag_t = scc.n

    if DBG:
        dbg_cp = p.sbuf("dbg_cp", (48, 8 * 96))
    last_ag_reader = {0: 0, 1: 0}
    for h in range(NH):
        p.E("v", lambda e, h=h: e.memzero(sall[:, h * 96:(h + 1) * 96]))
        for r in range(8):
            pr = r % 2
            p.E("y", lambda e, h=h, r=r, pr=pr, ag_t=ag_t,
                 v=last_ag_reader[pr]: e.dma_start(
                out=agld[pr][:, :],
                in_=agout[48 * r:48 * (r + 1), h * 144:(h + 1) * 144]),
                wait=[(scc, ag_t), (svx, last_ag_reader[pr])],
                inc=sbl[pr], inck=16)
            ld_t = sbl[pr].n
            if r % 4 == 0:
                p.E("v", lambda e, pr=pr, ld_t=ld_t: e.tensor_copy(
                    bbc[0:48, 0:96], agld[pr][:, 48:144]),
                    wait=[(sbl[pr], ld_t)], inc=svx)
            else:
                p.E("t", lambda e, pr=pr, ld_t=ld_t, v=svx.n: e.matmul(
                    ps[2][0:48, 0:96], agld[pr][:, 0:48], bbc[0:48, 0:96],
                    start=True, stop=True),
                    wait=[(sbl[pr], ld_t), (svx, svx.n)], inc=smm)
                cm_t = smm.n
                p.E("v", lambda e, pr=pr, cm_t=cm_t: e.tensor_tensor(
                    bbc[0:48, 0:96], ps[2][0:48, 0:96], agld[pr][:, 48:144],
                    ALU.add), wait=[(smm, cm_t)], inc=svx)
            last_ag_reader[pr] = svx.n
            if DBG and h == 0:
                p.E("v", lambda e, r=r: e.tensor_copy(
                    dbg_cp[:, r * 96:(r + 1) * 96], bbc[0:48, 0:96]))
            p.E("v", lambda e, h=h, r=r: e.scalar_tensor_tensor(
                sall[:, h * 96:(h + 1) * 96], bbc[0:48, 0:96], msel[:, r:r + 1],
                sall[:, h * 96:(h + 1) * 96], ALU.mult, ALU.add), inc=svx)
            last_ag_reader[pr] = svx.n
    comp_t = svx.n
    if DBG:
        dbg_cpo = p.outp("dbg_cpo", (48, 8 * 96))
        p.E("y", lambda e, v=comp_t: e.dma_start(out=dbg_cpo[:, :], in_=dbg_cp[:, :]),
            wait=[(svx, comp_t)], inc=sst, inck=16)

    last_bl = {0: 0, 1: 0}
    prev_bm = 0
    sp_drain = [(ssp[0], last_sp[0]), (ssp[1], last_sp[1])]
    if DBG:
        dbg_sin = p.outp("dbg_sin", (48, NH * 96))
        sdbg = p.sem("sdbg")
        p.E("y", lambda e, v=comp_t: e.dma_start(out=dbg_sin[:, :], in_=sall[:, :]),
            wait=[(svx, comp_t)], inc=sdbg, inck=16)
        sp_drain = sp_drain + [(sdbg, sdbg.n)]
    for h in range(NH):
        for c in range(NC):
            pb = (h * NC + c) % 2
            wld = [(sbl[pb], last_bl[pb]), (svx, comp_t)]
            if h == 0 and c == 0:
                wld = wld + sp_drain
            p.E("y", lambda e, h=h, c=c, pb=pb, wld=wld: e.dma_start(
                out=gql[pb][0:48, :],
                in_=gq_d[:, (h * NC + c) * 128:(h * NC + c + 1) * 128]),
                wait=wld, inc=sbl[pb], inck=16)
            p.E("y", lambda e, h=h, c=c, pb=pb: e.dma_start(
                out=tbl[pb][:, :],
                in_=tb_d[:, (h * NC + c) * 144:(h * NC + c + 1) * 144]),
                inc=sbl[pb], inck=16)
            p.E("y", lambda e, h=h, c=c, pb=pb: e.dma_start(
                out=olb[pb][:, :],
                in_=oloc_d[c * 128:(c + 1) * 128, h * 96:(h + 1) * 96]),
                inc=sbl[pb], inck=16)
            ld_t = sbl[pb].n
            p.E("v", lambda e, h=h, v=svx.n, w=prev_bm: e.tensor_copy(
                sbf[:, :], sall[:, h * 96:(h + 1) * 96]),
                wait=[(svx, svx.n), (smm, prev_bm)], inc=svx)
            sbf_t = svx.n
            p.E("t", lambda e, pb=pb, ld_t=ld_t, sbf_t=sbf_t: e.matmul(
                ps[0][:, 0:96], gql[pb][0:48, :], sbf[:, :],
                start=True, stop=True),
                wait=[(sbl[pb], ld_t), (svx, sbf_t)])
            p.E("t", lambda e, h=h, pb=pb: e.matmul(
                ps[1][0:48, 0:96], tbl[pb][:, 0:48], sall[:, h * 96:(h + 1) * 96],
                start=True, stop=True), inc=smm)
            bm_t = smm.n
            prev_bm = bm_t
            p.E("v", lambda e, pb=pb, bm_t=bm_t, v=sst.n: e.tensor_tensor(
                oob[pb][:, :], ps[0][:, 0:96], olb[pb][:, :], ALU.add),
                wait=[(smm, bm_t), (sst, sst.n if (h * NC + c) >= 2 else 0)])
            p.E("v", lambda e, h=h, pb=pb: e.tensor_tensor(
                sall[:, h * 96:(h + 1) * 96], ps[1][0:48, 0:96],
                tbl[pb][:, 48:144], ALU.add), inc=svx)
            st_t = svx.n
            p.E("y", lambda e, h=h, c=c, pb=pb, st_t=st_t: e.dma_start(
                out=o_out[c * 128:(c + 1) * 128, h * 96:(h + 1) * 96],
                in_=oob[pb][:, :]),
                wait=[(svx, st_t)], inc=sst, inck=16)
            last_bl[pb] = sbl[pb].n
    fin = sst.n
    p.q["y"].append(lambda e: e.wait_ge(sst.h, fin))
    return p


def _delta_consts(NH=H):
    j = np.arange(128)
    mstr = np.where(j[:, None] > j[None, :], 0.0, -1.0e5).astype(np.float32)
    mstrT = np.where(j[:, None] < j[None, :], 0.0, -1.0e5).astype(np.float32)
    minct = np.where(j[:, None] <= j[None, :], 0.0, -1.0e5).astype(np.float32)
    rst = np.ones((NH, TPC), np.float32)
    rst[:, 0::128] = 0.0
    return {
        "cMSTR": mstr, "cMSTRT": mstrT, "cMINCT": minct,
        "cIDB": np.eye(128).astype(BF),
        "cRST": rst,
    }


def _delta_msel(c):
    b, k = c // 4, c % 4
    m = np.zeros((48, 8), np.float32)
    if k > 0:
        m[:, 4 * b + k - 1] = 1.0
    return m



def build_attn_post(p=None, io=None, prefix="", xc0=0):
    if p is None:
        p = P("attnpost")
    p.stage(prefix)
    NT = TPC
    NIT = 5632 // 128

    x_in = p.io_inp(io, "x", (D, NT + xc0))
    o_in = p.io_inp(io, "oD", (NT, 3072), BF16)
    gate_in = p.io_inp(io, "gateF", (3072, NT), BF16)
    gnw_in = p.inp("gnwB", (128, 3072))
    wo_in = p.inp("Wo", (3072, D))
    rms2_in = p.inp("rms2P", (128, NKT))
    w1_in = p.inp("W1", (D, 5632))
    w3_in = p.inp("W3", (D, 5632))
    w2_in = p.inp("W2", (5632, D))
    ones_in = p.inp("cONESC", (128, 1))
    onesr_in = p.inp("cONESR", (1, 128))
    cmisc_in = p.inp("cMISC", (128, 8))
    idf_in = p.inp("cIDF", (128, 128))
    xo_out = p.io_out(io, "xo", (D, NT))

    ogfm_d = p.dram("ogfmD", (3072, NT), BF16)

    xtr = [p.sbuf(f"xtr{i}", (128, NT)) for i in range(2)]
    x2b = p.sbuf("x2b", (128, NT))
    rows = p.sbuf("rows", (1, 2100))
    xob = p.sbuf("xob", (128, NKT * NT))
    ob = [p.sbuf(f"ob{i}", (128, 3072), BF16) for i in range(2)]
    ogb = p.sbuf("ogb", (128, 3072))
    osq = ogb
    ssb = p.sbuf("ssb", (128, 32))
    gnw = p.sbuf("gnw_sb", (128, 3072))
    gblk = [p.sbuf(f"gblk{i}", (128, 128), BF16) for i in range(2)]
    ogsb = [p.sbuf(f"ogsb{i}", (128, 128), BF16) for i in range(2)]
    wob = p.sbuf("wob", (128, 4 * D), BF16)
    ogt = p.sbuf("ogt", (128, 4 * NT), BF16)
    ffb = p.sbuf("ffb", (128, NKT * NT), BF16)
    w13 = [p.sbuf(f"w13_{i}", (128, 2 * NKT * 128), BF16) for i in range(2)]
    w2t = [p.sbuf(f"w2t{i}", (128, D), BF16) for i in range(2)]
    actb = p.sbuf("actb", (128, NT), BF16)
    tmpv = p.sbuf("tmpv", (128, 512))
    ones_c = p.sbuf("onesc_sb", (128, 1))
    ones_r = p.sbuf("onesr_sb", (1, 128))
    cmisc = p.sbuf("cmisc_sb", (128, 8))
    idf = p.sbuf("idf_sb", (128, 128))
    rms2 = p.sbuf("rms2_sb", (128, NKT))

    ps = [p.psum(f"ps{i}") for i in range(8)]

    sx = p.sem("sx")
    sxa = [p.sem("sxa0"), p.sem("sxa1")]
    sol = [p.sem("sol0"), p.sem("sol1")]
    sgb = [p.sem("sgb0"), p.sem("sgb1")]
    sog = [p.sem("sog0"), p.sem("sog1")]
    swl = p.sem("swl")
    sgl = p.sem("sgl")
    smm = p.sem("smm")
    svx = p.sem("svx")
    srw = p.sem("srw")
    sst = p.sem("sst")

    for dst, src_ in ((ones_c, ones_in), (ones_r, onesr_in), (cmisc, cmisc_in),
                      (idf, idf_in), (gnw, gnw_in), (rms2, rms2_in)):
        p.E("y", lambda e, dst=dst, src_=src_: e.dma_start(out=dst[:], in_=src_[:, :]),
            inc=sx, inck=16)
    all_ld = sx.n

    RSTD, R2 = 0, 1050
    cg2 = [(0, 512), (512, 1024)]

    xa_t = {}
    for kt in range(NKT):
        xb = xtr[kt % 2]
        sxp = sxa[kt % 2]
        w = [(smm, xa_t[kt - 2])] if kt >= 2 else []
        p.E("y", lambda e, kt=kt, xb=xb: e.dma_start(
            out=xb[:], in_=x_in[kt * 128:(kt + 1) * 128, xc0:xc0 + NT]),
            wait=w, inc=sxp, inck=16)
        ld_t = sxp.n
        wsq = [(sxp, ld_t)]
        if kt >= 1:
            wsq.append((smm, xa_t[kt - 1]))
        p.E("s", lambda e, xb=xb: e.activation(x2b[:], xb[:], AF.Square),
            wait=wsq, inc=srw)
        sq_t = srw.n
        for cg, (c0, c1) in enumerate(cg2):
            p.E("t", lambda e, kt=kt, c0=c0, c1=c1, xb=xb, ld_t=ld_t, sxp=sxp, cg=cg: e.matmul(
                ps[cg][0:1, 0:c1 - c0], ones_c[:, 0:1], xb[:, c0:c1],
                start=(kt == 0), stop=(kt == NKT - 1)),
                wait=[(sxp, ld_t), (sx, all_ld)] if cg == 0 else [])
            p.E("t", lambda e, kt=kt, c0=c0, c1=c1, sq_t=sq_t, cg=cg: e.matmul(
                ps[2 + cg][0:1, 0:c1 - c0], ones_c[:, 0:1], x2b[:, c0:c1],
                start=(kt == 0), stop=(kt == NKT - 1)),
                wait=[(srw, sq_t)], inc=smm if cg == 1 else None)
        xa_t[kt] = smm.n
    stats_done = smm.n
    for cg, (c0, c1) in enumerate(cg2):
        n = c1 - c0
        p.E("v", lambda e, c0=c0, n=n, cg=cg: e.tensor_scalar(
            rows[0:1, R2 + c0:R2 + c0 + n], ps[cg][0:1, 0:n], -1.0 / D, None,
            ALU.mult), wait=[(smm, stats_done)] if cg == 0 else [])
        p.E("v", lambda e, c0=c0, n=n, cg=cg: e.tensor_scalar(
            rows[0:1, RSTD + c0:RSTD + c0 + n], ps[2 + cg][0:1, 0:n], 1.0 / D,
            None, ALU.mult))
        p.E("v", lambda e, c0=c0, n=n: e.tensor_tensor(
            tmpv[0:1, 0:n], rows[0:1, R2 + c0:R2 + c0 + n],
            rows[0:1, R2 + c0:R2 + c0 + n], ALU.mult))
        p.E("v", lambda e, c0=c0, n=n: e.tensor_tensor(
            rows[0:1, RSTD + c0:RSTD + c0 + n], rows[0:1, RSTD + c0:RSTD + c0 + n],
            tmpv[0:1, 0:n], ALU.subtract), inc=svx if cg == 1 else None)
    var_t = svx.n
    p.E("s", lambda e, var_t=var_t: e.activation(
        rows[0:1, RSTD:RSTD + NT], rows[0:1, RSTD:RSTD + NT], AF.Ln,
        bias=cmisc[0:1, 0:1]), wait=[(svx, var_t)], inc=srw)
    p.E("s", lambda e: e.activation(
        rows[0:1, RSTD:RSTD + NT], rows[0:1, RSTD:RSTD + NT], AF.Exp,
        scale=-0.5), inc=srw)
    rstd_t = srw.n
    p.E("v", lambda e, rstd_t=rstd_t: e.tensor_tensor(
        rows[0:1, R2:R2 + NT], rows[0:1, R2:R2 + NT],
        rows[0:1, RSTD:RSTD + NT], ALU.mult), wait=[(srw, rstd_t)], inc=svx)
    rows_done = svx.n

    for cg, (c0, c1) in enumerate(cg2):
        n = c1 - c0
        p.E("t", lambda e, c0=c0, n=n, cg=cg, v=rows_done: e.matmul(
            ps[4 + cg][:, 0:n], ones_r[0:1, :], rows[0:1, RSTD + c0:RSTD + c0 + n],
            start=True, stop=True), wait=[(svx, rows_done)] if cg == 0 else [])
        p.E("t", lambda e, c0=c0, n=n, cg=cg: e.matmul(
            ps[6 + cg][:, 0:n], ones_r[0:1, :], rows[0:1, R2 + c0:R2 + c0 + n],
            start=True, stop=True), inc=smm)
    bc_t = smm.n
    ap_t = {}
    for kt in range(NKT):
        xb = xtr[kt % 2]
        sxp = sxa[kt % 2]
        w = [(svx, ap_t[kt - 2])] if kt >= 2 else [(smm, xa_t[NKT - 1])]
        p.E("y", lambda e, kt=kt, xb=xb: e.dma_start(
            out=xb[:], in_=x_in[kt * 128:(kt + 1) * 128, xc0:xc0 + NT]),
            wait=w, inc=sxp, inck=16)
        ld_t = sxp.n
        for cg, (c0, c1) in enumerate(cg2):
            n = c1 - c0
            p.E("v", lambda e, kt=kt, c0=c0, n=n, cg=cg, xb=xb, ld_t=ld_t,
                 sxp=sxp, bc_t=bc_t: e.tensor_tensor(
                tmpv[:, 0:n], xb[:, c0:c0 + n], ps[4 + cg][:, 0:n], ALU.mult),
                wait=[(sxp, ld_t), (smm, bc_t)])
            p.E("v", lambda e, kt=kt, c0=c0, n=n, cg=cg: e.tensor_tensor(
                xob[:, kt * NT + c0:kt * NT + c0 + n], tmpv[:, 0:n],
                ps[6 + cg][:, 0:n], ALU.add), inc=svx)
        ap_t[kt] = svx.n
    ln_done = svx.n

    tr_ev = 0
    ob_rd = {0: 0, 1: 0}
    gb_rd = {0: 0, 1: 0}
    for c in range(8):
        par = c % 2
        p.E("y", lambda e, c=c, par=par, v=ob_rd[par]: e.dma_start(
            out=ob[par][:, :], in_=o_in[c * 128:(c + 1) * 128, :]),
            wait=[(svx, ob_rd[par])], inc=sol[par], inck=16)
        ol_t = sol[par].n
        p.E("s", lambda e, par=par, ol_t=ol_t: e.activation(
            osq[:, :], ob[par][:, :], AF.Square),
            wait=[(sol[par], ol_t)], inc=srw)
        osq_t = srw.n
        for h in range(H):
            p.E("v", lambda e, h=h, osq_t=osq_t: e.reduce_sum(
                ssb[:, h:h + 1], osq[:, h * 96:(h + 1) * 96],
                axis=mybir.AxisListType.X),
                wait=[(srw, osq_t)] if h == 0 else [])
        p.E("v", lambda e: e.tensor_scalar(
            ssb[:, :], ssb[:, :], 1.0 / 96.0, None, ALU.mult), inc=svx)
        ss_t = svx.n
        p.E("s", lambda e, ss_t=ss_t: e.activation(
            ssb[:, :], ssb[:, :], AF.Ln, bias=cmisc[:, 0:1]),
            wait=[(svx, ss_t)], inc=srw)
        p.E("s", lambda e: e.activation(
            ssb[:, :], ssb[:, :], AF.Exp, scale=-0.5), inc=srw)
        rs_t = srw.n
        for h in range(H):
            p.E("v", lambda e, h=h, par=par, rs_t=rs_t: e.tensor_scalar(
                ogb[:, h * 96:(h + 1) * 96], ob[par][:, h * 96:(h + 1) * 96],
                ssb[:, h:h + 1], None, ALU.mult),
                wait=[(rs_t and (srw, rs_t))] if h == 0 else [])
        p.E("v", lambda e: e.tensor_tensor(
            ogb[:, :], ogb[:, :], gnw[:, :], ALU.mult), inc=svx)
        og_t = svx.n
        ob_rd[par] = og_t
        for ft in range(24):
            fp = ft % 2
            p.E("y", lambda e, c=c, ft=ft, fp=fp, v=gb_rd[fp]: e.dma_start(
                out=gblk[fp][:, :],
                in_=gate_in[ft * 128:(ft + 1) * 128, c * 128:(c + 1) * 128]),
                wait=[(svx, gb_rd[fp])], inc=sgb[fp], inck=16)
            gl_t = sgb[fp].n
            p.E("s", lambda e, fp=fp, gl_t=gl_t: e.activation(
                gblk[fp][:, :], gblk[fp][:, :], AF.Silu),
                wait=[(sgb[fp], gl_t)], inc=srw)
            sil_t = srw.n
            pb = ps[ft % 2]
            p.E("t", lambda e, ft=ft, pb=pb, og_t=og_t, tr_ev=tr_ev: e.transpose(
                pb[:, 0:128], ogb[:, ft * 128:(ft + 1) * 128], idf[:, :]),
                wait=[(svx, og_t), (srw, tr_ev)], inc=smm)
            tr_t = smm.n
            p.E("s", lambda e, ft=ft, fp=fp, pb=pb, tr_t=tr_t, sil_t=sil_t,
                 v=sog[fp].n: e.activation(
                ogsb[fp][:, :], pb[:, 0:128], AF.Identity),
                wait=[(smm, tr_t), (srw, sil_t), (sog[fp], sog[fp].n)], inc=srw)
            p.E("v", lambda e, fp=fp, v=srw.n: e.tensor_tensor(
                ogsb[fp][:, :], ogsb[fp][:, :], gblk[fp][:, :], ALU.mult),
                wait=[(srw, srw.n)], inc=svx)
            ogs_t = svx.n
            gb_rd[fp] = ogs_t
            tr_ev = srw.n
            p.E("y", lambda e, c=c, ft=ft, fp=fp, ogs_t=ogs_t: e.dma_start(
                out=ogfm_d[ft * 128:(ft + 1) * 128, c * 128:(c + 1) * 128],
                in_=ogsb[fp][:, :]),
                wait=[(svx, ogs_t)], inc=sog[fp], inck=16)
        ogdma = max(sog[0].n, sog[1].n)
    og_done = [(sog[0], sog[0].n), (sog[1], sog[1].n)]

    evac_t = {b: ln_done for b in range(6)}
    wo_grp = 0
    for ktg in range(6):
        wld = og_done if ktg == 0 else []
        if ktg >= 1:
            wld = wld + [(svx, wo_grp)]
        p.E("g", lambda e, ktg=ktg, wld=wld: e.dma_start(
            out=wob[:, :].rearrange("p (k c) -> p k c", k=4),
            in_=wo_in[ktg * 512:(ktg + 1) * 512, :].rearrange(
                "(k p) c -> p k c", p=128)),
            wait=wld, inc=sgl, inck=16)
        p.E("y", lambda e, ktg=ktg, wld=wld: e.dma_start(
            out=ogt[:, :].rearrange("p (k c) -> p k c", k=4),
            in_=ogfm_d[ktg * 512:(ktg + 1) * 512, :].rearrange(
                "(k p) c -> p k c", p=128)),
            wait=wld, inc=swl, inck=16)
        wo_ld = (sgl.n, swl.n)
        for mt in range(NKT):
            for half in range(2):
                bank = (mt * 2 + half) % 6
                wmm = [(sgl, wo_ld[0]), (swl, wo_ld[1]),
                       (svx, evac_t[bank])]
                for k4 in range(4):
                    p.E("t", lambda e, mt=mt, half=half, k4=k4, bank=bank,
                         wmm=wmm: e.matmul(
                        ps[bank][:, 0:512],
                        wob[:, k4 * D + mt * 128:k4 * D + (mt + 1) * 128],
                        ogt[:, k4 * NT + half * 512:k4 * NT + (half + 1) * 512],
                        start=(k4 == 0), stop=(k4 == 3)),
                        wait=wmm if k4 == 0 else [],
                        inc=smm if k4 == 3 else None)
                mm_t = smm.n
                p.E("v", lambda e, mt=mt, half=half, bank=bank, mm_t=mm_t: e.tensor_tensor(
                    xob[:, mt * NT + half * 512:mt * NT + (half + 1) * 512],
                    xob[:, mt * NT + half * 512:mt * NT + (half + 1) * 512],
                    ps[bank][:, 0:512], ALU.add),
                    wait=[(smm, mm_t)], inc=svx)
                evac_t[bank] = svx.n
        wo_grp = svx.n
    h2_done = svx.n

    p4_mm = {}
    for kt in range(NKT):
        wsq = [(svx, h2_done)] if kt == 0 else [(smm, p4_mm[kt - 1])]
        p.E("s", lambda e, kt=kt, wsq=wsq: e.activation(
            x2b[:, :], xob[:, kt * NT:(kt + 1) * NT], AF.Square),
            wait=wsq, inc=srw)
        sq_t = srw.n
        for cg in range(2):
            p.E("t", lambda e, kt=kt, cg=cg, sq_t=sq_t: e.matmul(
                ps[cg][0:1, 0:512], ones_c[:, 0:1],
                x2b[:, cg * 512:(cg + 1) * 512],
                start=(kt == 0), stop=(kt == NKT - 1)),
                wait=[(srw, sq_t)], inc=smm if cg == 1 else None)
        p4_mm[kt] = smm.n
    ms_done = smm.n
    for cg in range(2):
        p.E("v", lambda e, cg=cg, v=ms_done: e.tensor_scalar(
            rows[0:1, RSTD + cg * 512:RSTD + (cg + 1) * 512],
            ps[cg][0:1, 0:512], 1.0 / D, None, ALU.mult),
            wait=[(smm, ms_done)] if cg == 0 else [], inc=svx)
    ms_t = svx.n
    p.E("s", lambda e, ms_t=ms_t: e.activation(
        rows[0:1, RSTD:RSTD + NT], rows[0:1, RSTD:RSTD + NT], AF.Ln,
        bias=cmisc[0:1, 0:1]), wait=[(svx, ms_t)], inc=srw)
    p.E("s", lambda e: e.activation(
        rows[0:1, RSTD:RSTD + NT], rows[0:1, RSTD:RSTD + NT], AF.Exp,
        scale=-0.5), inc=srw)
    rst2_t = srw.n
    for cg in range(2):
        p.E("t", lambda e, cg=cg, rst2_t=rst2_t: e.matmul(
            ps[4 + cg][:, 0:512], ones_r[0:1, :],
            rows[0:1, RSTD + cg * 512:RSTD + (cg + 1) * 512],
            start=True, stop=True),
            wait=[(srw, rst2_t)] if cg == 0 else [], inc=smm)
    bc2_t = smm.n
    for kt in range(NKT):
        for cg in range(2):
            p.E("v", lambda e, kt=kt, cg=cg, bc2_t=bc2_t: e.tensor_tensor(
                tmpv[:, 0:512], xob[:, kt * NT + cg * 512:kt * NT + (cg + 1) * 512],
                ps[4 + cg][:, 0:512], ALU.mult),
                wait=[(smm, bc2_t)] if kt == 0 and cg == 0 else [])
            p.E("v", lambda e, kt=kt, cg=cg: e.tensor_scalar(
                ffb[:, kt * NT + cg * 512:kt * NT + (cg + 1) * 512],
                tmpv[:, 0:512], rms2[:, kt:kt + 1], None, ALU.mult),
                inc=svx)
    ff_done = svx.n

    w2ev = {}
    ps45_sv = 0
    ps45_sr = 0
    last_w13 = {0: 0, 1: 0}
    last_w2 = {0: 0, 1: 0}
    act_rd = 0
    for it in range(NIT):
        wp = it % 2
        p.E("g", lambda e, it=it, wp=wp, v=last_w13[wp]: e.dma_start(
            out=w13[wp][:, 0:NKT * 128].rearrange("p (k c) -> p k c", k=NKT),
            in_=w1_in[:, it * 128:(it + 1) * 128].rearrange(
                "(k p) c -> p k c", p=128)),
            wait=[(smm, last_w13[wp])], inc=sgl, inck=16)
        p.E("g", lambda e, it=it, wp=wp: e.dma_start(
            out=w13[wp][:, NKT * 128:2 * NKT * 128].rearrange(
                "p (k c) -> p k c", k=NKT),
            in_=w3_in[:, it * 128:(it + 1) * 128].rearrange(
                "(k p) c -> p k c", p=128)),
            inc=sgl, inck=16)
        p.E("g", lambda e, it=it, wp=wp, v=last_w2[wp]: e.dma_start(
            out=w2t[wp][:, :], in_=w2_in[it * 128:(it + 1) * 128, :]),
            wait=[(smm, last_w2[wp])], inc=sgl, inck=16)
        w13_t = sgl.n
        for half in range(2):
            wmm = [(sgl, w13_t), (svx, max(ff_done, act_rd, ps45_sv)),
                   (srw, ps45_sr)]
            for kt in range(NKT):
                p.E("t", lambda e, it=it, wp=wp, kt=kt, half=half, wmm=wmm: e.matmul(
                    ps[4][:, 0:512], w13[wp][:, kt * 128:(kt + 1) * 128],
                    ffb[:, kt * NT + half * 512:kt * NT + (half + 1) * 512],
                    start=(kt == 0), stop=(kt == NKT - 1)),
                    wait=wmm if kt == 0 else [])
                p.E("t", lambda e, it=it, wp=wp, kt=kt, half=half: e.matmul(
                    ps[5][:, 0:512],
                    w13[wp][:, NKT * 128 + kt * 128:NKT * 128 + (kt + 1) * 128],
                    ffb[:, kt * NT + half * 512:kt * NT + (half + 1) * 512],
                    start=(kt == 0), stop=(kt == NKT - 1)),
                    inc=smm if kt == NKT - 1 else None)
            g13_t = smm.n
            p.E("s", lambda e, g13_t=g13_t: e.activation(
                tmpv[:, 0:512], ps[4][:, 0:512], AF.Silu),
                wait=[(smm, g13_t)], inc=srw)
            sil2_t = srw.n
            p.E("v", lambda e, half=half, sil2_t=sil2_t: e.tensor_tensor(
                actb[:, half * 512:(half + 1) * 512], tmpv[:, 0:512],
                ps[5][:, 0:512], ALU.mult),
                wait=[(srw, sil2_t)], inc=svx)
            ps45_sv = svx.n
            ps45_sr = srw.n
        act_t = svx.n
        last_w13[wp] = smm.n
        for mt in range(NKT):
            for half in range(2):
                bank = (mt * 2 + half) % 4
                wmm = [(sgl, w13_t), (svx, act_t), (svx, w2ev.get(bank, 0))]
                p.E("t", lambda e, it=it, wp=wp, mt=mt, half=half, bank=bank,
                     wmm=wmm: e.matmul(
                    ps[bank][:, 0:512], w2t[wp][:, mt * 128:(mt + 1) * 128],
                    actb[:, half * 512:(half + 1) * 512],
                    start=True, stop=True), wait=wmm, inc=smm)
                m2_t = smm.n
                p.E("v", lambda e, mt=mt, half=half, bank=bank, m2_t=m2_t: e.tensor_tensor(
                    xob[:, mt * NT + half * 512:mt * NT + (half + 1) * 512],
                    xob[:, mt * NT + half * 512:mt * NT + (half + 1) * 512],
                    ps[bank][:, 0:512], ALU.add),
                    wait=[(smm, m2_t)], inc=svx)
                w2ev[bank] = svx.n
        act_rd = svx.n
        last_w2[wp] = smm.n
    sw_done = svx.n

    for kt in range(NKT):
        xb = xtr[kt % 2]
        sxp = sxa[kt % 2]
        w = [(svx, sw_done)] if kt < 2 else [(sst, sst.n)]
        p.E("y", lambda e, kt=kt, xb=xb, w=w: e.dma_start(
            out=xb[:], in_=x_in[kt * 128:(kt + 1) * 128, xc0:xc0 + NT]),
            wait=w, inc=sxp, inck=16)
        ld_t = sxp.n
        p.E("v", lambda e, kt=kt, xb=xb, ld_t=ld_t, sxp=sxp: e.tensor_tensor(
            xob[:, kt * NT:(kt + 1) * NT], xob[:, kt * NT:(kt + 1) * NT],
            xb[:, :], ALU.add), wait=[(sxp, ld_t)], inc=svx)
        fin_t = svx.n
        p.E("y", lambda e, kt=kt, fin_t=fin_t: e.dma_start(
            out=xo_out[kt * 128:(kt + 1) * 128, :],
            in_=xob[:, kt * NT:(kt + 1) * NT]),
            wait=[(svx, fin_t)], inc=sst, inck=16)
    fin = sst.n
    p.q["y"].append(lambda e: e.wait_ge(sst.h, fin))
    return p


def attn_post_weights(inputs):
    gnw = np.asarray(inputs["L0_attn_gnorm_w"], np.float32)
    def coltile(v, nt):
        return np.ascontiguousarray(
            v.reshape(nt, 128, -1).transpose(1, 0, 2).reshape(128, -1))
    return {
        "gnwB": np.ascontiguousarray(
            np.broadcast_to(np.tile(gnw, 32)[None, :], (128, 3072)).copy()),
        "Wo": np.ascontiguousarray(np.asarray(inputs["L0_attn_Wo"], np.float32)),
        "rms2P": coltile(np.asarray(inputs["L0_attn_rms2_w"], np.float32), NKT),
        "W1": np.ascontiguousarray(np.asarray(inputs["L0_attn_W1"], np.float32)),
        "W3": np.ascontiguousarray(np.asarray(inputs["L0_attn_W3"], np.float32)),
        "W2": np.ascontiguousarray(np.asarray(inputs["L0_attn_W2"], np.float32)),
    }



def _fm_slices(x, halo=HALO):
    out = []
    xt = np.ascontiguousarray(np.asarray(x, np.float32))
    for c in range(NCORES):
        b = c // 4
        t0 = (c % 4) * TPC
        lo = t0 - halo
        sl = xt[b, max(lo, 0):t0 + TPC, :].T
        if lo < 0:
            sl = np.concatenate([np.zeros((D, -lo), np.float32), sl], 1)
        out.append(np.ascontiguousarray(sl))
    return out


def _coltiles(*vs):
    outs = []
    for v in vs:
        outs.append(np.ascontiguousarray(np.asarray(v, np.float32).reshape(NKT, 128).T))
    return np.ascontiguousarray(np.concatenate(outs, 1))


_cm = np.zeros((128, 8), np.float32)
_cm[:, 0] = LN_EPS
_cm[:, 1] = 1.0
_cm[:, 2] = 1e-6
_cm[:, 3] = 1e-38
_CONSTS = {
    "cONESC": np.ones((128, 1), np.float32),
    "cONESR": np.ones((1, 128), np.float32),
    "cMISC": _cm,
    "cEYE16": np.eye(16, dtype=np.float32),
    "cIDF": np.eye(128, dtype=np.float32),
    "cJ": np.tile(np.arange(TPC, dtype=np.float32)[None, :], (128, 1)),
    "cONESW": np.ones((128, TPC), np.float32),
}


def run_sign_ffn(x_fm_slices, wT, gamma, beta, bias, alpha, width):
    p = build_sign_ffn(width)
    nc = p.finish()
    pars = _coltiles(gamma, beta, bias, alpha)
    in_maps = []
    for c in range(NCORES):
        m = {"x": x_fm_slices[c], "wT": wT, "pars_in": pars}
        m.update(_CONSTS)
        in_maps.append(m)
    res = run_bass_kernel_spmd(nc, in_maps, core_ids=list(range(NCORES)))
    return [r["xo"] for r in res.results]


_PROG_CACHE = {}


def _prog(name, builder):
    if name not in _PROG_CACHE:
        p = builder()
        _PROG_CACHE[name] = p.finish()
    return _PROG_CACHE[name]


def build_mega(dbg=False):
    p = P("mega")
    p.init_barrier()
    mk = p.outp if dbg else p.dram
    x1_d = mk("x1D", (D, W))
    qf_d = mk("qfD", (2048, TPC), BF16)
    kf_d = mk("kfD", (2048, TPC), BF16)
    vf_d = mk("vfD", (3072, TPC), BF16)
    gate_d = mk("gateD", (3072, TPC), BF16)
    g_d = mk("gD", (H, TPC))
    b_d = mk("bD", (H, TPC))
    oo_d = mk("ooD", (TPC, 3072), BF16)
    x2_d = mk("x2D", (D, TPC))
    x3_d = mk("x3D", (D, TPC))
    x4_d = mk("x4D", (D, TPC))
    x5_d = mk("x5D", (D, TPC))
    out_ap = p.outp("out", (D, TPC))

    build_sign_ffn(W, p=p, io={"xo": x1_d}, prefix="s1_")
    p.barrier()
    build_attn_pre(p=p, io={"x": x1_d, "qf": qf_d, "kf": kf_d, "vf": vf_d,
                            "gate": gate_d, "gout": g_d, "bout": b_d},
                   prefix="ap_")
    p.barrier()
    build_delta(H, p=p, io={"qf": qf_d, "kf": kf_d, "vf": vf_d, "gout": g_d,
                            "bout": b_d, "oout": oo_d}, prefix="dl_")
    p.barrier()
    build_attn_post(p=p, io={"x": x1_d, "oD": oo_d, "gateF": gate_d,
                             "xo": x2_d}, prefix="po_", xc0=HALO)
    p.barrier()
    build_sign_ffn(TPC, p=p, io={"x": x2_d, "xo": x3_d}, prefix="s2_")
    p.barrier()
    build_moe(p=p, io={"x": x3_d, "xo": x4_d}, prefix="m1_")
    p.barrier()
    build_lru(p=p, io={"x": x4_d, "xo": x5_d}, prefix="lr_")
    p.barrier()
    build_moe(p=p, io={"x": x5_d, "xo": out_ap}, prefix="m2_")
    return p


def _mega_inputs(inputs):
    x0 = np.asarray(inputs["L0_attn_x"], np.float32)
    slices = _fm_slices(x0, halo=HALO)
    wT = np.ascontiguousarray(np.asarray(inputs["L0_ffn_weight"], np.float32).T)
    pars = _coltiles(inputs["L0_ffn_gamma"], inputs["L0_ffn_beta"],
                     inputs["L0_ffn_bias"], inputs["L0_ffn_alpha"])
    shared = {"wT": wT, "pars_in": pars}
    shared.update(attn_pre_weights(inputs))
    shared.update(_delta_consts(H))
    shared.update(attn_post_weights(inputs))
    shared.update(moe_weights(inputs))
    shared.update(lru_weights(inputs))
    shared.update(_CONSTS)
    maps = []
    for c in range(NCORES):
        hm = np.ones((128, 4), np.float32)
        if c % 4 == 0:
            hm[:, 0:3] = 0.0
        m = dict(shared)
        m["x"] = slices[c]
        m["hmask"] = hm
        m["mSELD"] = _delta_msel(c)
        m["mSEL"] = _lru_msel(c)
        maps.append(m)
    return maps


def kernel_fused(**inputs):
    nc = _prog("mega", build_mega)
    maps = _mega_inputs(inputs)
    res = run_bass_kernel_spmd(nc, maps, core_ids=list(range(NCORES)))
    out = np.empty((B, T, D), np.float32)
    for c in range(NCORES):
        b, t0 = c // 4, (c % 4) * TPC
        out[b, t0:t0 + TPC, :] = np.asarray(res.results[c]["out"]).T
    return out


def _launch(nc, in_maps, outs):
    res = run_bass_kernel_spmd(nc, in_maps, core_ids=list(range(NCORES)))
    return [{o: np.asarray(r[o]) for o in outs} for r in res.results]


def kernel(**inputs):
    return kernel_fused(**inputs)


def kernel_unfused(**inputs):
    x0 = np.asarray(inputs["L0_attn_x"], np.float32)

    nc1 = _prog("sign_w", lambda: build_sign_ffn(W))
    wT = np.ascontiguousarray(np.asarray(inputs["L0_ffn_weight"], np.float32).T)
    pars = _coltiles(inputs["L0_ffn_gamma"], inputs["L0_ffn_beta"],
                     inputs["L0_ffn_bias"], inputs["L0_ffn_alpha"])
    slices = _fm_slices(x0, halo=HALO)
    maps = []
    for c in range(NCORES):
        m = {"x": slices[c], "wT": wT, "pars_in": pars}
        m.update(_CONSTS)
        maps.append(m)
    x1s = [r["xo"] for r in _launch(nc1, maps, ["xo"])]

    nc2 = _prog("attnpre", build_attn_pre)
    wpre = attn_pre_weights(inputs)
    maps = []
    for c in range(NCORES):
        hm = np.ones((128, 4), np.float32)
        if c % 4 == 0:
            hm[:, 0:3] = 0.0
        m = {"x": x1s[c], "hmask": hm}
        m.update(wpre)
        m.update(_CONSTS)
        maps.append(m)
    pre = _launch(nc2, maps, ["qf", "kf", "vf", "gate", "gout", "bout"])

    nc3 = _prog("delta", build_delta)
    dc = _delta_consts(H)
    maps = []
    for c in range(NCORES):
        m = {"qf": pre[c]["qf"], "kf": pre[c]["kf"], "vf": pre[c]["vf"],
             "gout": pre[c]["gout"], "bout": pre[c]["bout"],
             "mSELD": _delta_msel(c)}
        m.update(dc)
        for k in ("cIDF", "cONESR", "cMISC"):
            m[k] = _CONSTS[k]
        maps.append(m)
    oo = [r["oout"] for r in _launch(nc3, maps, ["oout"])]

    nc4 = _prog("attnpost", build_attn_post)
    wpost = attn_post_weights(inputs)
    maps = []
    for c in range(NCORES):
        m = {"x": np.ascontiguousarray(x1s[c][:, HALO:]), "oD": oo[c],
             "gateF": pre[c]["gate"]}
        m.update(wpost)
        m.update(_CONSTS)
        maps.append(m)
    x2s = [r["xo"] for r in _launch(nc4, maps, ["xo"])]

    nc5 = _prog("sign_n", lambda: build_sign_ffn(TPC))
    maps = []
    for c in range(NCORES):
        m = {"x": x2s[c], "wT": wT, "pars_in": pars}
        m.update(_CONSTS)
        maps.append(m)
    x3s = [r["xo"] for r in _launch(nc5, maps, ["xo"])]

    nc6 = _prog("moe", build_moe)
    wmoe = moe_weights(inputs)
    maps = []
    for c in range(NCORES):
        m = {"x": x3s[c]}
        m.update(wmoe)
        m.update(_CONSTS)
        maps.append(m)
    x4s = [r["xo"] for r in _launch(nc6, maps, ["xo"])]

    nc7 = _prog("lru", build_lru)
    wlru = lru_weights(inputs)
    maps = []
    for c in range(NCORES):
        m = {"x": x4s[c], "mSEL": _lru_msel(c)}
        m.update(wlru)
        m.update(_CONSTS)
        maps.append(m)
    x5s = [r["xo"] for r in _launch(nc7, maps, ["xo"])]

    maps = []
    for c in range(NCORES):
        m = {"x": x5s[c]}
        m.update(wmoe)
        m.update(_CONSTS)
        maps.append(m)
    x6s = [r["xo"] for r in _launch(nc6, maps, ["xo"])]

    out = np.empty((B, T, D), np.float32)
    for c in range(NCORES):
        b, t0 = c // 4, (c % 4) * TPC
        out[b, t0:t0 + TPC, :] = x6s[c].T
    return out


# version 05
# speedup vs baseline: 1.0409x; 1.0409x over previous
import numpy as np
import ml_dtypes
from contextlib import ExitStack

import concourse.bass as bass
import concourse.mybir as mybir
from concourse.bass_utils import run_bass_kernel_spmd

F32 = mybir.dt.float32
BF16 = mybir.dt.bfloat16
AF = mybir.ActivationFunctionType
ALU = mybir.AluOpType
BF = ml_dtypes.bfloat16

NCORES = 8
B = 2
T = 4096
D = 2048
TPC = 1024
HALO = 3
W = TPC + HALO
NKT = D // 128

LN_EPS = 1e-5
RMS_EPS = 1e-5
L2_EPS = 1e-6
H = 32
DBG = False



class SemW:
    def __init__(self, h):
        self.h = h
        self.n = 0

    def inc(self, k=1):
        self.n += k
        return self.n


class P:

    def __init__(self, name):
        self.name = name
        self.nc = bass.Bass("TRN2", target_bir_lowering=False, debug=False,
                            detect_race_conditions=False)
        self.q = {k: [] for k in ("t", "v", "s", "g", "y")}
        self.ctx = ExitStack()
        self.input_names = []
        self.output_names = []
        self._auto = {}
        self._last_sync = {}
        self._prefix = ""
        self._scope = None
        self._sems = {}
        self._psums = {}
        self._inps = {}
        self._dma_sems = set()
        self._bar = None
        self._barwait = []
        self._pending = {}

    def sem(self, name):
        if name not in self._sems:
            self._sems[name] = SemW(self.ctx.enter_context(self.nc.semaphore(name)))
        return self._sems[name]

    def sbuf(self, name, shape, dt=F32):
        ctx = self._scope if self._scope is not None else self.ctx
        return ctx.enter_context(
            self.nc.sbuf_tensor(self._prefix + name, list(shape), dt))

    def psum(self, name, shape=(128, 512), dt=F32):
        if name not in self._psums:
            self._psums[name] = self.ctx.enter_context(
                self.nc.psum_tensor(name, list(shape), dt))
        return self._psums[name]

    def inp(self, name, shape, dt=F32):
        if name not in self._inps:
            h = self.nc.dram_tensor(name, list(shape), dt, kind="ExternalInput")
            self.input_names.append(name)
            self._inps[name] = h.ap()
        return self._inps[name]

    def outp(self, name, shape, dt=F32):
        h = self.nc.dram_tensor(name, list(shape), dt, kind="ExternalOutput")
        self.output_names.append(name)
        return h.ap()

    def dram(self, name, shape, dt=F32):
        return self.nc.dram_tensor(self._prefix + name, list(shape), dt,
                                   kind="Internal").ap()

    def io_inp(self, io, name, shape, dt=F32):
        if io is not None and name in io:
            return io[name]
        return self.inp(name, shape, dt)

    def io_out(self, io, name, shape, dt=F32):
        if io is not None and name in io:
            return io[name]
        return self.outp(name, shape, dt)

    def init_barrier(self):
        self._bar = self.sem("sbar")
        self._barc = self.ctx.enter_context(
            self.nc.sbuf_tensor("barc", [1, 2], F32))
        self.psum("ps0")
        self.E("v", lambda e: e.memzero(self._barc[:, :]))

    def stage(self, prefix):
        self._prefix = prefix
        self._scope = ExitStack()

    def barrier(self):
        waits = []
        for qn in ("v", "s"):
            ls = self._last_sync.get(qn)
            if ls is not None and ls[1] > 0:
                waits.append((ls[0], ls[1]))
        waits += [(s, s.n) for s in self._dma_sems if s.n > 0]
        assert self._bar is not None, "call init_barrier() before stages"
        bar = self._bar
        ps0 = self.psum("ps0")
        self.E("t", lambda e: e.matmul(
            ps0[0:1, 0:1], self._barc[0:1, 0:1], self._barc[0:1, 1:2],
            start=True, stop=True), wait=waits, inc=bar)
        self._barwait = [(bar, bar.n)]
        self._pending = {q: True for q in self.q}
        self._dma_sems = set()
        if self._scope is not None:
            self._scope.close()
            self._scope = None
        self._prefix = ""

    def E(self, qn, mk, wait=(), inc=None, inck=1):
        wait = tuple(wait)
        if self._pending.get(qn):
            wait = tuple(self._barwait) + wait
            self._pending[qn] = False
        if inc is not None:
            inc.inc(inck)
            if qn in ("y", "g"):
                self._dma_sems.add(inc)
        chain = None
        eff_inc, eff_inck = inc, inck
        if qn in ("v", "s"):
            chain = self._last_sync.get(qn)
            if inc is None:
                if qn not in self._auto:
                    self._auto[qn] = self.sem(f"auto_{qn}")
                eff_inc, eff_inck = self._auto[qn], 1
                eff_inc.inc(1)
            self._last_sync[qn] = (eff_inc, eff_inc.n)

        def f(eng, mk=mk, wait=wait, eff_inc=eff_inc, eff_inck=eff_inck, chain=chain):
            if chain is not None and chain[1] > 0:
                eng.wait_ge(chain[0].h, chain[1])
            for (s, v) in wait:
                if v > 0:
                    eng.wait_ge(s.h, v)
            ins = mk(eng)
            if eff_inc is not None:
                ins.then_inc(eff_inc.h, eff_inck)
        self.q[qn].append(f)

    def finish(self):
        if self._scope is not None:
            self._scope.close()
            self._scope = None
        blk = self.ctx.enter_context(self.nc.Block())
        qs = self.q
        if qs["t"]:
            blk.tensor(lambda e: [f(e) for f in qs["t"]])
        if qs["v"]:
            blk.vector(lambda e: [f(e) for f in qs["v"]])
        if qs["s"]:
            blk.scalar(lambda e: [f(e) for f in qs["s"]])
        if qs["g"]:
            blk.gpsimd(lambda e: [f(e) for f in qs["g"]])
        if qs["y"]:
            blk.sync(lambda e: [f(e) for f in qs["y"]])
        self.ctx.close()
        return self.nc


def col_groups(width, step=512):
    return [(c, min(width, c + step)) for c in range(0, width, step)]



def emit_ln_rows2(p, xs, width, ones_c, ones_r, rows, RSTD, R2, TMP, x2bufs, ps,
                  sq, smm, srow, eps_ap, extra_wait=()):
    cgs = col_groups(width)
    assert 2 * len(cgs) <= len(ps)
    mm_done_by_kt = {}
    for kt in range(NKT):
        x2 = x2bufs[kt % 2]
        w = list(extra_wait) if kt == 0 else []
        if kt >= 2:
            w.append((smm, mm_done_by_kt[kt - 2]))
        p.E("s", lambda e, kt=kt, x2=x2: e.activation(x2[:], xs[kt][:, 0:width], AF.Square),
            wait=w, inc=sq)
        sq_t = sq.n
        for cg, (c0, c1) in enumerate(cgs):
            p.E("t", lambda e, kt=kt, c0=c0, c1=c1, cg=cg: e.matmul(
                ps[2 * cg][0:1, 0:c1 - c0], ones_c[:, 0:1], xs[kt][:, c0:c1],
                start=(kt == 0), stop=(kt == NKT - 1)),
                wait=list(extra_wait) if (kt == 0 and cg == 0) else [])
            p.E("t", lambda e, kt=kt, c0=c0, c1=c1, cg=cg, x2=x2, sq_t=sq_t: e.matmul(
                ps[2 * cg + 1][0:1, 0:c1 - c0], ones_c[:, 0:1], x2[:, c0:c1],
                start=(kt == 0), stop=(kt == NKT - 1)),
                wait=[(sq, sq_t)],
                inc=smm)
        mm_done_by_kt[kt] = smm.n
    stats_done = smm.n
    MU = R2
    for cg, (c0, c1) in enumerate(cgs):
        n = c1 - c0
        p.E("v", lambda e, cg=cg, c0=c0, n=n: e.tensor_scalar(
            rows[0:1, MU + c0:MU + c0 + n], ps[2 * cg][0:1, 0:n], 1.0 / D, None, ALU.mult),
            wait=[(smm, stats_done)] if cg == 0 else [])
        p.E("v", lambda e, cg=cg, c0=c0, n=n: e.tensor_scalar(
            rows[0:1, RSTD + c0:RSTD + c0 + n], ps[2 * cg + 1][0:1, 0:n], 1.0 / D, None, ALU.mult))
    p.E("v", lambda e: e.tensor_tensor(
        rows[0:1, TMP:TMP + width], rows[0:1, MU:MU + width],
        rows[0:1, MU:MU + width], ALU.mult))
    p.E("v", lambda e: e.tensor_tensor(
        rows[0:1, RSTD:RSTD + width], rows[0:1, RSTD:RSTD + width],
        rows[0:1, TMP:TMP + width], ALU.subtract), inc=smm)
    var_t = smm.n
    p.E("s", lambda e, var_t=var_t: e.activation(
        rows[0:1, RSTD:RSTD + width], rows[0:1, RSTD:RSTD + width], AF.Ln,
        bias=eps_ap), wait=[(smm, var_t)], inc=sq)
    p.E("s", lambda e: e.activation(
        rows[0:1, RSTD:RSTD + width], rows[0:1, RSTD:RSTD + width], AF.Exp,
        scale=-0.5), inc=sq)
    p.E("v", lambda e, v=sq.n: e.engine_nop(), wait=[(sq, sq.n)])
    p.E("v", lambda e: e.scalar_tensor_tensor(
        rows[0:1, R2:R2 + width], rows[0:1, MU:MU + width], -1.0,
        rows[0:1, RSTD:RSTD + width], ALU.mult, ALU.mult), inc=srow)
    return srow.n



def build_sign_ffn(width, p=None, io=None, prefix=""):
    if p is None:
        p = P("signffn")
    p.stage(prefix)

    x_in = p.io_inp(io, "x", (D, width))
    wT_in = p.inp("wT", (D, D))
    par_in = p.inp("pars_in", (128, 4 * NKT))
    ones_in = p.inp("cONESC", (128, 1))
    onesr_in = p.inp("cONESR", (1, 128))
    cmisc_in = p.inp("cMISC", (128, 8))
    xo = p.io_out(io, "xo", (D, width))

    xs = [p.sbuf(f"x{i}", (128, width)) for i in range(NKT)]
    wtile = [p.sbuf(f"w{i}", (128, D)) for i in range(2)]
    swT = [p.sbuf(f"sw{i}", (128, D), BF16) for i in range(NKT)]
    a_b = [p.sbuf(f"a{i}", (128, width), BF16) for i in range(NKT)]
    x2 = [p.sbuf(f"x2_{i}", (128, width)) for i in range(2)]
    tn = [p.sbuf(f"tn{i}", (128, 512)) for i in range(2)]
    xot = [p.sbuf(f"xot{i}", (128, 512)) for i in range(2)]
    rows = p.sbuf("rows", (1, 3300))
    pars = p.sbuf("pars", (128, 4 * NKT))
    ones_c = p.sbuf("onesc", (128, 1))
    ones_r = p.sbuf("onesr", (1, 128))
    cmisc = p.sbuf("cmisc", (128, 8))

    ps = [p.psum(f"ps{i}") for i in range(8)]

    sx = p.sem("sx")
    swl2 = [p.sem("swl0"), p.sem("swl1")]
    scons = p.sem("scons")
    ssq = p.sem("ssq")
    smm = p.sem("smm")
    srow = p.sem("srow")
    sa = p.sem("sa")
    sev = p.sem("sev")
    sst2 = [p.sem("sst0"), p.sem("sst1")]

    for i in range(NKT):
        p.E("y", lambda e, i=i: e.dma_start(out=xs[i][:], in_=x_in[i * 128:(i + 1) * 128, :]),
            inc=sx, inck=16)
    p.E("y", lambda e: e.dma_start(out=ones_c[:], in_=ones_in[:, :]), inc=sx, inck=16)
    p.E("y", lambda e: e.dma_start(out=ones_r[:], in_=onesr_in[:, :]), inc=sx, inck=16)
    p.E("y", lambda e: e.dma_start(out=pars[:], in_=par_in[:, :]), inc=sx, inck=16)
    p.E("y", lambda e: e.dma_start(out=cmisc[:], in_=cmisc_in[:, :]), inc=sx, inck=16)
    all_ld = sx.n

    GCOL = 0
    BCOL = NKT
    BACOL = 2 * NKT
    AACOL = 3 * NKT

    scons_b = scons.n
    for i in range(NKT):
        buf = wtile[i % 2]
        sw = swl2[i % 2]
        w = [(scons, scons_b + 4 * (i - 1))] if i >= 2 else []
        p.E("y", lambda e, buf=buf, i=i: e.dma_start(
            out=buf[:], in_=wT_in[i * 128:(i + 1) * 128, :]),
            wait=w, inc=sw, inck=16)
        ld_t = sw.n
        for oh in range(4):
            p.E("t", lambda e, buf=buf, i=i, oh=oh: e.matmul(
                ps[oh][0:1, 0:512], ones_c[:, 0:1], buf[:, oh * 512:(oh + 1) * 512],
                start=(i == 0), stop=(i == NKT - 1)),
                wait=[(sw, ld_t), (sx, all_ld)] if oh == 0 else [],
                inc=scons)
    mean_done = scons.n

    for oh in range(4):
        p.E("v", lambda e, oh=oh: e.tensor_scalar(
            rows[0:1, oh * 512:(oh + 1) * 512], ps[oh][0:1, 0:512],
            -1.0 / D, None, ALU.mult),
            wait=[(scons, mean_done)] if oh == 0 else [],
            inc=srow if oh == 3 else None)
    nmean_done = srow.n

    for oh in range(4):
        p.E("t", lambda e, oh=oh: e.matmul(
            ps[4 + oh][:, 0:512], ones_r[0:1, :], rows[0:1, oh * 512:(oh + 1) * 512],
            start=True, stop=True),
            wait=[(srow, nmean_done)] if oh == 0 else [],
            inc=smm if oh == 3 else None)
    bc4_t = smm.n
    tt_done = {}
    tile_tt = {}
    for i in range(NKT):
        buf = wtile[i % 2]
        sw = swl2[i % 2]
        w = [(sa, tile_tt[i - 2])] if i >= 2 else [(scons, mean_done)]
        p.E("y", lambda e, buf=buf, i=i: e.dma_start(
            out=buf[:], in_=wT_in[i * 128:(i + 1) * 128, :]),
            wait=w, inc=sw, inck=16)
        ld_t = sw.n
        for oh in range(4):
            tb = tn[(i * 4 + oh) % 2]
            w2 = [(smm, bc4_t), (sw, ld_t)]
            k = (i * 4 + oh)
            if k >= 2:
                w2.append((ssq, tt_done[k - 2]))
            p.E("v", lambda e, buf=buf, oh=oh, tb=tb: e.tensor_tensor(
                tb[:, 0:512], buf[:, oh * 512:(oh + 1) * 512], ps[4 + oh][:, 0:512],
                ALU.add), wait=w2, inc=sa)
            tt_t = sa.n
            p.E("s", lambda e, i=i, oh=oh, tb=tb, tt_t=tt_t: e.activation(
                swT[i][:, oh * 512:(oh + 1) * 512], tb[:, 0:512], AF.Sign),
                wait=[(sa, tt_t)],
                inc=ssq)
            tt_done[k] = ssq.n
        tile_tt[i] = sa.n
    sign_done = ssq.n
    prep_done = sign_done

    RSTD, R2, TMP = 0, 1100, 2200
    ln_t = emit_ln_rows2(p, xs, width, ones_c, ones_r, rows, RSTD, R2, TMP, x2, ps,
                         ssq, smm, srow, cmisc[0:1, 0:1],
                         extra_wait=[(sx, all_ld), (ssq, prep_done)])

    cgs = col_groups(width)
    a_done = {}
    prev_cg_tt = 0
    for cg, (c0, c1) in enumerate(cgs):
        n = c1 - c0
        pC = ps[4] if cg % 2 == 0 else ps[2]
        pD = ps[5] if cg % 2 == 0 else ps[3]
        p.E("t", lambda e, c0=c0, n=n, pC=pC: e.matmul(
            pC[:, 0:n], ones_r[0:1, :], rows[0:1, RSTD + c0:RSTD + c0 + n],
            start=True, stop=True),
            wait=[(srow, ln_t), (ssq, sign_done), (sa, prev_cg_tt)])
        p.E("t", lambda e, c0=c0, n=n, pD=pD: e.matmul(
            pD[:, 0:n], ones_r[0:1, :], rows[0:1, R2 + c0:R2 + c0 + n],
            start=True, stop=True), inc=smm)
        bc_t = smm.n
        for kt in range(NKT):
            tb = tn[kt % 2]
            w = [(smm, bc_t)]
            if kt >= 2:
                w.append((ssq, a_done[(cg, kt - 2)]))
            elif cg > 0:
                w.append((ssq, a_done[(cg - 1, NKT - 1)]))
            p.E("v", lambda e, kt=kt, c0=c0, c1=c1, n=n, tb=tb, pC=pC: e.tensor_tensor(
                tb[:, 0:n], xs[kt][:, c0:c1], pC[:, 0:n], ALU.mult), wait=w)
            p.E("v", lambda e, n=n, tb=tb, pD=pD: e.tensor_tensor(
                tb[:, 0:n], tb[:, 0:n], pD[:, 0:n], ALU.add))
            p.E("v", lambda e, kt=kt, n=n, tb=tb: e.tensor_scalar(
                tb[:, 0:n], tb[:, 0:n], pars[:, GCOL + kt:GCOL + kt + 1], None, ALU.mult),
                inc=sa)
            ts_t = sa.n
            p.E("s", lambda e, kt=kt, c0=c0, n=n, tb=tb, ts_t=ts_t: e.activation(
                a_b[kt][:, c0:c0 + n], tb[:, 0:n], AF.Sign,
                bias=pars[:, BCOL + kt:BCOL + kt + 1]),
                wait=[(sa, ts_t)], inc=ssq)
            a_done[(cg, kt)] = ssq.n
        prev_cg_tt = sa.n
    a_all = ssq.n

    p.E("v", lambda e: e.tensor_tensor(
        pars[:, BACOL:BACOL + NKT], pars[:, BACOL:BACOL + NKT],
        pars[:, AACOL:AACOL + NKT], ALU.mult), wait=[(sx, all_ld)])
    p.E("v", lambda e: e.tensor_scalar(
        pars[:, BACOL:BACOL + NKT], pars[:, BACOL:BACOL + NKT], 0.5, None, ALU.mult))
    p.E("v", lambda e: e.tensor_scalar(
        pars[:, AACOL:AACOL + NKT], pars[:, AACOL:AACOL + NKT], 0.5, None, ALU.mult),
        inc=srow)
    cols_done = srow.n

    ev_done = {}
    st_tgt = {}
    for cg, (c0, c1) in enumerate(cgs):
        n = c1 - c0
        for mt in range(NKT):
            pb = ps[6 + mt % 2]
            gi = cg * NKT + mt
            for kt in range(NKT):
                w = []
                if kt == 0:
                    w = [(ssq, a_all)]
                    if gi >= 2:
                        w.append((sev, ev_done[gi - 2]))
                p.E("t", lambda e, mt=mt, kt=kt, n=n, pb=pb, c0=c0, c1=c1: e.matmul(
                    pb[:, 0:n], swT[kt][:, mt * 128:(mt + 1) * 128], a_b[kt][:, c0:c1],
                    start=(kt == 0), stop=(kt == NKT - 1)),
                    wait=w, inc=smm if kt == NKT - 1 else None)
            mm_t = smm.n
            xob = xot[gi % 2]
            w = [(smm, mm_t), (srow, cols_done)]
            if gi >= 2:
                w.append((sst2[gi % 2], st_tgt[gi - 2]))
            p.E("v", lambda e, mt=mt, c0=c0, c1=c1, n=n, pb=pb, xob=xob: e.scalar_tensor_tensor(
                xob[:, 0:n], pb[:, 0:n], pars[:, AACOL + mt:AACOL + mt + 1],
                xs[mt][:, c0:c1], ALU.mult, ALU.add),
                wait=w, inc=sa)
            ev1 = sa.n
            p.E("s", lambda e, mt=mt, n=n, xob=xob, ev1=ev1: e.activation(
                xob[:, 0:n], xob[:, 0:n], AF.Identity, bias=pars[:, BACOL + mt:BACOL + mt + 1]),
                wait=[(sa, ev1)], inc=sev)
            ev_done[gi] = sev.n
            p.E("y", lambda e, mt=mt, c0=c0, n=n, xob=xob, v=sev.n: e.dma_start(
                out=xo[mt * 128:(mt + 1) * 128, c0:c0 + n], in_=xob[:, 0:n]),
                wait=[(sev, sev.n)], inc=sst2[gi % 2], inck=16)
            st_tgt[gi] = sst2[gi % 2].n
    if not DBG:
        st0, st1 = sst2[0].n, sst2[1].n
        p.q["y"].append(lambda e: e.wait_ge(sst2[0].h, st0))
        p.q["y"].append(lambda e: e.wait_ge(sst2[1].h, st1))
        return p
    dbg_sw = p.outp("dbg_sw", (128, D), BF16)
    dbg_a = p.outp("dbg_a", (128, width), BF16)
    dbg_af = p.outp("dbg_af", (D, width), BF16)
    dbg_tn = p.outp("dbg_tn", (128, 1024))
    p.E("y", lambda e: e.dma_start(out=dbg_tn[:, 0:512], in_=tn[0][:]),
        wait=[(ssq, a_all)], inc=sst2[0], inck=16)
    p.E("y", lambda e: e.dma_start(out=dbg_tn[:, 512:1024], in_=tn[1][:]),
        wait=[(ssq, a_all)], inc=sst2[0], inck=16)
    for kt in range(NKT):
        p.E("y", lambda e, kt=kt: e.dma_start(out=dbg_af[kt * 128:(kt + 1) * 128, :], in_=a_b[kt][:]),
            wait=[(ssq, a_all)], inc=sst2[0], inck=16)
    dbg_rows = p.outp("dbg_rows", (1, 3300))
    p.E("y", lambda e: e.dma_start(out=dbg_sw[:, :], in_=swT[0][:]),
        wait=[(ssq, sign_done)], inc=sst2[0], inck=16)
    p.E("y", lambda e: e.dma_start(out=dbg_a[:, :], in_=a_b[0][:]),
        wait=[(ssq, a_all)], inc=sst2[0], inck=16)
    p.E("y", lambda e: e.dma_start(out=dbg_rows[0:1, 0:width], in_=rows[0:1, 0:width]),
        wait=[(srow, ln_t)], inc=sst2[0], inck=16)
    p.E("y", lambda e: e.dma_start(out=dbg_rows[0:1, 1100:1100 + width],
                                   in_=rows[0:1, 1100:1100 + width]),
        wait=[(srow, ln_t)], inc=sst2[0], inck=16)
    st0, st1 = sst2[0].n, sst2[1].n
    p.q["y"].append(lambda e: e.wait_ge(sst2[0].h, st0))
    p.q["y"].append(lambda e: e.wait_ge(sst2[1].h, st1))
    return p




def build_moe(trunc=99, use_w=True, p=None, io=None, prefix=""):
    width = TPC
    FF = 1408
    NFF = FF // 128
    if p is None:
        p = P("moe")
    p.stage(prefix)

    x_in = p.io_inp(io, "x", (D, width))
    gateT_in = p.inp("gateT", (128, 16 * NKT))
    guT_in = p.inp("guT", (16, D, 2 * FF))
    dnT_in = p.inp("dnT", (16, FF, D))
    sguT_in = p.inp("sguT", (D, 4 * FF))
    sdT_in = p.inp("sdT", (2 * FF, D))
    ones_in = p.inp("cONESC", (128, 1))
    onesr_in = p.inp("cONESR", (1, 128))
    cmisc_in = p.inp("cMISC", (128, 8))
    eye16_in = p.inp("cEYE16", (16, 16))
    idf_in = p.inp("cIDF", (128, 128))
    xo = p.io_out(io, "xo", (D, width))

    xn = [p.sbuf(f"xn{i}", (128, width), BF16) for i in range(NKT)]
    ysb = [p.sbuf(f"y{i}", (128, width)) for i in range(NKT)]
    act = [p.sbuf(f"act{i}", (128, width), BF16) for i in range(NFF)]
    guw = p.sbuf("guw", (128, 16 * 256), BF16)
    guw2 = p.sbuf("guw2", (128, 16 * 256), BF16)
    dnw = p.sbuf("dnw", (128, 6 * D), BF16)
    xtr = [p.sbuf(f"xtr{i}", (128, width)) for i in range(2)]
    x2b = [p.sbuf("x2b0", (128, width))] * 2
    gateT = p.sbuf("gateT_sb", (128, 16 * NKT))
    rows = p.sbuf("rows", (1, 3250))
    gsum_sb = p.sbuf("gsum", (1, 16))
    lg = p.sbuf("lg", (16, width))
    lgt = p.sbuf("lgt", (128, 16))
    scs = [p.sbuf(f"sc{i}", (128, 16)) for i in range(8)]
    scw = p.sbuf("scw", (128, 16))
    msk = p.sbuf("msk", (128, 16))
    mcol = p.sbuf("mcol", (128, 8))
    wrows = p.sbuf("wrows", (16, width))
    wrow1 = p.sbuf("wrow1", (1, 512))
    wbc = [p.sbuf(f"wbc{i}", (128, 512)) for i in range(2)]
    sg2 = [p.sbuf(f"sgb{i}", (128, 512), BF16) for i in range(2)]
    tu2 = [p.sbuf(f"tub{i}", (128, 512), BF16) for i in range(2)]
    rtb = p.sbuf("rtb", (16, 512))
    tmpv = p.sbuf("tmpv", (128, 512))
    ones_c = p.sbuf("onesc", (128, 1))
    ones_r = p.sbuf("onesr", (1, 128))
    cmisc = p.sbuf("cmisc", (128, 8))
    eye16 = p.sbuf("eye16", (16, 16))
    idf = p.sbuf("idf", (128, 128))
    xob = wbc

    ps = [p.psum(f"ps{i}") for i in range(8)]

    sx = p.sem("sx")
    sxa = [p.sem("sxa0"), p.sem("sxa1")]
    swg = [p.sem("swg0"), p.sem("swg1")]
    swd = p.sem("swd")
    smm = p.sem("smm")
    srow = p.sem("srow")
    svx = p.sem("svx")
    sev = p.sem("sev")
    sst2 = [p.sem("sst0"), p.sem("sst1")]

    cgs = col_groups(width)

    for dst, src_ in ((ones_c, ones_in), (ones_r, onesr_in), (cmisc, cmisc_in),
                      (eye16, eye16_in), (idf, idf_in), (gateT, gateT_in)):
        p.E("y", lambda e, dst=dst, src_=src_: e.dma_start(out=dst[:], in_=src_[:, :]),
            inc=sx, inck=16)
    const_ld = sx.n

    xa_t = {}
    for kt in range(NKT):
        xb = xtr[kt % 2]
        sxp = sxa[kt % 2]
        w = [(smm, xa_t[kt - 2])] if kt >= 2 else []
        p.E("y", lambda e, kt=kt, xb=xb: e.dma_start(
            out=xb[:], in_=x_in[kt * 128:(kt + 1) * 128, :]),
            wait=w, inc=sxp, inck=16)
        ld_t = sxp.n
        x2 = x2b[kt % 2]
        wsq = [(sxp, ld_t)]
        if kt >= 1:
            wsq.append((smm, xa_t[kt - 1]))
        p.E("s", lambda e, xb=xb, x2=x2, wsq=0, ld_t=ld_t, sxp=sxp: e.activation(
            x2[:], xb[:], AF.Square), wait=wsq, inc=srow)
        sq_t = srow.n
        for cg, (c0, c1) in enumerate(cgs):
            p.E("t", lambda e, kt=kt, c0=c0, c1=c1, cg=cg, xb=xb, ld_t=ld_t, sxp=sxp: e.matmul(
                ps[cg][0:1, 0:c1 - c0], ones_c[:, 0:1], xb[:, c0:c1],
                start=(kt == 0), stop=(kt == NKT - 1)),
                wait=[(sxp, ld_t), (sx, const_ld)] if cg == 0 else [])
            p.E("t", lambda e, kt=kt, c0=c0, c1=c1, cg=cg, x2=x2, sq_t=sq_t: e.matmul(
                ps[2 + cg][0:1, 0:c1 - c0], ones_c[:, 0:1], x2[:, c0:c1],
                start=(kt == 0), stop=(kt == NKT - 1)),
                wait=[(srow, sq_t)])
            p.E("t", lambda e, kt=kt, c0=c0, c1=c1, cg=cg, xb=xb: e.matmul(
                ps[6 + cg][0:16, 0:c1 - c0], gateT[:, kt * 16:(kt + 1) * 16], xb[:, c0:c1],
                start=(kt == 0), stop=False))
        p.E("t", lambda e, kt=kt: e.matmul(
            ps[5][0:1, 0:16], ones_c[:, 0:1], gateT[:, kt * 16:(kt + 1) * 16],
            start=(kt == 0), stop=(kt == NKT - 1)),
            inc=smm)
        xa_t[kt] = smm.n
    stats_done = smm.n

    p.E("s", lambda e: e.activation(gsum_sb[0:1, :], ps[5][0:1, 0:16], AF.Identity),
        wait=[(smm, stats_done)], inc=srow)
    gsum_t = srow.n

    for cg, (c0, c1) in enumerate(cgs):
        n = c1 - c0
        p.E("v", lambda e, cg=cg, c0=c0, n=n: e.tensor_scalar(
            rows[0:1, 1100 + c0:1100 + c0 + n], ps[cg][0:1, 0:n], -1.0 / D, None, ALU.mult),
            wait=[(smm, stats_done)] if cg == 0 else [])
        p.E("v", lambda e, cg=cg, c0=c0, n=n: e.tensor_scalar(
            rows[0:1, c0:c0 + n], ps[2 + cg][0:1, 0:n], 1.0 / D, None, ALU.mult))
    for h2 in range(2):
        p.E("v", lambda e, h2=h2: e.tensor_tensor(
            tmpv[0:1, 0:512], rows[0:1, 1100 + h2 * 512:1100 + h2 * 512 + 512],
            rows[0:1, 1100 + h2 * 512:1100 + h2 * 512 + 512], ALU.mult))
        p.E("v", lambda e, h2=h2: e.tensor_tensor(
            rows[0:1, h2 * 512:h2 * 512 + 512], rows[0:1, h2 * 512:h2 * 512 + 512],
            tmpv[0:1, 0:512], ALU.subtract), inc=svx if h2 == 1 else None)
    var_t = svx.n
    p.E("s", lambda e, var_t=var_t: e.activation(
        rows[0:1, 0:width], rows[0:1, 0:width], AF.Ln,
        bias=cmisc[0:1, 0:1]), wait=[(svx, var_t)], inc=srow)
    p.E("s", lambda e: e.activation(
        rows[0:1, 0:width], rows[0:1, 0:width], AF.Exp,
        scale=-0.5), inc=srow)
    p.E("v", lambda e, v=srow.n: e.engine_nop(), wait=[(srow, srow.n)])
    p.E("v", lambda e: e.tensor_tensor(
        rows[0:1, 2200:2200 + width], rows[0:1, 1100:1100 + width],
        rows[0:1, 0:width], ALU.mult), inc=svx)
    rows_done = svx.n

    for cg, (c0, c1) in enumerate(cgs):
        n = c1 - c0
        p.E("t", lambda e, cg=cg, c0=c0, n=n: e.matmul(
            ps[6 + cg][0:16, 0:n], gsum_sb[0:1, :], rows[0:1, 1100 + c0:1100 + c0 + n],
            start=False, stop=True),
            wait=[(svx, rows_done), (srow, gsum_t)] if cg == 0 else [])
        p.E("t", lambda e, cg=cg, c0=c0, n=n: e.matmul(
            ps[4][0:16, 0:n], ones_r[0:1, 0:16], rows[0:1, c0:c0 + n],
            start=True, stop=True),
            wait=[(sev, sev.n), (svx, svx.n)], inc=smm)
        bc_t = smm.n
        p.E("s", lambda e, c0=c0, n=n, bc_t=bc_t: e.activation(
            rtb[0:16, 0:n], ps[4][0:16, 0:n], AF.Identity),
            wait=[(smm, bc_t)], inc=sev)
        rtb_t = sev.n
        p.E("v", lambda e, cg=cg, c0=c0, n=n, rtb_t=rtb_t: e.tensor_tensor(
            lg[0:16, c0:c0 + n], ps[6 + cg][0:16, 0:n], rtb[0:16, 0:n], ALU.mult),
            wait=[(sev, rtb_t)], inc=svx)
    lg_done = svx.n

    BIG = 1.0e30
    for blk in range(8):
        p.E("t", lambda e, blk=blk: e.transpose(
            ps[6][0:128, 0:16], lg[0:16, blk * 128:(blk + 1) * 128], idf[0:16, 0:16]),
            wait=[(svx, lg_done)] if blk == 0 else [(srow, srow.n)], inc=smm)
        tr_t = smm.n
        p.E("s", lambda e, tr_t=tr_t: e.activation(
            lgt[:, :], ps[6][0:128, 0:16], AF.Identity),
            wait=[(smm, tr_t)], inc=srow)
        lgt_t = srow.n
        p.E("v", lambda e, lgt_t=lgt_t: e.reduce_max(
            mcol[:, 0:1], lgt[:, :], axis=mybir.AxisListType.X, negate=True),
            wait=[(srow, lgt_t)], inc=svx)
        p.E("s", lambda e, blk=blk, v=svx.n: e.activation(
            scs[blk][:, :], lgt[:, :], AF.Exp, bias=mcol[:, 0:1]),
            wait=[(svx, svx.n)], inc=srow)
        p.E("v", lambda e, blk=blk, v=srow.n: e.reduce_sum(
            mcol[:, 1:2], scs[blk][:, :], axis=mybir.AxisListType.X),
            wait=[(srow, srow.n)])
        p.E("v", lambda e: e.reciprocal(mcol[:, 2:3], mcol[:, 1:2]))
        p.E("v", lambda e, blk=blk: e.tensor_scalar(
            scs[blk][:, :], scs[blk][:, :], mcol[:, 2:3], None, ALU.mult))
        p.E("v", lambda e, blk=blk: e.tensor_copy(scw[:, :], scs[blk][:, :]))
        for r in range(4):
            p.E("v", lambda e: e.reduce_max(mcol[:, 3:4], scw[:, :], axis=mybir.AxisListType.X))
            if r < 3:
                p.E("v", lambda e: e.tensor_scalar(
                    msk[:, :], scw[:, :], mcol[:, 3:4], None, ALU.is_ge))
                p.E("v", lambda e: e.scalar_tensor_tensor(
                    scw[:, :], msk[:, :], -BIG, scw[:, :], ALU.mult, ALU.add))
        p.E("v", lambda e, blk=blk: e.tensor_scalar(
            msk[:, :], scs[blk][:, :], mcol[:, 3:4], None, ALU.is_ge))
        p.E("v", lambda e, blk=blk: e.tensor_tensor(
            scs[blk][:, :], scs[blk][:, :], msk[:, :], ALU.mult))
        p.E("v", lambda e, blk=blk: e.tensor_scalar(
            scs[blk][:, :], scs[blk][:, :], 0.5, None, ALU.mult), inc=svx)
        w_t = svx.n
        p.E("t", lambda e, blk=blk, w_t=w_t: e.transpose(
            ps[7][0:16, 0:128], scs[blk][:, :], idf[:, :]),
            wait=[(svx, w_t), (srow, srow.n)], inc=smm)
        tr2_t = smm.n
        p.E("s", lambda e, blk=blk, tr2_t=tr2_t: e.activation(
            wrows[0:16, blk * 128:(blk + 1) * 128], ps[7][0:16, 0:128], AF.Identity),
            wait=[(smm, tr2_t)], inc=srow)
    routing_done = srow.n

    for cg, (c0, c1) in enumerate(cgs):
        n = c1 - c0
        p.E("t", lambda e, c0=c0, n=n, cg=cg: e.matmul(
            ps[4 + 2 * cg][:, 0:n], ones_r[0:1, :], rows[0:1, c0:c0 + n],
            start=True, stop=True),
            wait=[(srow, routing_done), (svx, svx.n)] if cg == 0 else [])
        p.E("t", lambda e, c0=c0, n=n, cg=cg: e.matmul(
            ps[5 + 2 * cg][:, 0:n], ones_r[0:1, :], rows[0:1, 2200 + c0:2200 + c0 + n],
            start=True, stop=True), inc=smm)
    bcB_t = smm.n
    xnv = {}
    for kt in range(NKT):
        xb = xtr[kt % 2]
        sxp = sxa[kt % 2]
        w = [(svx, xnv[kt - 2])] if kt >= 2 else [(svx, svx.n), (srow, routing_done)]
        p.E("y", lambda e, kt=kt, xb=xb: e.dma_start(
            out=xb[:], in_=x_in[kt * 128:(kt + 1) * 128, :]),
            wait=w, inc=sxp, inck=16)
        ld_t = sxp.n
        for cg, (c0, c1) in enumerate(cgs):
            n = c1 - c0
            p.E("v", lambda e, xb=xb, c0=c0, c1=c1, n=n, cg=cg, ld_t=ld_t, sxp=sxp: e.tensor_tensor(
                tmpv[:, 0:n], xb[:, c0:c1], ps[4 + 2 * cg][:, 0:n], ALU.mult),
                wait=[(sxp, ld_t), (smm, bcB_t)])
            p.E("v", lambda e, kt=kt, c0=c0, n=n, cg=cg: e.tensor_tensor(
                xn[kt][:, c0:c0 + n], tmpv[:, 0:n], ps[5 + 2 * cg][:, 0:n], ALU.add),
                inc=svx)
        xnv[kt] = svx.n
    xn_done = svx.n

    st = {"gu_ev": {}, "dn_ev": {}, "guw_free": {0: 0, 1: 0}, "dn_free": 0,
          "gu_gi": 0, "dn_gi": 0, "wbc_t": 0}

    def emit_glu(guT_ap, dnT_ap, inter, with_w, first):
        n_half = inter // FF
        for half in range(n_half):
            g_base = half * FF
            u_base = inter + half * FF
            chs = [(0, 256), (256, 512), (512, 768), (768, 1024), (1024, 1280), (1280, 1408)]
            for ich, (i0, i1) in enumerate(chs):
                ni = i1 - i0
                nmt = (ni + 127) // 128
                sw = swg[ich % 2]
                wld = [(smm, st["guw_free"][ich % 2])]
                p.E("g", lambda e, guT_ap=guT_ap, g_base=g_base, i0=i0, ni=ni: e.dma_start(
                    out=guw[:, 0:16 * ni].rearrange("p (k c) -> p k c", k=16),
                    in_=guT_ap[:, g_base + i0:g_base + i0 + ni].rearrange(
                        "(k p) c -> p k c", p=128)),
                    wait=wld, inc=sw, inck=16)
                p.E("g", lambda e, guT_ap=guT_ap, u_base=u_base, i0=i0, ni=ni: e.dma_start(
                    out=guw2[:, 0:16 * ni].rearrange("p (k c) -> p k c", k=16),
                    in_=guT_ap[:, u_base + i0:u_base + i0 + ni].rearrange(
                        "(k p) c -> p k c", p=128)),
                    inc=sw, inck=16)
                ld_t = sw.n
                for th in range(2):
                    t0 = th * 512
                    for mt in range(nmt):
                        m0 = i0 + mt * 128
                        nm = min(128, ni - mt * 128)
                        gi = st["gu_gi"]
                        st["gu_gi"] += 1
                        pg = ps[gi % 2]
                        pu = ps[2 + gi % 2]
                        sgb = sg2[gi % 2]
                        tub = tu2[gi % 2]
                        for kt in range(NKT):
                            w = []
                            if kt == 0:
                                w = [(sw, ld_t), (svx, xn_done),
                                     (svx, st["gu_ev"].get(gi - 2, 0))]
                            p.E("t", lambda e, kt=kt, mt=mt, nm=nm, t0=t0, pg=pg, ni=ni: e.matmul(
                                pg[0:nm, 0:512],
                                guw[:, kt * ni + mt * 128:kt * ni + mt * 128 + nm],
                                xn[kt][:, t0:t0 + 512],
                                start=(kt == 0), stop=(kt == NKT - 1)),
                                wait=w)
                            p.E("t", lambda e, kt=kt, mt=mt, nm=nm, t0=t0, pu=pu, ni=ni: e.matmul(
                                pu[0:nm, 0:512],
                                guw2[:, kt * ni + mt * 128:kt * ni + mt * 128 + nm],
                                xn[kt][:, t0:t0 + 512],
                                start=(kt == 0), stop=(kt == NKT - 1)),
                                inc=smm if kt == NKT - 1 else None)
                        mm_t = smm.n
                        p.E("s", lambda e, nm=nm, pg=pg, sgb=sgb, mm_t=mm_t: e.activation(
                            sgb[0:nm, 0:512], pg[0:nm, 0:512], AF.Silu),
                            wait=[(smm, mm_t)], inc=srow)
                        sil_t = srow.n
                        if with_w:
                            wbt = st["wbc_t"]
                            p.E("v", lambda e, nm=nm, pu=pu, tub=tub, th=th: e.tensor_tensor(
                                tub[0:nm, 0:512], pu[0:nm, 0:512], wbc[th][0:nm, 0:512],
                                ALU.mult),
                                wait=[(smm, mm_t), (sev, wbt)])
                        else:
                            p.E("v", lambda e, nm=nm, pu=pu, tub=tub, mm_t=mm_t: e.tensor_scalar(
                                tub[0:nm, 0:512], pu[0:nm, 0:512], 0.5, None, ALU.mult),
                                wait=[(smm, mm_t)])
                        mtile = m0 // 128
                        dnf = st["dn_free"]
                        p.E("v", lambda e, nm=nm, sgb=sgb, tub=tub, mtile=mtile, t0=t0: e.tensor_tensor(
                            act[mtile][0:nm, t0:t0 + 512], sgb[0:nm, 0:512],
                            tub[0:nm, 0:512], ALU.mult),
                            wait=[(srow, sil_t), (smm, dnf)], inc=svx)
                        st["gu_ev"][gi] = svx.n
                st["guw_free"][ich % 2] = smm.n
            act_done = svx.n
            for kseg, (k0, k1) in enumerate(((0, 6), (6, NFF))):
                nk = k1 - k0
                p.E("g", lambda e, dnT_ap=dnT_ap, half=half, k0=k0, nk=nk: e.dma_start(
                    out=dnw[:, 0:nk * D].rearrange("p (k c) -> p k c", k=nk),
                    in_=dnT_ap[half * FF + k0 * 128:half * FF + (k0 + nk) * 128, :].rearrange(
                        "(k p) c -> p k c", p=128)),
                    wait=[(smm, st["dn_free"])], inc=swd, inck=16)
                dn_t = swd.n
                for th in range(2):
                    t0 = th * 512
                    for mt in range(NKT):
                        gi = st["dn_gi"]
                        st["dn_gi"] += 1
                        pd = ps[4 + gi % 3]
                        for kt in range(nk):
                            w = []
                            if kt == 0:
                                w = [(swd, dn_t), (svx, act_done),
                                     (svx, st["dn_ev"].get(gi - 3, 0))]
                            p.E("t", lambda e, kt=kt, k0=k0, mt=mt, pd=pd, t0=t0, nk=nk: e.matmul(
                                pd[:, 0:512], dnw[:, kt * D + mt * 128:kt * D + mt * 128 + 128],
                                act[k0 + kt][:, t0:t0 + 512],
                                start=(kt == 0), stop=(kt == nk - 1)),
                                wait=w, inc=smm if kt == nk - 1 else None)
                        dmm_t = smm.n
                        if first and half == 0 and kseg == 0:
                            p.E("v", lambda e, mt=mt, pd=pd, t0=t0, dmm_t=dmm_t: e.tensor_copy(
                                ysb[mt][:, t0:t0 + 512], pd[:, 0:512]),
                                wait=[(smm, dmm_t)], inc=svx)
                        else:
                            p.E("v", lambda e, mt=mt, pd=pd, t0=t0, dmm_t=dmm_t: e.tensor_tensor(
                                ysb[mt][:, t0:t0 + 512], ysb[mt][:, t0:t0 + 512], pd[:, 0:512],
                                ALU.add),
                                wait=[(smm, dmm_t)], inc=svx)
                        st["dn_ev"][gi] = svx.n
                st["dn_free"] = smm.n

    if trunc >= 1:
        emit_glu(sguT_in, sdT_in, 2 * FF, with_w=False, first=True)

    for ei in range(min(16, max(0, trunc - 1))):
        if not use_w:
            emit_glu(guT_in[ei], dnT_in[ei], FF, with_w=False, first=False)
            continue
        for th in range(2):
            t0 = th * 512
            p.E("t", lambda e, ei=ei, t0=t0: e.matmul(
                ps[7][0:1, 0:512], eye16[:, ei:ei + 1], wrows[0:16, t0:t0 + 512],
                start=True, stop=True),
                wait=[(srow, routing_done), (sev, sev.n)], inc=smm)
            sel_t = smm.n
            p.E("s", lambda e, sel_t=sel_t: e.activation(
                wrow1[0:1, :], ps[7][0:1, 0:512], AF.Identity),
                wait=[(smm, sel_t)], inc=srow)
            wr_t = srow.n
            p.E("t", lambda e, wr_t=wr_t: e.matmul(
                ps[7][:, 0:512], ones_r[0:1, :], wrow1[0:1, :],
                start=True, stop=True),
                wait=[(srow, wr_t)], inc=smm)
            bc2_t = smm.n
            p.E("s", lambda e, th=th, bc2_t=bc2_t: e.activation(
                wbc[th][:, :], ps[7][:, 0:512], AF.Identity),
                wait=[(smm, bc2_t)], inc=sev)
        st["wbc_t"] = sev.n
        emit_glu(guT_in[ei], dnT_in[ei], FF, with_w=True, first=False)
    y_done = svx.n

    stv = {}
    sst_b = (sst2[0].n, sst2[1].n)
    for kt in range(NKT):
        xb = xtr[kt % 2]
        sxp = sxa[kt % 2]
        w = [(svx, y_done)]
        if kt >= 2:
            w.append((svx, stv[kt - 2]))
        p.E("y", lambda e, kt=kt, xb=xb: e.dma_start(
            out=xb[:], in_=x_in[kt * 128:(kt + 1) * 128, :]),
            wait=w, inc=sxp, inck=16)
        ld_t = sxp.n
        for th in range(2):
            t0 = th * 512
            ob = xob[th]
            wst = [(sxp, ld_t)]
            if kt >= 1:
                wst.append((sst2[th], sst_b[th] + 16 * kt))
            if trunc >= 1:
                p.E("v", lambda e, kt=kt, xb=xb, t0=t0, ob=ob: e.tensor_tensor(
                    ob[:, :], xb[:, t0:t0 + 512], ysb[kt][:, t0:t0 + 512], ALU.add),
                    wait=wst, inc=svx)
            else:
                p.E("v", lambda e, kt=kt, xb=xb, t0=t0, ob=ob: e.tensor_copy(
                    ob[:, :], xb[:, t0:t0 + 512]),
                    wait=wst, inc=svx)
            ev_t = svx.n
            p.E("y", lambda e, kt=kt, t0=t0, ob=ob, ev_t=ev_t: e.dma_start(
                out=xo[kt * 128:(kt + 1) * 128, t0:t0 + 512], in_=ob[:, :]),
                wait=[(svx, ev_t)], inc=sst2[th], inck=16)
        stv[kt] = svx.n
    stA, stBv = sst2[0].n, sst2[1].n
    p.q["y"].append(lambda e: e.wait_ge(sst2[0].h, stA))
    p.q["y"].append(lambda e: e.wait_ge(sst2[1].h, stBv))
    return p


def run_moe(x_fm_slices, wts, trunc=99, use_w=True):
    p = build_moe(trunc, use_w)
    nc = p.finish()
    in_maps = []
    for c in range(NCORES):
        m = {"x": x_fm_slices[c]}
        m.update(wts)
        m.update(_CONSTS)
        in_maps.append(m)
    res = run_bass_kernel_spmd(nc, in_maps, core_ids=list(range(NCORES)))
    return [r["xo"] for r in res.results]


def moe_weights(inputs):
    gw = np.asarray(inputs["L1_ffn_gate_weight"], np.float32)
    gu = np.asarray(inputs["L1_ffn_gate_up_weights"], np.float32)
    dn = np.asarray(inputs["L1_ffn_down_weights"], np.float32)
    sgu = np.asarray(inputs["L1_ffn_shared_gate_up"], np.float32)
    sd = np.asarray(inputs["L1_ffn_shared_down"], np.float32)
    return {
        "gateT": np.ascontiguousarray(gw.T.reshape(NKT, 128, 16).transpose(1, 0, 2).reshape(128, NKT * 16)),
        "guT": np.ascontiguousarray(gu.transpose(0, 2, 1)),
        "dnT": np.ascontiguousarray(dn.transpose(0, 2, 1)),
        "sguT": np.ascontiguousarray(sgu.T),
        "sdT": np.ascontiguousarray(sd.T),
    }




def build_lru(p=None, io=None, prefix=""):
    width = TPC
    N2 = 2
    if p is None:
        p = P("lru")
    p.stage(prefix)

    x_in = p.io_inp(io, "x", (D, width))
    BTre_in = p.inp("BTre", (D, 256))
    BTim_in = p.inp("BTim", (D, 256))
    CTre_in = p.inp("CTre", (256, D))
    CTim_in = p.inp("CTim", (256, D))
    nu_in = p.inp("nu_log", (128, 2))
    th_in = p.inp("theta_log", (128, 2))
    dv_in = p.inp("Dvec", (128, NKT))
    cj_in = p.inp("cJ", (128, width))
    onesw_in = p.inp("cONESW", (128, width))
    msel_in = p.inp("mSEL", (128, 8))
    ones_in = p.inp("cONESC", (128, 1))
    onesr_in = p.inp("cONESR", (1, 128))
    cmisc_in = p.inp("cMISC", (128, 8))
    xo = p.io_out(io, "xo", (D, width))

    agin = p.dram("lru_agin", (128, 4))
    agout = p.dram("lru_agout", (8 * 128, 4))

    xn = [p.sbuf(f"xn{i}", (128, width), BF16) for i in range(NKT)]
    xtr = [p.sbuf(f"xtr{i}", (128, width)) for i in range(2)]
    x2b = [p.sbuf("x2b0", (128, width))] * 2
    rows = p.sbuf("rows", (1, 3250))
    BT = p.sbuf("BT", (128, NKT * 256), BF16)
    BT2 = p.sbuf("BT2", (128, NKT * 256), BF16)
    CT = p.sbuf("CT", (128, N2 * D), BF16)
    CT2 = p.sbuf("CT2", (128, N2 * D), BF16)
    cph = [p.sbuf(f"cph{i}", (128, width)) for i in range(N2)]
    sph = [p.sbuf(f"sph{i}", (128, width)) for i in range(N2)]
    cph1 = [p.sbuf(f"cph1_{i}", (128, width)) for i in range(N2)]
    sph1 = [p.sbuf(f"sph1_{i}", (128, width)) for i in range(N2)]
    magb = [p.sbuf(f"magb{i}", (128, width)) for i in range(N2)]
    magp = [p.sbuf(f"magp{i}", (128, width)) for i in range(N2)]
    bur = [p.sbuf(f"bur{i}", (128, width)) for i in range(N2)]
    bui = [p.sbuf(f"bui{i}", (128, width)) for i in range(N2)]
    wre = [p.sbuf(f"wre{i}", (128, width)) for i in range(N2)]
    wim = [p.sbuf(f"wim{i}", (128, width)) for i in range(N2)]
    hbr = [p.sbuf(f"hbr{i}", (128, width), BF16) for i in range(N2)]
    hbi = [p.sbuf(f"hbi{i}", (128, width), BF16) for i in range(N2)]
    cols = p.sbuf("cols", (128, 32))
    fins = p.sbuf("fins", (128, 32))
    cj = p.sbuf("cj", (128, width))
    onesw = p.sbuf("onesw", (128, width))
    msel = p.sbuf("msel", (128, 8))
    tq = p.sbuf("tq", (128, width))
    tq2 = p.sbuf("tq2", (128, width))
    ones_c = p.sbuf("onesc", (128, 1))
    ones_r = p.sbuf("onesr", (1, 128))
    cmisc = p.sbuf("cmisc", (128, 8))
    dcol = p.sbuf("dcol", (128, NKT))
    tmpv = p.sbuf("tmpv", (128, 512))
    xob = [p.sbuf(f"xob{i}", (128, 512)) for i in range(2)]

    ps = [p.psum(f"ps{i}") for i in range(8)]

    sx = p.sem("sx")
    sg = p.sem("sg")
    sxa = [p.sem("sxa0"), p.sem("sxa1")]
    smm = p.sem("smm")
    srow = p.sem("srow")
    svx = p.sem("svx")
    sev = p.sem("sev")
    scc = p.sem("scc")
    sst2 = [p.sem("sst0"), p.sem("sst1")]

    cgs = col_groups(width)
    PI = float(np.pi)
    TWOPI = float(2 * np.pi)

    for dst, src_ in ((ones_c, ones_in), (ones_r, onesr_in), (cmisc, cmisc_in),
                      (cj, cj_in), (onesw, onesw_in), (msel, msel_in),
                      (dv_in_sb := dcol, dv_in)):
        p.E("y", lambda e, dst=dst, src_=src_: e.dma_start(out=dst[:], in_=src_[:, :]),
            inc=sx, inck=16)
    p.E("y", lambda e: e.dma_start(out=cols[:, 0:2], in_=th_in[:, :]), inc=sx, inck=16)
    p.E("y", lambda e: e.dma_start(out=cols[:, 2:4], in_=nu_in[:, :]), inc=sx, inck=16)
    p.E("g", lambda e: e.dma_start(
        out=BT[:, :].rearrange("p (k c) -> p k c", k=NKT),
        in_=BTre_in[:, :].rearrange("(k p) c -> p k c", p=128)), inc=sg, inck=16)
    p.E("g", lambda e: e.dma_start(
        out=BT2[:, :].rearrange("p (k c) -> p k c", k=NKT),
        in_=BTim_in[:, :].rearrange("(k p) c -> p k c", p=128)), inc=sg, inck=16)
    p.E("g", lambda e: e.dma_start(
        out=CT[:, :].rearrange("p (k c) -> p k c", k=N2),
        in_=CTre_in[:, :].rearrange("(k p) c -> p k c", p=128)), inc=sg, inck=16)
    p.E("g", lambda e: e.dma_start(
        out=CT2[:, :].rearrange("p (k c) -> p k c", k=N2),
        in_=CTim_in[:, :].rearrange("(k p) c -> p k c", p=128)), inc=sg, inck=16)
    wload = sg.n
    all_ld = sx.n

    p.E("s", lambda e: e.activation(cols[:, 4:6], cols[:, 0:2], AF.Exp),
        wait=[(sx, all_ld)], inc=srow)
    p.E("v", lambda e, v=srow.n: e.tensor_scalar(
        cols[:, 16:18], cols[:, 2:4], -1.0, None, ALU.mult), wait=[(srow, srow.n)])
    p.E("v", lambda e: e.tensor_tensor(cols[:, 16:18], cols[:, 16:18], cols[:, 2:4], ALU.max))
    p.E("v", lambda e: e.tensor_scalar(cols[:, 18:20], cols[:, 2:4], 0.0, None, ALU.max),
        inc=svx)
    p.E("s", lambda e, v=svx.n: e.activation(cols[:, 16:18], cols[:, 16:18], AF.Exp,
                                             scale=-1.0), wait=[(svx, svx.n)], inc=srow)
    p.E("s", lambda e: e.activation(cols[:, 16:18], cols[:, 16:18], AF.Ln,
                                    bias=1.0), inc=srow)
    p.E("v", lambda e, v=srow.n: e.tensor_tensor(
        cols[:, 6:8], cols[:, 16:18], cols[:, 18:20], ALU.add),
        wait=[(srow, srow.n)], inc=svx)
    p.E("s", lambda e, v=svx.n: e.activation(cols[:, 6:8], cols[:, 6:8], AF.Exp),
        wait=[(svx, svx.n)], inc=srow)
    p.E("s", lambda e: e.activation(cols[:, 8:10], cols[:, 6:8], AF.Exp, scale=-1.0),
        inc=srow)
    p.E("v", lambda e, v=srow.n: e.tensor_tensor(
        cols[:, 16:18], cols[:, 8:10], cols[:, 8:10], ALU.mult),
        wait=[(srow, srow.n)])
    p.E("v", lambda e: e.tensor_scalar(cols[:, 16:18], cols[:, 16:18], -1.0, 1.0,
                                       ALU.mult, ALU.add))
    p.E("v", lambda e: e.tensor_scalar(cols[:, 16:18], cols[:, 16:18], 0.0, None,
                                       ALU.max), inc=svx)
    p.E("s", lambda e, v=svx.n: e.activation(cols[:, 10:12], cols[:, 16:18], AF.Ln,
                                             bias=cmisc[:, 3:4]),
        wait=[(svx, svx.n)], inc=srow)
    p.E("s", lambda e: e.activation(cols[:, 10:12], cols[:, 10:12], AF.Exp,
                                    scale=0.5), inc=srow)
    MAGIC = 8388608.0
    p.E("s", lambda e: e.activation(cols[:, 18:20], cols[:, 6:8], AF.Exp,
                                    scale=-1024.0), inc=srow)
    p.E("v", lambda e, v=srow.n: e.tensor_scalar(
        cols[:, 28:30], cols[:, 4:6], 1.0 / TWOPI, None, ALU.mult),
        wait=[(srow, srow.n)])
    p.E("v", lambda e: e.tensor_scalar(
        cols[:, 20:22], cols[:, 28:30], 1024.0, None, ALU.mult))
    p.E("v", lambda e: e.tensor_scalar(cols[:, 26:28], cols[:, 20:22], MAGIC, None,
                                       ALU.add))
    p.E("v", lambda e: e.tensor_scalar(cols[:, 26:28], cols[:, 26:28], -MAGIC, None,
                                       ALU.add))
    p.E("v", lambda e: e.tensor_tensor(cols[:, 20:22], cols[:, 20:22], cols[:, 26:28],
                                       ALU.subtract))
    p.E("v", lambda e: e.tensor_scalar(cols[:, 20:22], cols[:, 20:22], TWOPI, None,
                                       ALU.mult), inc=svx)
    p.E("s", lambda e, v=svx.n: e.activation(cols[:, 22:24], cols[:, 20:22], AF.Sin),
        wait=[(svx, svx.n)], inc=srow)
    p.E("s", lambda e: e.activation(cols[:, 24:26], cols[:, 20:22], AF.Sin, scale=0.5),
        inc=srow)
    p.E("v", lambda e, v=srow.n: e.tensor_tensor(
        cols[:, 24:26], cols[:, 24:26], cols[:, 24:26], ALU.mult),
        wait=[(srow, srow.n)])
    p.E("v", lambda e: e.tensor_scalar(cols[:, 24:26], cols[:, 24:26], -2.0, 1.0,
                                       ALU.mult, ALU.add))
    p.E("v", lambda e: e.tensor_tensor(cols[:, 12:14], cols[:, 18:20], cols[:, 24:26],
                                       ALU.mult))
    p.E("v", lambda e: e.tensor_tensor(cols[:, 14:16], cols[:, 18:20], cols[:, 22:24],
                                       ALU.mult), inc=svx)
    cols_done = svx.n

    def emit_phase(b, jshift, sdst, cdst, w0):
        t2p = cols[:, 28 + b:29 + b]
        p.E("v", lambda e, b=b, t2p=t2p: e.tensor_scalar(
            tq[:, :], cj[:, :], float(jshift), t2p, ALU.add, ALU.mult), wait=w0)
        p.E("v", lambda e: e.tensor_scalar(tq2[:, :], tq[:, :], MAGIC, None, ALU.add))
        p.E("v", lambda e: e.tensor_scalar(tq2[:, :], tq2[:, :], -MAGIC, None, ALU.add))
        p.E("v", lambda e: e.tensor_tensor(tq[:, :], tq[:, :], tq2[:, :], ALU.subtract))
        p.E("v", lambda e: e.tensor_scalar(tq[:, :], tq[:, :], TWOPI, None, ALU.mult),
            inc=svx)
        p.E("s", lambda e, sdst=sdst, v=svx.n: e.activation(sdst[:, :], tq[:, :], AF.Sin),
            wait=[(svx, svx.n)], inc=srow)
        p.E("s", lambda e, cdst=cdst: e.activation(cdst[:, :], tq[:, :], AF.Sin,
                                                   scale=0.5), inc=srow)
        p.E("v", lambda e, cdst=cdst, v=srow.n: e.tensor_tensor(
            cdst[:, :], cdst[:, :], cdst[:, :], ALU.mult), wait=[(srow, srow.n)])
        p.E("v", lambda e, cdst=cdst: e.tensor_scalar(
            cdst[:, :], cdst[:, :], -2.0, 1.0, ALU.mult, ALU.add))

    for b in range(N2):
        nu_c = cols[:, 6 + b:7 + b]
        emit_phase(b, 0, sph[b], cph[b], [(svx, cols_done)])
        emit_phase(b, 1, sph1[b], cph1[b], [])
        p.E("v", lambda e, b=b, nu_c=nu_c: e.tensor_scalar(
            tq2[:, :], cj[:, :], 1.0, nu_c, ALU.add, ALU.mult), inc=svx)
        p.E("s", lambda e, b=b, v=svx.n: e.activation(
            magp[b][:, :], tq2[:, :], AF.Exp, scale=-1.0),
            wait=[(svx, svx.n)], inc=srow)
        p.E("v", lambda e, b=b, mg=cols[:, 8 + b:9 + b], v=srow.n: e.tensor_scalar(
            magb[b][:, :], onesw[:, :], mg, None, ALU.mult),
            wait=[(srow, srow.n)], inc=svx)
    tables_done = svx.n

    RSTD, MU, R2 = 0, 1100, 2200
    xa_t = {}
    for kt in range(NKT):
        xb = xtr[kt % 2]
        sxp = sxa[kt % 2]
        w = [(smm, xa_t[kt - 2])] if kt >= 2 else []
        p.E("y", lambda e, kt=kt, xb=xb: e.dma_start(
            out=xb[:], in_=x_in[kt * 128:(kt + 1) * 128, :]),
            wait=w, inc=sxp, inck=16)
        ld_t = sxp.n
        x2 = x2b[0]
        wsq = [(sxp, ld_t)]
        if kt >= 1:
            wsq.append((smm, xa_t[kt - 1]))
        p.E("s", lambda e, xb=xb, x2=x2: e.activation(x2[:], xb[:], AF.Square),
            wait=wsq, inc=srow)
        sq_t = srow.n
        for cg, (c0, c1) in enumerate(cgs):
            p.E("t", lambda e, kt=kt, c0=c0, c1=c1, cg=cg, xb=xb, ld_t=ld_t, sxp=sxp: e.matmul(
                ps[cg][0:1, 0:c1 - c0], ones_c[:, 0:1], xb[:, c0:c1],
                start=(kt == 0), stop=(kt == NKT - 1)),
                wait=[(sxp, ld_t), (sx, all_ld)] if cg == 0 else [])
            p.E("t", lambda e, kt=kt, c0=c0, c1=c1, cg=cg, x2=x2, sq_t=sq_t: e.matmul(
                ps[2 + cg][0:1, 0:c1 - c0], ones_c[:, 0:1], x2[:, c0:c1],
                start=(kt == 0), stop=(kt == NKT - 1)),
                wait=[(srow, sq_t)],
                inc=smm if cg == len(cgs) - 1 else None)
        xa_t[kt] = smm.n
    stats_done = smm.n
    for cg, (c0, c1) in enumerate(cgs):
        n = c1 - c0
        p.E("v", lambda e, cg=cg, c0=c0, n=n: e.tensor_scalar(
            rows[0:1, MU + c0:MU + c0 + n], ps[cg][0:1, 0:n], -1.0 / D, None, ALU.mult),
            wait=[(smm, stats_done)] if cg == 0 else [])
        p.E("v", lambda e, cg=cg, c0=c0, n=n: e.tensor_scalar(
            rows[0:1, RSTD + c0:RSTD + c0 + n], ps[2 + cg][0:1, 0:n], 1.0 / D, None, ALU.mult))
    for h2 in range(2):
        p.E("v", lambda e, h2=h2: e.tensor_tensor(
            tmpv[0:1, 0:512], rows[0:1, MU + h2 * 512:MU + h2 * 512 + 512],
            rows[0:1, MU + h2 * 512:MU + h2 * 512 + 512], ALU.mult))
        p.E("v", lambda e, h2=h2: e.tensor_tensor(
            rows[0:1, RSTD + h2 * 512:RSTD + h2 * 512 + 512],
            rows[0:1, RSTD + h2 * 512:RSTD + h2 * 512 + 512],
            tmpv[0:1, 0:512], ALU.subtract), inc=svx if h2 == 1 else None)
    var_t = svx.n
    p.E("s", lambda e, var_t=var_t: e.activation(
        rows[0:1, RSTD:RSTD + width], rows[0:1, RSTD:RSTD + width], AF.Ln,
        bias=cmisc[0:1, 0:1]), wait=[(svx, var_t)], inc=srow)
    p.E("s", lambda e: e.activation(
        rows[0:1, RSTD:RSTD + width], rows[0:1, RSTD:RSTD + width], AF.Exp,
        scale=-0.5), inc=srow)
    p.E("v", lambda e, v=srow.n: e.engine_nop(), wait=[(srow, srow.n)])
    p.E("v", lambda e: e.tensor_tensor(
        rows[0:1, R2:R2 + width], rows[0:1, MU:MU + width],
        rows[0:1, RSTD:RSTD + width], ALU.mult), inc=svx)
    rows_done = svx.n

    for cg, (c0, c1) in enumerate(cgs):
        n = c1 - c0
        p.E("t", lambda e, c0=c0, n=n, cg=cg: e.matmul(
            ps[4 + 2 * cg][:, 0:n], ones_r[0:1, :], rows[0:1, RSTD + c0:RSTD + c0 + n],
            start=True, stop=True),
            wait=[(svx, rows_done)] if cg == 0 else [])
        p.E("t", lambda e, c0=c0, n=n, cg=cg: e.matmul(
            ps[5 + 2 * cg][:, 0:n], ones_r[0:1, :], rows[0:1, R2 + c0:R2 + c0 + n],
            start=True, stop=True), inc=smm)
    bcB_t = smm.n
    xnv = {}
    for kt in range(NKT):
        xb = xtr[kt % 2]
        sxp = sxa[kt % 2]
        w = [(svx, xnv[kt - 2])] if kt >= 2 else [(svx, svx.n)]
        p.E("y", lambda e, kt=kt, xb=xb: e.dma_start(
            out=xb[:], in_=x_in[kt * 128:(kt + 1) * 128, :]),
            wait=w, inc=sxp, inck=16)
        ld_t = sxp.n
        for cg, (c0, c1) in enumerate(cgs):
            n = c1 - c0
            p.E("v", lambda e, xb=xb, c0=c0, c1=c1, n=n, cg=cg, ld_t=ld_t, sxp=sxp: e.tensor_tensor(
                tmpv[:, 0:n], xb[:, c0:c1], ps[4 + 2 * cg][:, 0:n], ALU.mult),
                wait=[(sxp, ld_t), (smm, bcB_t)])
            p.E("v", lambda e, kt=kt, c0=c0, n=n, cg=cg: e.tensor_tensor(
                xn[kt][:, c0:c0 + n], tmpv[:, 0:n], ps[5 + 2 * cg][:, 0:n], ALU.add),
                inc=svx)
        xnv[kt] = svx.n
    xn_done = svx.n

    bu_ev = {}
    gi = 0
    for b in range(N2):
        for comp, (bt, dst) in enumerate(((BT, bur), (BT2, bui))):
            for th in range(2):
                t0 = th * 512
                pb = ps[gi % 2]
                for kt in range(NKT):
                    w = []
                    if kt == 0:
                        w = [(svx, xn_done), (sg, wload), (svx, bu_ev.get(gi - 2, 0))]
                    p.E("t", lambda e, kt=kt, b=b, t0=t0, pb=pb, bt=bt: e.matmul(
                        pb[:, 0:512], bt[:, kt * 256 + b * 128:kt * 256 + b * 128 + 128],
                        xn[kt][:, t0:t0 + 512],
                        start=(kt == 0), stop=(kt == NKT - 1)),
                        wait=w, inc=smm if kt == NKT - 1 else None)
                mm_t = smm.n
                p.E("v", lambda e, b=b, t0=t0, pb=pb, dst=dst, mm_t=mm_t,
                    gc=cols[:, 10 + b:11 + b]: e.tensor_scalar(
                    dst[b][:, t0:t0 + 512], pb[:, 0:512], gc, None, ALU.mult),
                    wait=[(smm, mm_t), (svx, tables_done)], inc=svx)
                bu_ev[gi] = svx.n
                gi += 1
    bu_done = svx.n

    scan_done = {}
    for b in range(N2):
        p.E("v", lambda e, b=b: e.tensor_tensor(tq[:, :], cph[b][:, :], bur[b][:, :],
                                                ALU.mult), wait=[(svx, bu_done)])
        p.E("v", lambda e, b=b: e.tensor_tensor(tq2[:, :], sph[b][:, :], bui[b][:, :],
                                                ALU.mult))
        p.E("v", lambda e, b=b: e.tensor_tensor(wre[b][:, :], tq[:, :], tq2[:, :],
                                                ALU.add))
        p.E("v", lambda e, b=b: e.tensor_tensor(tq[:, :], cph[b][:, :], bui[b][:, :],
                                                ALU.mult))
        p.E("v", lambda e, b=b: e.tensor_tensor(tq2[:, :], sph[b][:, :], bur[b][:, :],
                                                ALU.mult))
        p.E("v", lambda e, b=b: e.tensor_tensor(wim[b][:, :], tq[:, :], tq2[:, :],
                                                ALU.subtract))
        p.E("v", lambda e, b=b: e.tensor_tensor_scan(
            bur[b][:, :], magb[b][:, :], wre[b][:, :], 0.0, ALU.mult, ALU.add))
        p.E("v", lambda e, b=b: e.tensor_tensor_scan(
            bui[b][:, :], magb[b][:, :], wim[b][:, :], 0.0, ALU.mult, ALU.add))
        p.E("v", lambda e, b=b: e.tensor_tensor(tq[:, :], cph[b][:, :], bur[b][:, :],
                                                ALU.mult))
        p.E("v", lambda e, b=b: e.tensor_tensor(tq2[:, :], sph[b][:, :], bui[b][:, :],
                                                ALU.mult))
        p.E("v", lambda e, b=b: e.tensor_tensor(wre[b][:, :], tq[:, :], tq2[:, :],
                                                ALU.subtract))
        p.E("v", lambda e, b=b: e.tensor_tensor(tq[:, :], cph[b][:, :], bui[b][:, :],
                                                ALU.mult))
        p.E("v", lambda e, b=b: e.tensor_tensor(tq2[:, :], sph[b][:, :], bur[b][:, :],
                                                ALU.mult))
        p.E("v", lambda e, b=b: e.tensor_tensor(wim[b][:, :], tq[:, :], tq2[:, :],
                                                ALU.add), inc=svx)
        scan_done[b] = svx.n
    h_local_done = svx.n

    for b in range(N2):
        p.E("v", lambda e, b=b: e.tensor_copy(
            fins[:, 2 * b:2 * b + 1], wre[b][:, width - 1:width]),
            wait=[(svx, h_local_done)])
        p.E("v", lambda e, b=b: e.tensor_copy(
            fins[:, 2 * b + 1:2 * b + 2], wim[b][:, width - 1:width]),
            inc=svx if b == N2 - 1 else None)
    fin_t = svx.n
    p.E("y", lambda e, v=fin_t: e.dma_start(out=agin[:, :], in_=fins[:, 0:4]),
        wait=[(svx, fin_t)], inc=sx, inck=16)
    ag_ld = sx.n
    p.E("g", lambda e, v=ag_ld: e.collective_compute(
        "AllGather", ALU.bypass, replica_groups=[list(range(NCORES))],
        ins=[agin[:, :].opt()], outs=[agout[:, :].opt()]),
        wait=[(sx, ag_ld)], inc=scc)
    p.E("g", lambda e, v=scc.n: e.dma_start(
        out=fins[:, 0:32].rearrange("p (r c) -> p r c", r=8),
        in_=agout[:, :].rearrange("(r p) c -> p r c", p=128)),
        wait=[(scc, scc.n)], inc=sg, inck=16)
    fins_ld = sg.n

    p.E("v", lambda e, v=fins_ld: e.memzero(cols[:, 24:28]), wait=[(sg, fins_ld)])
    for r in range(8):
        fof = 4 * r
        if r % 4 == 0:
            p.E("v", lambda e, fof=fof: e.tensor_copy(cols[:, 16:20], fins[:, fof:fof + 4]))
        else:
            for b in range(N2):
                are = cols[:, 12 + b:13 + b]
                aim = cols[:, 14 + b:15 + b]
                pre = cols[:, 16 + 2 * b:17 + 2 * b]
                pim = cols[:, 17 + 2 * b:18 + 2 * b]
                p.E("v", lambda e, pre=pre, are=are: e.tensor_scalar(
                    cols[:, 20:21], pre, are, None, ALU.mult))
                p.E("v", lambda e, pim=pim, aim=aim: e.tensor_scalar(
                    cols[:, 21:22], pim, aim, None, ALU.mult))
                p.E("v", lambda e, pre=pre, aim=aim: e.tensor_scalar(
                    cols[:, 22:23], pre, aim, None, ALU.mult))
                p.E("v", lambda e, pim=pim, are=are: e.tensor_scalar(
                    cols[:, 23:24], pim, are, None, ALU.mult))
                p.E("v", lambda e: e.tensor_tensor(
                    cols[:, 20:21], cols[:, 20:21], cols[:, 21:22], ALU.subtract))
                p.E("v", lambda e: e.tensor_tensor(
                    cols[:, 22:23], cols[:, 22:23], cols[:, 23:24], ALU.add))
                p.E("v", lambda e, b=b, fof=fof, pre=pre: e.tensor_tensor(
                    pre, cols[:, 20:21], fins[:, fof + 2 * b:fof + 2 * b + 1], ALU.add))
                p.E("v", lambda e, b=b, fof=fof, pim=pim: e.tensor_tensor(
                    pim, cols[:, 22:23], fins[:, fof + 2 * b + 1:fof + 2 * b + 2], ALU.add))
        for q in range(4):
            p.E("v", lambda e, q=q, r=r: e.scalar_tensor_tensor(
                cols[:, 24 + q:25 + q], cols[:, 16 + q:17 + q], msel[:, r:r + 1],
                cols[:, 24 + q:25 + q], ALU.mult, ALU.add),
                inc=svx if (q == 3 and r == 7) else None)
    carry_done = svx.n

    for b in range(N2):
        cre = cols[:, 24 + 2 * b:25 + 2 * b]
        cim = cols[:, 25 + 2 * b:26 + 2 * b]
        p.E("v", lambda e, b=b, cre=cre: e.tensor_scalar(
            tq[:, :], cph1[b][:, :], cre, None, ALU.mult), wait=[(svx, carry_done)])
        p.E("v", lambda e, b=b, cim=cim: e.tensor_scalar(
            tq2[:, :], sph1[b][:, :], cim, None, ALU.mult))
        p.E("v", lambda e: e.tensor_tensor(tq[:, :], tq[:, :], tq2[:, :], ALU.subtract))
        p.E("v", lambda e, b=b: e.tensor_tensor(tq[:, :], tq[:, :], magp[b][:, :],
                                                ALU.mult))
        p.E("v", lambda e, b=b: e.tensor_tensor(wre[b][:, :], wre[b][:, :], tq[:, :],
                                                ALU.add))
        p.E("v", lambda e, b=b, cim=cim: e.tensor_scalar(
            tq[:, :], cph1[b][:, :], cim, None, ALU.mult))
        p.E("v", lambda e, b=b, cre=cre: e.tensor_scalar(
            tq2[:, :], sph1[b][:, :], cre, None, ALU.mult))
        p.E("v", lambda e: e.tensor_tensor(tq[:, :], tq[:, :], tq2[:, :], ALU.add))
        p.E("v", lambda e, b=b: e.tensor_tensor(tq[:, :], tq[:, :], magp[b][:, :],
                                                ALU.mult))
        p.E("v", lambda e, b=b: e.tensor_tensor(wim[b][:, :], wim[b][:, :], tq[:, :],
                                                ALU.add))
        p.E("v", lambda e, b=b: e.tensor_copy(hbr[b][:, :], wre[b][:, :]))
        p.E("v", lambda e, b=b: e.tensor_scalar(hbi[b][:, :], wim[b][:, :], -1.0, None,
                                                ALU.mult), inc=svx)
    h_done = svx.n

    yst = {}
    sst_b = (sst2[0].n, sst2[1].n)
    for kt in range(NKT):
        xb = xtr[kt % 2]
        sxp = sxa[kt % 2]
        w = [(svx, h_done)]
        if kt >= 2:
            w.append((svx, yst[kt - 2]))
        p.E("y", lambda e, kt=kt, xb=xb: e.dma_start(
            out=xb[:], in_=x_in[kt * 128:(kt + 1) * 128, :]),
            wait=w, inc=sxp, inck=16)
        ld_t = sxp.n
        for th in range(2):
            t0 = th * 512
            pb = ps[6 + (kt * 2 + th) % 2]
            for b in range(N2):
                p.E("t", lambda e, kt=kt, b=b, t0=t0, pb=pb: e.matmul(
                    pb[:, 0:512], CT[:, b * D + kt * 128:b * D + kt * 128 + 128],
                    hbr[b][:, t0:t0 + 512],
                    start=(b == 0), stop=False),
                    wait=[(svx, h_done), (svx, yst.get(kt - 1, 0))] if b == 0 and th == 0 else [])
                p.E("t", lambda e, kt=kt, b=b, t0=t0, pb=pb: e.matmul(
                    pb[:, 0:512], CT2[:, b * D + kt * 128:b * D + kt * 128 + 128],
                    hbi[b][:, t0:t0 + 512],
                    start=False, stop=(b == N2 - 1)),
                    inc=smm if b == N2 - 1 else None)
            mm_t = smm.n
            ob = xob[th]
            wst = [(smm, mm_t), (sxp, ld_t)]
            if kt >= 1:
                wst.append((sst2[th], sst_b[th] + 16 * kt))
            p.E("v", lambda e, kt=kt, xb=xb, t0=t0, ob=ob,
                dc=dcol[:, kt:kt + 1]: e.scalar_tensor_tensor(
                ob[:, :], xn[kt][:, t0:t0 + 512], dc, xb[:, t0:t0 + 512],
                ALU.mult, ALU.add), wait=wst)
            p.E("v", lambda e, ob=ob, pb=pb: e.tensor_tensor(
                ob[:, :], ob[:, :], pb[:, 0:512], ALU.add), inc=svx)
            ev_t = svx.n
            p.E("y", lambda e, kt=kt, t0=t0, ob=ob, ev_t=ev_t: e.dma_start(
                out=xo[kt * 128:(kt + 1) * 128, t0:t0 + 512], in_=ob[:, :]),
                wait=[(svx, ev_t)], inc=sst2[th], inck=16)
        yst[kt] = svx.n
    stA, stBv = sst2[0].n, sst2[1].n
    p.q["y"].append(lambda e: e.wait_ge(sst2[0].h, stA))
    p.q["y"].append(lambda e: e.wait_ge(sst2[1].h, stBv))
    return p


def run_lru(x_fm_slices, wts):
    p = build_lru()
    nc = p.finish()
    in_maps = []
    for c in range(NCORES):
        m = {"x": x_fm_slices[c], "mSEL": _lru_msel(c)}
        m.update(wts)
        m.update(_CONSTS)
        in_maps.append(m)
    res = run_bass_kernel_spmd(nc, in_maps, core_ids=list(range(NCORES)))
    return [r["xo"] for r in res.results]


def _lru_msel(c):
    b, k = c // 4, c % 4
    m = np.zeros((128, 8), np.float32)
    if k > 0:
        m[:, 4 * b + k - 1] = 1.0
    return m


def lru_weights(inputs):
    return {
        "BTre": np.ascontiguousarray(np.asarray(inputs["L1_attn_B_re"], np.float32).T),
        "BTim": np.ascontiguousarray(np.asarray(inputs["L1_attn_B_im"], np.float32).T),
        "CTre": np.ascontiguousarray(np.asarray(inputs["L1_attn_C_re"], np.float32).T),
        "CTim": np.ascontiguousarray(np.asarray(inputs["L1_attn_C_im"], np.float32).T),
        "nu_log": np.ascontiguousarray(np.asarray(inputs["L1_attn_nu_log"], np.float32).reshape(2, 128).T),
        "theta_log": np.ascontiguousarray(np.asarray(inputs["L1_attn_theta_log"], np.float32).reshape(2, 128).T),
        "Dvec": np.ascontiguousarray(np.asarray(inputs["L1_attn_D"], np.float32).reshape(NKT, 128).T),
    }




def build_attn_pre(p=None, io=None, prefix=""):
    width = W
    if p is None:
        p = P("attnpre")
    p.stage(prefix)

    x_in = p.io_inp(io, "x", (D, width))
    wq_in = p.inp("WqP", (D, 2048))
    wk_in = p.inp("WkP", (D, 2048))
    wv_in = p.inp("Wv", (D, 3072))
    wg_in = p.inp("Wg", (D, 3072))
    wab_in = p.inp("Wab", (D, 64))
    cqw_in = p.inp("convQ", (128, 16 * 4))
    ckw_in = p.inp("convK", (128, 16 * 4))
    cvw_in = p.inp("convV", (128, 24 * 4))
    rms1_in = p.inp("rms1P", (128, NKT))
    adt_in = p.inp("adt", (128, 4))
    hmask_in = p.inp("hmask", (128, 4))
    sel48_in = p.inp("cSEL48", (128, 2))
    bcq_in = p.inp("cBCQ", (2, 128))
    bck_in = p.inp("cBCK", (2, 128))
    ones_in = p.inp("cONESC", (128, 1))
    onesr_in = p.inp("cONESR", (1, 128))
    cmisc_in = p.inp("cMISC", (128, 8))
    qf_o = p.io_out(io, "qf", (2048, TPC), BF16)
    kf_o = p.io_out(io, "kf", (2048, TPC), BF16)
    vf_o = p.io_out(io, "vf", (3072, TPC), BF16)
    gate_o = p.io_out(io, "gate", (3072, TPC), BF16)
    g_o = p.io_out(io, "gout", (32, TPC))
    b_o = p.io_out(io, "bout", (32, TPC))

    xtr = [p.sbuf(f"xtr{i}", (128, width)) for i in range(2)]
    x2b = p.sbuf("x2b0", (128, width))
    rows = p.sbuf("rows", (1, 3250))
    hb = [p.sbuf(f"h{i}", (128, width), BF16) for i in range(NKT)]
    hf = p.sbuf("hf", (128, 512))
    wabf = p.sbuf("wabf", (128, NKT * 64))
    wbuf = [p.sbuf(f"wb{i}", (128, NKT * 128), BF16) for i in range(2)]
    ph = p.sbuf("ph", (128, width))
    pc = p.sbuf("pc", (128, width))
    cw = p.sbuf("cw", (128, (16 + 16 + 24) * 4))
    rms1 = p.sbuf("rms1_sb", (128, NKT))
    adt = p.sbuf("adt_sb", (128, 4))
    hmask = p.sbuf("hmask_sb", (128, 4))
    sel48 = p.sbuf("sel48", (128, 2))
    bcq = p.sbuf("bcq", (2, 128))
    bck = p.sbuf("bck", (2, 128))
    rsq = p.sbuf("rsq", (2, 1024))
    gb = p.sbuf("gb", (64, width))
    gb2 = p.sbuf("gb2", (64, width))
    tmpv = p.sbuf("tmpv", (128, 512))
    obuf = [p.sbuf(f"ob{i}", (128, TPC), BF16) for i in range(2)]
    ones_c = p.sbuf("onesc", (128, 1))
    ones_r = p.sbuf("onesr", (1, 128))
    cmisc = p.sbuf("cmisc", (128, 8))

    ps = [p.psum(f"ps{i}") for i in range(8)]

    sx = p.sem("sx")
    sg = p.sem("sg")
    sxa = [p.sem("sxa0"), p.sem("sxa1")]
    smm = p.sem("smm")
    srow = p.sem("srow")
    svx = p.sem("svx")
    sev = p.sem("sev")
    sst = p.sem("sst")

    RSTD, MU, R2 = 0, 1100, 2200

    for dst, src_ in ((ones_c, ones_in), (ones_r, onesr_in), (cmisc, cmisc_in),
                      (rms1, rms1_in), (adt, adt_in), (hmask, hmask_in),
                      (sel48, sel48_in), (bcq, bcq_in), (bck, bck_in)):
        p.E("y", lambda e, dst=dst, src_=src_: e.dma_start(out=dst[:], in_=src_[:, :]),
            inc=sx, inck=16)
    p.E("y", lambda e: e.dma_start(out=cw[:, 0:64], in_=cqw_in[:, :]), inc=sx, inck=16)
    p.E("y", lambda e: e.dma_start(out=cw[:, 64:128], in_=ckw_in[:, :]), inc=sx, inck=16)
    p.E("y", lambda e: e.dma_start(out=cw[:, 128:224], in_=cvw_in[:, :]), inc=sx, inck=16)
    p.E("y", lambda e: e.dma_start(
        out=wabf[:, :].rearrange("p (k c) -> p k c", k=NKT),
        in_=wab_in[:, :].rearrange("(k p) c -> p k c", p=128)), inc=sx, inck=16)
    all_ld = sx.n

    cg3 = [(0, 512), (512, 1024), (1024, width)]

    xa_t = {}
    for kt in range(NKT):
        xb = xtr[kt % 2]
        sxp = sxa[kt % 2]
        w = [(smm, xa_t[kt - 2])] if kt >= 2 else []
        p.E("y", lambda e, kt=kt, xb=xb: e.dma_start(
            out=xb[:], in_=x_in[kt * 128:(kt + 1) * 128, :]),
            wait=w, inc=sxp, inck=16)
        ld_t = sxp.n
        wsq = [(sxp, ld_t)]
        if kt >= 1:
            wsq.append((smm, xa_t[kt - 1]))
        p.E("s", lambda e, xb=xb: e.activation(x2b[:], xb[:], AF.Square),
            wait=wsq, inc=srow)
        sq_t = srow.n
        for cg, (c0, c1) in enumerate(cg3):
            pa = ps[0] if cg == 0 else (ps[1] if cg == 1 else ps[2])
            pq = ps[4] if cg == 0 else (ps[5] if cg == 1 else ps[6])
            p.E("t", lambda e, kt=kt, c0=c0, c1=c1, xb=xb, ld_t=ld_t, sxp=sxp, pa=pa: e.matmul(
                pa[0:1, 0:c1 - c0], ones_c[:, 0:1], xb[:, c0:c1],
                start=(kt == 0), stop=(kt == NKT - 1)),
                wait=[(sxp, ld_t), (sx, all_ld)] if cg == 0 else [])
            p.E("t", lambda e, kt=kt, c0=c0, c1=c1, sq_t=sq_t, pq=pq: e.matmul(
                pq[0:1, 0:c1 - c0], ones_c[:, 0:1], x2b[:, c0:c1],
                start=(kt == 0), stop=(kt == NKT - 1)),
                wait=[(srow, sq_t)],
                inc=smm if cg == 2 else None)
        xa_t[kt] = smm.n
    stats_done = smm.n
    for cg, (c0, c1) in enumerate(cg3):
        n = c1 - c0
        pa = ps[cg]
        pq = ps[4 + cg]
        p.E("v", lambda e, c0=c0, n=n, pa=pa: e.tensor_scalar(
            rows[0:1, MU + c0:MU + c0 + n], pa[0:1, 0:n], -1.0 / D, None, ALU.mult),
            wait=[(smm, stats_done)] if cg == 0 else [])
        p.E("v", lambda e, c0=c0, n=n, pq=pq: e.tensor_scalar(
            rows[0:1, RSTD + c0:RSTD + c0 + n], pq[0:1, 0:n], 1.0 / D, None, ALU.mult))
    for h2 in range(3):
        c0 = h2 * 512
        n = min(512, width - c0)
        p.E("v", lambda e, c0=c0, n=n: e.tensor_tensor(
            tmpv[0:1, 0:n], rows[0:1, MU + c0:MU + c0 + n],
            rows[0:1, MU + c0:MU + c0 + n], ALU.mult))
        p.E("v", lambda e, c0=c0, n=n: e.tensor_tensor(
            rows[0:1, RSTD + c0:RSTD + c0 + n], rows[0:1, RSTD + c0:RSTD + c0 + n],
            tmpv[0:1, 0:n], ALU.subtract), inc=svx if h2 == 2 else None)
    var_t = svx.n
    p.E("v", lambda e, var_t=var_t: e.tensor_scalar(
        rows[0:1, R2:R2 + width], rows[0:1, RSTD:RSTD + width], float(LN_EPS), None,
        ALU.add), wait=[(svx, var_t)])
    p.E("v", lambda e: e.reciprocal(rows[0:1, R2:R2 + width], rows[0:1, R2:R2 + width]))
    p.E("v", lambda e: e.tensor_tensor(
        rows[0:1, RSTD:RSTD + width], rows[0:1, RSTD:RSTD + width],
        rows[0:1, R2:R2 + width], ALU.mult))
    p.E("v", lambda e: e.tensor_scalar(
        rows[0:1, RSTD:RSTD + width], rows[0:1, RSTD:RSTD + width], float(RMS_EPS),
        None, ALU.add), inc=svx)
    q_t = svx.n
    p.E("s", lambda e, q_t=q_t: e.activation(
        rows[0:1, RSTD:RSTD + width], rows[0:1, RSTD:RSTD + width], AF.Ln),
        wait=[(svx, q_t)], inc=srow)
    p.E("s", lambda e: e.activation(
        rows[0:1, RSTD:RSTD + width], rows[0:1, RSTD:RSTD + width], AF.Exp,
        scale=-0.5), inc=srow)
    p.E("s", lambda e: e.activation(
        rows[0:1, R2:R2 + width], rows[0:1, R2:R2 + width], AF.Ln), inc=srow)
    p.E("s", lambda e: e.activation(
        rows[0:1, R2:R2 + width], rows[0:1, R2:R2 + width], AF.Exp,
        scale=0.5), inc=srow)
    p.E("v", lambda e, v=srow.n: e.tensor_tensor(
        rows[0:1, RSTD:RSTD + width], rows[0:1, RSTD:RSTD + width],
        rows[0:1, R2:R2 + width], ALU.mult),
        wait=[(srow, srow.n)])
    p.E("v", lambda e: e.tensor_tensor(
        rows[0:1, R2:R2 + width], rows[0:1, MU:MU + width],
        rows[0:1, RSTD:RSTD + width], ALU.mult), inc=svx)
    rows_done = svx.n

    hv = {}
    bc_by_cg = {}
    for cg, (c0, c1) in enumerate(cg3):
        n = c1 - c0
        pA = ps[2 + cg]
        pB = ps[5 + cg]
        wv0 = [(svx, rows_done)] if cg == 0 else [(svx, svx.n)]
        p.E("t", lambda e, c0=c0, n=n, pA=pA: e.matmul(
            pA[:, 0:n], ones_r[0:1, :], rows[0:1, RSTD + c0:RSTD + c0 + n],
            start=True, stop=True), wait=wv0)
        p.E("t", lambda e, c0=c0, n=n, pB=pB: e.matmul(
            pB[:, 0:n], ones_r[0:1, :], rows[0:1, R2 + c0:R2 + c0 + n],
            start=True, stop=True), inc=smm)
        bc_by_cg[cg] = (pA, pB, smm.n)
    abv = 0
    for kt in range(NKT):
        xb = xtr[kt % 2]
        sxp = sxa[kt % 2]
        w = [(svx, hv[kt - 2])] if kt >= 2 else [(svx, svx.n)]
        p.E("y", lambda e, kt=kt, xb=xb: e.dma_start(
            out=xb[:], in_=x_in[kt * 128:(kt + 1) * 128, :]),
            wait=w, inc=sxp, inck=16)
        ld_t = sxp.n
        for cg, (c0, c1) in enumerate(cg3):
            n = c1 - c0
            pA, pB, bt = bc_by_cg[cg]
            p.E("v", lambda e, xb=xb, c0=c0, n=n, pA=pA, ld_t=ld_t, sxp=sxp, bt=bt: e.tensor_tensor(
                tmpv[:, 0:n], xb[:, c0:c0 + n], pA[:, 0:n], ALU.mult),
                wait=[(sxp, ld_t), (smm, bt)])
            p.E("v", lambda e, c0=c0, n=n, pB=pB: e.tensor_tensor(
                tmpv[:, 0:n], tmpv[:, 0:n], pB[:, 0:n], ALU.add))
            p.E("v", lambda e, kt=kt, n=n: e.tensor_scalar(
                hf[:, 0:n], tmpv[:, 0:n], rms1[:, kt:kt + 1], None, ALU.mult),
                inc=svx)
            hf_t = svx.n
            p.E("v", lambda e, kt=kt, c0=c0, n=n: e.tensor_copy(
                hb[kt][:, c0:c0 + n], hf[:, 0:n]), inc=svx)
            p.E("t", lambda e, kt=kt, n=n, hf_t=hf_t: e.matmul(
                ps[0][0:64, 0:n], wabf[:, kt * 64:(kt + 1) * 64], hf[:, 0:n],
                start=True, stop=True),
                wait=[(svx, hf_t), (svx, abv)], inc=smm)
            ab_mt = smm.n
            if kt == 0:
                p.E("v", lambda e, c0=c0, n=n, ab_mt=ab_mt: e.tensor_copy(
                    gb[0:64, c0:c0 + n], ps[0][0:64, 0:n]),
                    wait=[(smm, ab_mt)], inc=svx)
            else:
                p.E("v", lambda e, c0=c0, n=n, ab_mt=ab_mt: e.tensor_tensor(
                    gb[0:64, c0:c0 + n], gb[0:64, c0:c0 + n], ps[0][0:64, 0:n],
                    ALU.add),
                    wait=[(smm, ab_mt)], inc=svx)
            abv = svx.n
        hv[kt] = svx.n
    h_done = svx.n

    pev = {}
    wfree = {}
    ost = []
    sq_free = []
    gidx = 0

    def emit_tile(w_in, gmt, out_ap, kind, conv_base):
        nonlocal gidx
        wb = wbuf[gidx % 2]
        wld = [(smm, wfree.get(gidx - 2, 0))]
        p.E("g", lambda e, w_in=w_in, gmt=gmt, wb=wb: e.dma_start(
            out=wb[:, :].rearrange("p (k c) -> p k c", k=NKT),
            in_=w_in[:, gmt * 128:(gmt + 1) * 128].rearrange("(k p) c -> p k c", p=128)),
            wait=wld, inc=sg, inck=16)
        ld_t = sg.n
        pb = ps[gidx % 2]
        for cg, (c0, c1) in enumerate(cg3):
            n = c1 - c0
            for kt in range(NKT):
                wq = []
                if kt == 0:
                    if cg == 0:
                        wq = [(sg, ld_t), (svx, h_done), (svx, pev.get(gidx - 2, 0))]
                    else:
                        wq = [(svx, svx.n)]
                p.E("t", lambda e, kt=kt, c0=c0, n=n, pb=pb, wb=wb: e.matmul(
                    pb[:, 0:n],
                    wb[:, kt * 128:(kt + 1) * 128], hb[kt][:, c0:c0 + n],
                    start=(kt == 0), stop=(kt == NKT - 1)),
                    wait=wq, inc=smm if kt == NKT - 1 else None)
            mm_t = smm.n
            p.E("v", lambda e, c0=c0, n=n, pb=pb, mm_t=mm_t: e.tensor_copy(
                ph[:, c0:c0 + n], pb[:, 0:n]),
                wait=[(smm, mm_t)], inc=svx)
        pev[gidx] = svx.n
        wfree[gidx] = smm.n
        gi_here = gidx
        gidx += 1
        ev = pev[gi_here]
        if kind == "gate":
            ob = obuf[gi_here % 2]
            wob = [(svx, ev)]
            if len(ost) >= 2:
                wob.append((sst, ost[-2]))
            p.E("v", lambda e, ob=ob, ev=ev: e.tensor_copy(
                ob[:, :], ph[:, HALO:width]),
                wait=wob, inc=svx)
            ev2 = svx.n
            p.E("y", lambda e, gmt=gmt, ob=ob, ev2=ev2, out_ap=out_ap: e.dma_start(
                out=out_ap[gmt * 128:(gmt + 1) * 128, :], in_=ob[:, :]),
                wait=[(svx, ev2)], inc=sst, inck=16)
            ost.append(sst.n)
            return
        p.E("v", lambda e, ev=ev: e.tensor_tensor(
            ph[:, 0:HALO], ph[:, 0:HALO], hmask[:, 0:HALO], ALU.mult),
            wait=[(svx, ev)])
        cwc0 = conv_base + gmt * 4
        p.E("v", lambda e, cwc0=cwc0: e.tensor_scalar(
            pc[:, 0:TPC], ph[:, 0:TPC], cw[:, cwc0:cwc0 + 1], None, ALU.mult))
        for tap in range(1, 4):
            p.E("v", lambda e, cwc0=cwc0, tap=tap: e.scalar_tensor_tensor(
                pc[:, 0:TPC], ph[:, tap:TPC + tap], cw[:, cwc0 + tap:cwc0 + tap + 1],
                pc[:, 0:TPC], ALU.mult, ALU.add),
                inc=svx if tap == 3 else None)
        cv_t = svx.n
        p.E("s", lambda e, cv_t=cv_t: e.activation(pc[:, 0:TPC], pc[:, 0:TPC], AF.Silu),
            wait=[(svx, cv_t)], inc=srow)
        sil_t = srow.n
        if kind == "v":
            ob = obuf[gi_here % 2]
            wob = [(srow, sil_t)]
            if len(ost) >= 2:
                wob.append((sst, ost[-2]))
            p.E("v", lambda e, ob=ob: e.tensor_copy(
                ob[:, :], pc[:, 0:TPC]),
                wait=wob, inc=svx)
            ev2 = svx.n
            p.E("y", lambda e, gmt=gmt, ob=ob, ev2=ev2, out_ap=out_ap: e.dma_start(
                out=out_ap[gmt * 128:(gmt + 1) * 128, :], in_=ob[:, :]),
                wait=[(svx, ev2)], inc=sst, inck=16)
            ost.append(sst.n)
            return
        p.E("s", lambda e, sil_t=sil_t: e.activation(x2b[:, 0:TPC], pc[:, 0:TPC],
                                                     AF.Square), inc=srow)
        ssq_t = srow.n
        bcw = bcq if kind == "q" else bck
        ob = obuf[gi_here % 2]
        for th in range(2):
            t0 = th * 512
            wsum = [(srow, ssq_t)] + list(sq_free)
            p.E("t", lambda e, t0=t0: e.matmul(
                ps[4][0:2, 0:512], sel48[:, 0:2], x2b[:, t0:t0 + 512],
                start=True, stop=True),
                wait=wsum, inc=smm)
            smt = smm.n
            p.E("s", lambda e, t0=t0, smt=smt: e.activation(
                rsq[0:2, t0:t0 + 512], ps[4][0:2, 0:512], AF.Ln,
                bias=cmisc[0:2, 2:3]), wait=[(smm, smt)], inc=sev)
            p.E("s", lambda e, t0=t0: e.activation(
                rsq[0:2, t0:t0 + 512], rsq[0:2, t0:t0 + 512], AF.Exp,
                scale=-0.5), inc=sev)
            rq_t = sev.n
            p.E("v", lambda e, v=rq_t: e.engine_nop(), wait=[(sev, rq_t)], inc=svx)
            rr_t = svx.n
            p.E("t", lambda e, t0=t0, rr_t=rr_t, bcw=bcw: e.matmul(
                ps[5][:, 0:512], bcw[0:2, :], rsq[0:2, t0:t0 + 512],
                start=True, stop=True),
                wait=[(svx, rr_t)], inc=smm)
            bmt = smm.n
            wob = [(smm, bmt)]
            if th == 0 and len(ost) >= 2:
                wob.append((sst, ost[-2]))
            p.E("v", lambda e, t0=t0, ob=ob, bmt=bmt: e.tensor_tensor(
                ob[:, t0:t0 + 512], pc[:, t0:t0 + 512], ps[5][:, 0:512], ALU.mult),
                wait=wob, inc=svx)
            sq_free.clear()
            sq_free.append((svx, svx.n))
        ev2 = svx.n
        p.E("y", lambda e, gmt=gmt, ob=ob, ev2=ev2, out_ap=out_ap: e.dma_start(
            out=out_ap[gmt * 128:(gmt + 1) * 128, :], in_=ob[:, :]),
            wait=[(svx, ev2)], inc=sst, inck=16)
        ost.append(sst.n)

    for gmt in range(16):
        emit_tile(wq_in, gmt, qf_o, "q", 0)
    for gmt in range(16):
        emit_tile(wk_in, gmt, kf_o, "k", 64)
    for gmt in range(24):
        emit_tile(wv_in, gmt, vf_o, "v", 128)
    for gmt in range(24):
        emit_tile(wg_in, gmt, gate_o, "gate", 0)

    ab_t = svx.n
    p.E("v", lambda e, v=ab_t: e.tensor_scalar(
        gb2[0:32, :], gb[0:32, :], adt[0:32, 1:2], None, ALU.add),
        wait=[(svx, ab_t)])
    p.E("v", lambda e: e.tensor_scalar(
        gb[0:32, :], gb2[0:32, :], -1.0, None, ALU.mult))
    p.E("v", lambda e: e.tensor_tensor(
        gb[0:32, :], gb[0:32, :], gb2[0:32, :], ALU.max))
    p.E("v", lambda e: e.tensor_scalar(
        ph[0:32, :], gb2[0:32, :], 0.0, None, ALU.max), inc=svx)
    sp_t = svx.n
    p.E("s", lambda e, sp_t=sp_t: e.activation(
        gb[0:32, :], gb[0:32, :], AF.Exp, scale=-1.0),
        wait=[(svx, sp_t)], inc=srow)
    p.E("s", lambda e: e.activation(gb[0:32, :], gb[0:32, :], AF.Ln, bias=1.0),
        inc=srow)
    p.E("s", lambda e: e.activation(adt[0:32, 2:3], adt[0:32, 0:1], AF.Exp), inc=srow)
    sp2_t = srow.n
    p.E("v", lambda e, sp2_t=sp2_t: e.tensor_scalar(
        adt[0:32, 2:3], adt[0:32, 2:3], -1.0, None, ALU.mult), wait=[(srow, sp2_t)])
    p.E("v", lambda e: e.tensor_tensor(
        gb[0:32, :], gb[0:32, :], ph[0:32, :], ALU.add))
    p.E("v", lambda e: e.tensor_scalar(
        gb[0:32, :], gb[0:32, :], adt[0:32, 2:3], None, ALU.mult), inc=svx)
    g_t = svx.n
    p.E("s", lambda e, g_t=g_t: e.activation(
        gb[32:64, :], gb[32:64, :], AF.Sigmoid), wait=[(svx, g_t)], inc=srow)
    beta_t = srow.n
    p.E("y", lambda e, g_t=g_t: e.dma_start(out=g_o[:, :], in_=gb[0:32, HALO:width]),
        wait=[(svx, g_t)], inc=sst, inck=16)
    p.E("y", lambda e, beta_t=beta_t: e.dma_start(out=b_o[:, :], in_=gb[32:64, HALO:width]),
        wait=[(srow, beta_t)], inc=sst, inck=16)
    if DBG:
        dbg_a = p.outp("dbg_a2", (32, width))
        p.E("y", lambda e, g_t=g_t: e.dma_start(out=dbg_a[:, :], in_=gb2[0:32, :]),
            wait=[(svx, g_t)], inc=sst, inck=16)
    if DBG:
        dbg_h = p.outp("dbg_h", (D, width), BF16)
        for kt in range(NKT):
            p.E("y", lambda e, kt=kt: e.dma_start(
                out=dbg_h[kt * 128:(kt + 1) * 128, :], in_=hb[kt][:]),
                wait=[(svx, h_done)], inc=sst, inck=16)
    stf = sst.n
    p.q["y"].append(lambda e: e.wait_ge(sst.h, stf))
    return p


def attn_pre_weights(inputs):
    Wq = np.asarray(inputs["L0_attn_Wq"], np.float32)
    Wk = np.asarray(inputs["L0_attn_Wk"], np.float32)
    Wv = np.asarray(inputs["L0_attn_Wv"], np.float32)
    Wg = np.asarray(inputs["L0_attn_Wg"], np.float32)
    Wa = np.asarray(inputs["L0_attn_Wa"], np.float32)
    Wb = np.asarray(inputs["L0_attn_Wb"], np.float32)
    cq = np.asarray(inputs["L0_attn_conv_q_w"], np.float32)
    ck = np.asarray(inputs["L0_attn_conv_k_w"], np.float32)
    cv = np.asarray(inputs["L0_attn_conv_v_w"], np.float32)

    def padqk(Wx):
        o = np.zeros((D, 2048), np.float32)
        o3 = o.reshape(D, 32, 64)
        o3[:, :, 0:48] = Wx.reshape(D, 32, 48)
        return np.ascontiguousarray(o)

    def padconv(c):
        o = np.zeros((2048, 4), np.float32)
        o.reshape(32, 64, 4)[:, 0:48, :] = c.reshape(32, 48, 4)
        return o

    def coltile(v, nt):
        return np.ascontiguousarray(v.reshape(nt, 128, -1).transpose(1, 0, 2).reshape(128, -1))

    adt = np.zeros((128, 4), np.float32)
    adt[0:32, 0] = np.asarray(inputs["L0_attn_A_log"], np.float32)
    adt[0:32, 1] = np.asarray(inputs["L0_attn_dt_bias"], np.float32)
    sel48 = np.zeros((128, 2), np.float32)
    sel48[0:48, 0] = 1.0
    sel48[64:112, 1] = 1.0
    bcq = np.zeros((2, 128), np.float32)
    bcq[0, 0:48] = 48.0 ** -0.5
    bcq[1, 64:112] = 48.0 ** -0.5
    bck = np.zeros((2, 128), np.float32)
    bck[0, 0:48] = 1.0
    bck[1, 64:112] = 1.0
    return {
        "WqP": padqk(Wq), "WkP": padqk(Wk),
        "Wv": np.ascontiguousarray(Wv), "Wg": np.ascontiguousarray(Wg),
        "Wab": np.ascontiguousarray(np.concatenate([Wa, Wb], 1)),
        "convQ": coltile(padconv(cq), 16), "convK": coltile(padconv(ck), 16),
        "convV": coltile(cv, 24),
        "rms1P": coltile(np.asarray(inputs["L0_attn_rms1_w"], np.float32), NKT),
        "adt": adt, "cSEL48": sel48, "cBCQ": bcq, "cBCK": bck,
    }



def build_delta(NH=H, p=None, io=None, prefix=""):
    if p is None:
        p = P("delta")
    p.stage(prefix)
    NC = 8
    MBIG = -1.0e5

    qf_in = p.io_inp(io, "qf", (NH * 64, TPC), BF16)
    kf_in = p.io_inp(io, "kf", (NH * 64, TPC), BF16)
    vf_in = p.io_inp(io, "vf", (NH * 96, TPC), BF16)
    g_in = p.io_inp(io, "gout", (NH, TPC))
    b_in = p.io_inp(io, "bout", (NH, TPC))
    rst_in = p.inp("cRST", (NH, TPC))
    msel_in = p.inp("mSELD", (48, 8))
    mstr_in = p.inp("cMSTR", (128, 128))
    mstrT_in = p.inp("cMSTRT", (128, 128))
    minct_in = p.inp("cMINCT", (128, 128))
    idb_in = p.inp("cIDB", (128, 128), BF16)
    idf_in = p.inp("cIDF", (128, 128))
    onesr_in = p.inp("cONESR", (1, 128))
    cmisc_in = p.inp("cMISC", (128, 8))
    o_out = p.io_out(io, "oout", (TPC, NH * 96), BF16)

    oloc_d = p.dram("olocD", (TPC, NH * 96), BF16)
    gq_d = p.dram("gqD", (48, NH * NC * 128), BF16)
    tb_d = p.dram("tbD", (48, NH * NC * 144))
    cs_d = p.dram("csD", (NH, TPC))
    agin = p.dram("aginD", (48, NH * 144))
    agout = p.dram("agoutD", (48 * NCORES, NH * 144))

    qh = [p.sbuf(f"qh{i}", (48, TPC), BF16) for i in range(2)]
    kh = [p.sbuf(f"kh{i}", (48, TPC), BF16) for i in range(2)]
    vh = [p.sbuf(f"vh{i}", (96, TPC), BF16) for i in range(2)]
    gsb = p.sbuf("gsb", (NH, TPC))
    bsb = p.sbuf("bsb", (NH, TPC))
    rstm = p.sbuf("rstm", (NH, TPC))
    cssb = p.sbuf("cssb", (NH, TPC))
    csst = [p.sbuf(f"csst{i}", (1, TPC)) for i in range(2)]
    negcs = [p.sbuf(f"negcs{i}", (1, TPC)) for i in range(2)]
    bst = [p.sbuf(f"bst{i}", (1, TPC)) for i in range(2)]
    rowscr = p.sbuf("rowscr", (1, 128))
    dstf = p.sbuf("dstf", (128, 128))
    dstTf = p.sbuf("dstTf", (128, 128))
    dincTf = p.sbuf("dincTf", (128, 128))
    lb = p.sbuf("lb", (128, 128), BF16)
    pbuf = [p.sbuf(f"pbuf{i}", (128, 128), BF16) for i in range(2)]
    ptall = p.sbuf("ptall", (128, 7 * 128), BF16)
    aqt = p.sbuf("aqt", (128, 128), BF16)
    ybuf = p.sbuf("ybuf", (128, 144), BF16)
    ktil = p.sbuf("ktil", (48, 128), BF16)
    ksb = p.sbuf("ksb", (128, 48), BF16)
    bbc = p.sbuf("bbc", (48, 128))
    gtmp = p.sbuf("gtmp", (48, 128))
    kv32 = p.sbuf("kv32", (96, 256))
    scr = p.sbuf("scr", (128, 8))
    scr48 = p.sbuf("scr48", (48, 48))
    olocb = [p.sbuf(f"olocb{i}", (128, 96), BF16) for i in range(2)]
    gqb = [p.sbuf(f"gqb{i}", (48, 128), BF16) for i in range(2)]
    tbb = [p.sbuf(f"tbb{i}", (48, 144)) for i in range(2)]
    tcrun = p.sbuf("tcrun", (48, 48))
    bcrun = p.sbuf("bcrun", (48, 96))
    sumb = [p.sbuf(f"sumb{i}", (48, 144)) for i in range(2)]
    sall = p.sbuf("sall", (48, NH * 96))
    sbf = p.sbuf("sbf", (48, 96), BF16)
    agld = [p.sbuf(f"agld{i}", (48, 144)) for i in range(2)]
    gql = [p.sbuf(f"gql{i}", (48, 128), BF16) for i in range(2)]
    tbl = [p.sbuf(f"tbl{i}", (48, 144)) for i in range(2)]
    olb = [p.sbuf(f"olb{i}", (128, 96), BF16) for i in range(2)]
    oob = [p.sbuf(f"oob{i}", (128, 96), BF16) for i in range(2)]
    msel = p.sbuf("msel_sb", (48, 8))
    mstr = p.sbuf("mstr_sb", (128, 128))
    mstrT = p.sbuf("mstrT_sb", (128, 128))
    minct = p.sbuf("minct_sb", (128, 128))
    idb = p.sbuf("idb_sb", (128, 128), BF16)
    idf = p.sbuf("idf_sb", (128, 128))
    ones_r = p.sbuf("onesr_sb", (1, 128))
    cmisc = p.sbuf("cmisc_sb", (128, 8))

    ps = [p.psum(f"ps{i}") for i in range(8)]

    sx = p.sem("sx")
    sqk = [p.sem("sqk0"), p.sem("sqk1")]
    srl = [p.sem("srl0"), p.sem("srl1")]
    smm = p.sem("smm")
    svx = p.sem("svx")
    srw = p.sem("srw")
    scc = p.sem("scc")
    ssp = [p.sem("ssp0"), p.sem("ssp1")]
    ssum = p.sem("ssum")
    sbl = [p.sem("sbl0"), p.sem("sbl1")]
    sst = p.sem("sst")

    one11 = cmisc[0:1, 1:2]

    for dst, src_ in ((msel, msel_in), (mstr, mstr_in), (mstrT, mstrT_in),
                      (minct, minct_in), (idb, idb_in), (idf, idf_in),
                      (ones_r, onesr_in), (cmisc, cmisc_in),
                      (gsb, g_in), (bsb, b_in), (rstm, rst_in)):
        p.E("y", lambda e, dst=dst, src_=src_: e.dma_start(out=dst[:], in_=src_[:, :]),
            inc=sx, inck=16)
    all_ld = sx.n

    p.E("v", lambda e, v=all_ld: e.tensor_tensor_scan(
        cssb[:, :], rstm[:, :], gsb[:, :], 0.0, ALU.mult, ALU.add),
        wait=[(sx, all_ld)], inc=svx)
    cs_t = svx.n
    p.E("y", lambda e, cs_t=cs_t: e.dma_start(out=cs_d[:, :], in_=cssb[:, :]),
        wait=[(svx, cs_t)], inc=sx, inck=16)
    csd_t = sx.n

    last_qkv_reader = {0: 0, 1: 0}
    last_row_reader = {0: 0, 1: 0}
    last_ps7_read = 0
    last_sp = {0: 0, 1: 0}
    dst_read = 0
    last_tp = 0
    prev_trc = 0

    for h in range(NH):
        par = h % 2
        wq = [(smm, last_qkv_reader[par])]
        p.E("y", lambda e, h=h, par=par: e.dma_start(
            out=qh[par][:, :], in_=qf_in[h * 64:h * 64 + 48, :]),
            wait=wq, inc=sqk[par], inck=16)
        p.E("y", lambda e, h=h, par=par: e.dma_start(
            out=kh[par][:, :], in_=kf_in[h * 64:h * 64 + 48, :]),
            inc=sqk[par], inck=16)
        p.E("y", lambda e, h=h, par=par: e.dma_start(
            out=vh[par][:, :], in_=vf_in[h * 96:h * 96 + 96, :]),
            inc=sqk[par], inck=16)
        qkv_t = sqk[par].n
        wr = [(smm, last_row_reader[par]), (sx, csd_t)]
        p.E("y", lambda e, h=h, par=par: e.dma_start(
            out=csst[par][:, :], in_=cs_d[h:h + 1, :]),
            wait=wr, inc=srl[par], inck=16)
        p.E("y", lambda e, h=h, par=par: e.dma_start(
            out=bst[par][:, :], in_=b_in[h:h + 1, :]),
            inc=srl[par], inck=16)
        row_t = srl[par].n
        p.E("v", lambda e, par=par, row_t=row_t: e.tensor_scalar(
            negcs[par][:, :], csst[par][:, :], -1.0, None, ALU.mult),
            wait=[(srl[par], row_t)], inc=svx)
        neg_t = svx.n
        p.E("v", lambda e, v=prev_trc: e.tensor_copy(tcrun[:, :], idf[0:48, 0:48]),
            wait=[(sx, all_ld), (smm, prev_trc)])
        p.E("v", lambda e: e.memzero(bcrun[:, :]), inc=svx)
        chain_t = svx.n

        for c in range(NC):
            c0 = c * 128
            p.E("v", lambda e, par=par, c0=c0: e.tensor_scalar(
                rowscr[0:1, :], negcs[par][0:1, c0:c0 + 128],
                csst[par][0:1, c0 + 127:c0 + 128], None, ALU.add),
                wait=[(svx, neg_t)], inc=svx)
            dec_t = svx.n
            p.E("t", lambda e, par=par, c0=c0: e.matmul(
                ps[0][:, 0:128], csst[par][0:1, c0:c0 + 128], ones_r[0:1, :],
                start=True, stop=False),
                wait=[(srl[par], row_t), (svx, dst_read), (sx, all_ld)])
            p.E("t", lambda e, par=par, c0=c0: e.matmul(
                ps[0][:, 0:128], ones_r[0:1, :], negcs[par][0:1, c0:c0 + 128],
                start=False, stop=True),
                wait=[(svx, neg_t)], inc=smm)
            od_t = smm.n
            p.E("t", lambda e, par=par, c0=c0, v=last_ps7_read: e.transpose(
                ps[7][:, 272:273], bst[par][0:1, c0:c0 + 128], one11),
                wait=[(svx, last_ps7_read)])
            p.E("t", lambda e, par=par, c0=c0: e.transpose(
                ps[7][:, 273:274], csst[par][0:1, c0:c0 + 128], one11))
            p.E("t", lambda e, dec_t=dec_t: e.transpose(
                ps[7][:, 274:275], rowscr[0:1, :], one11),
                wait=[(svx, dec_t)])
            p.E("t", lambda e, par=par, c0=c0: e.matmul(
                ps[7][0:48, 275:276], ones_r[0:1, 0:48],
                csst[par][0:1, c0 + 127:c0 + 128], start=True, stop=True))
            p.E("t", lambda e, par=par, c0=c0: e.matmul(
                ps[7][0:48, 144:272], ones_r[0:1, 0:48],
                bst[par][0:1, c0:c0 + 128], start=True, stop=True))
            p.E("t", lambda e, par=par, c0=c0: e.matmul(
                ps[7][0:48, 276:404], ones_r[0:1, 0:48],
                csst[par][0:1, c0:c0 + 128], start=True, stop=True))
            p.E("v", lambda e, par=par, c0=c0, qkv_t=qkv_t, v=last_tp: e.tensor_copy(
                kv32[0:48, 0:128], kh[par][:, c0:c0 + 128]),
                wait=[(sqk[par], qkv_t), (smm, last_tp)])
            p.E("v", lambda e, par=par, c0=c0: e.tensor_copy(
                kv32[0:96, 128:256], vh[par][:, c0:c0 + 128]), inc=svx)
            kv32_t = svx.n
            p.E("t", lambda e, kv32_t=kv32_t: e.transpose(
                ps[7][:, 0:48], kv32[0:48, 0:128], idf[0:48, 0:48]),
                wait=[(svx, kv32_t)])
            p.E("t", lambda e: e.transpose(
                ps[7][:, 48:144], kv32[0:96, 128:256], idf[0:96, 0:96]),
                inc=smm)
            tp_t = smm.n
            last_tp = tp_t
            p.E("s", lambda e, tp_t=tp_t: e.activation(
                scr[:, 0:1], ps[7][:, 272:273], AF.Identity),
                wait=[(smm, tp_t)])
            p.E("s", lambda e: e.activation(scr[:, 1:2], ps[7][:, 273:274], AF.Exp))
            p.E("s", lambda e: e.activation(scr[:, 2:3], ps[7][:, 274:275], AF.Exp))
            p.E("s", lambda e: e.activation(
                scr[0:48, 4:5], ps[7][0:48, 275:276], AF.Exp))
            p.E("s", lambda e: e.activation(
                bbc[0:48, :], ps[7][0:48, 276:404], AF.Exp), inc=srw)
            ev7_t = srw.n
            p.E("v", lambda e, ev7_t=ev7_t: e.tensor_tensor(
                scr[:, 3:4], scr[:, 0:1], scr[:, 1:2], ALU.mult),
                wait=[(srw, ev7_t)])
            p.E("v", lambda e, par=par, c0=c0: e.tensor_tensor(
                ktil[0:48, :], ps[7][0:48, 144:272], kh[par][0:48, c0:c0 + 128],
                ALU.mult))
            p.E("v", lambda e: e.tensor_scalar(
                ybuf[:, 0:96], ps[7][:, 48:144], scr[:, 0:1], None, ALU.mult))
            p.E("v", lambda e: e.tensor_scalar(
                ybuf[:, 96:144], ps[7][:, 0:48], scr[:, 3:4], None, ALU.mult))
            p.E("v", lambda e: e.tensor_scalar(
                ksb[:, 0:48], ps[7][:, 0:48], scr[:, 2:3], None, ALU.mult),
                inc=svx)
            last_ps7_read = svx.n
            ktil_t = last_ps7_read
            p.E("t", lambda e, par=par, c0=c0, ktil_t=ktil_t: e.matmul(
                ps[1][:, 0:128], ktil[0:48, :], kh[par][0:48, c0:c0 + 128],
                start=True, stop=True),
                wait=[(svx, ktil_t)])
            p.E("t", lambda e, par=par, c0=c0: e.matmul(
                ps[2][:, 0:128], kh[par][0:48, c0:c0 + 128], ktil[0:48, :],
                start=True, stop=True))
            p.E("t", lambda e, par=par, c0=c0: e.matmul(
                ps[3][:, 0:128], kh[par][0:48, c0:c0 + 128],
                qh[par][0:48, c0:c0 + 128], start=True, stop=True), inc=smm)
            gram_t = smm.n
            p.E("v", lambda e, od_t=od_t: e.tensor_tensor(
                dstf[:, :], ps[0][:, 0:128], mstr[:, :], ALU.add),
                wait=[(smm, od_t)])
            p.E("v", lambda e: e.scalar_tensor_tensor(
                dstTf[:, :], ps[0][:, 0:128], -1.0, mstrT[:, :],
                ALU.mult, ALU.add))
            p.E("v", lambda e: e.scalar_tensor_tensor(
                dincTf[:, :], ps[0][:, 0:128], -1.0, minct[:, :],
                ALU.mult, ALU.add), inc=svx)
            mk_t = svx.n
            p.E("s", lambda e, mk_t=mk_t: e.activation(
                dstf[:, :], dstf[:, :], AF.Exp), wait=[(svx, mk_t)])
            p.E("s", lambda e: e.activation(dstTf[:, :], dstTf[:, :], AF.Exp))
            p.E("s", lambda e: e.activation(dincTf[:, :], dincTf[:, :], AF.Exp),
                inc=srw)
            exp_t = srw.n
            p.E("v", lambda e, exp_t=exp_t, gram_t=gram_t: e.tensor_tensor(
                lb[:, :], ps[1][:, 0:128], dstf[:, :], ALU.mult),
                wait=[(srw, exp_t), (smm, gram_t)])
            p.E("v", lambda e: e.tensor_tensor(
                ptall[:, 0:128], ps[2][:, 0:128], dstTf[:, :], ALU.mult))
            p.E("v", lambda e: e.tensor_tensor(
                aqt[:, :], ps[3][:, 0:128], dincTf[:, :], ALU.mult), inc=svx)
            dst_read = svx.n
            laq_t = dst_read
            def ap_mm(j, y_t, extra=()):
                p.E("t", lambda e, j=j, y_t=y_t: e.matmul(
                    ps[6][:, 0:144], ptall[:, j * 128:(j + 1) * 128],
                    ybuf[:, :], start=True, stop=True),
                    wait=[(svx, y_t)] + list(extra), inc=smm)
                ap_t = smm.n
                if j == 0:
                    p.E("v", lambda e, ap_t=ap_t: e.scalar_tensor_tensor(
                        ybuf[:, :], ps[6][:, 0:144], -1.0, ybuf[:, :],
                        ALU.mult, ALU.add), wait=[(smm, ap_t)], inc=svx)
                else:
                    p.E("v", lambda e, ap_t=ap_t: e.tensor_tensor(
                        ybuf[:, :], ps[6][:, 0:144], ybuf[:, :], ALU.add),
                        wait=[(smm, ap_t)], inc=svx)
                return svx.n

            y_t = last_ps7_read
            y_t = ap_mm(0, y_t, extra=[(svx, laq_t)])
            pcur = lb
            sq_done = laq_t
            for lev in range(1, 7):
                p.E("t", lambda e, lev=lev, pcur=pcur, sq_done=sq_done: e.matmul(
                    ps[4][:, 0:128], ptall[:, (lev - 1) * 128:lev * 128],
                    pcur[:, :], start=True, stop=True),
                    wait=[(svx, sq_done), (srw, srw.n)])
                p.E("t", lambda e, lev=lev, pcur=pcur: e.matmul(
                    ps[5][:, 0:128], pcur[:, :],
                    ptall[:, (lev - 1) * 128:lev * 128], start=True, stop=True),
                    inc=smm)
                sqm_t = smm.n
                pnext = pbuf[lev % 2]
                p.E("s", lambda e, pnext=pnext, sqm_t=sqm_t: e.activation(
                    pnext[:, :], ps[4][:, 0:128], AF.Identity),
                    wait=[(smm, sqm_t)])
                p.E("s", lambda e, lev=lev: e.activation(
                    ptall[:, lev * 128:(lev + 1) * 128], ps[5][:, 0:128],
                    AF.Identity), inc=srw)
                sq_done = 0
                pcur = pnext
                y_t = ap_mm(lev, y_t, extra=[(srw, srw.n)])
            u_t = y_t
            p.E("t", lambda e, u_t=u_t: e.matmul(
                ps[0][:, 0:96], aqt[:, :], ybuf[:, 0:96],
                start=True, stop=True), wait=[(svx, u_t)])
            p.E("t", lambda e: e.matmul(
                ps[1][0:48, 0:128], ybuf[:, 96:144], aqt[:, :],
                start=True, stop=True))
            p.E("t", lambda e: e.matmul(
                ps[2][0:48, 0:96], ksb[:, 0:48], ybuf[:, 0:96],
                start=True, stop=True))
            p.E("t", lambda e: e.matmul(
                ps[3][0:48, 0:48], ybuf[:, 96:144], ksb[:, 0:48],
                start=True, stop=True), inc=smm)
            om_t = smm.n
            par2 = c % 2
            wsp = [(ssp[par2], last_sp[par2])]
            p.E("s", lambda e, par2=par2, om_t=om_t, wsp=wsp: e.activation(
                olocb[par2][:, :], ps[0][:, 0:96], AF.Identity),
                wait=[(smm, om_t)] + wsp, inc=srw)
            oloc_t = srw.n
            p.E("v", lambda e, par=par, c0=c0, wsp=wsp, om_t=om_t: e.tensor_tensor(
                gtmp[0:48, :], bbc[0:48, :], qh[par][0:48, c0:c0 + 128],
                ALU.mult), wait=wsp + [(smm, om_t)])
            p.E("v", lambda e, par2=par2: e.tensor_tensor(
                gqb[par2][0:48, :], gtmp[0:48, :], ps[1][0:48, 0:128],
                ALU.subtract))
            p.E("v", lambda e: e.tensor_scalar(
                scr48[0:48, 0:48], idf[0:48, 0:48], scr[0:48, 4:5], None,
                ALU.mult))
            p.E("v", lambda e, par2=par2: e.scalar_tensor_tensor(
                tbb[par2][:, 0:48], ps[3][0:48, 0:48], -1.0, scr48[0:48, 0:48],
                ALU.mult, ALU.add))
            p.E("v", lambda e, par2=par2: e.tensor_copy(
                tbb[par2][:, 48:144], ps[2][0:48, 0:96]), inc=svx)
            ev_t = svx.n
            p.E("y", lambda e, h=h, c=c, par2=par2, oloc_t=oloc_t: e.dma_start(
                out=oloc_d[c * 128:(c + 1) * 128, h * 96:(h + 1) * 96],
                in_=olocb[par2][:, :]),
                wait=[(srw, oloc_t)], inc=ssp[par2], inck=16)
            p.E("y", lambda e, h=h, c=c, par2=par2, ev_t=ev_t: e.dma_start(
                out=gq_d[:, (h * NC + c) * 128:(h * NC + c + 1) * 128],
                in_=gqb[par2][0:48, :]),
                wait=[(svx, ev_t)], inc=ssp[par2], inck=16)
            p.E("y", lambda e, h=h, c=c, par2=par2: e.dma_start(
                out=tb_d[:, (h * NC + c) * 144:(h * NC + c + 1) * 144],
                in_=tbb[par2][:, :]),
                inc=ssp[par2], inck=16)
            last_sp[par2] = ssp[par2].n
            p.E("t", lambda e, par2=par2, ev_t=ev_t: e.matmul(
                ps[4][0:48, 0:48], tbb[par2][:, 0:48], tcrun[0:48, 0:48],
                start=True, stop=True),
                wait=[(svx, ev_t if c > 0 else max(ev_t, chain_t))])
            p.E("t", lambda e, par2=par2: e.matmul(
                ps[5][0:48, 0:96], tbb[par2][:, 0:48], bcrun[0:48, 0:96],
                start=True, stop=True), inc=smm)
            ch_t = smm.n
            p.E("v", lambda e, ch_t=ch_t: e.tensor_copy(
                tcrun[0:48, 0:48], ps[4][0:48, 0:48]), wait=[(smm, ch_t)])
            p.E("v", lambda e, par2=par2: e.tensor_tensor(
                bcrun[0:48, 0:96], ps[5][0:48, 0:96], tbb[par2][:, 48:144],
                ALU.add), inc=svx)
            chain_t = svx.n
        last_qkv_reader[par] = smm.n
        last_row_reader[par] = smm.n
        hpar = h % 2
        p.E("t", lambda e, chain_t=chain_t: e.transpose(
            ps[4][0:48, 0:48], tcrun[0:48, 0:48], idf[0:48, 0:48]),
            wait=[(svx, chain_t)], inc=smm)
        trc_t = smm.n
        prev_trc = trc_t
        p.E("v", lambda e, hpar=hpar, trc_t=trc_t, v=ssum.n: e.tensor_copy(
            sumb[hpar][:, 0:48], ps[4][0:48, 0:48]),
            wait=[(smm, trc_t), (ssum, ssum.n if h >= 2 else 0)])
        p.E("v", lambda e, hpar=hpar: e.tensor_copy(
            sumb[hpar][:, 48:144], bcrun[0:48, 0:96]), inc=svx)
        sum_t = svx.n
        p.E("y", lambda e, h=h, hpar=hpar, sum_t=sum_t: e.dma_start(
            out=agin[:, h * 144:(h + 1) * 144], in_=sumb[hpar][:, :]),
            wait=[(svx, sum_t)], inc=ssum, inck=16)

    ag_need = ssum.n
    p.E("g", lambda e, ag_need=ag_need: e.collective_compute(
        "AllGather", ALU.bypass, replica_groups=[list(range(NCORES))],
        ins=[agin[:, :].opt()], outs=[agout[:, :].opt()]),
        wait=[(ssum, ag_need)], inc=scc)
    ag_t = scc.n

    if DBG:
        dbg_cp = p.sbuf("dbg_cp", (48, 8 * 96))
    last_ag_reader = {0: 0, 1: 0}
    for h in range(NH):
        p.E("v", lambda e, h=h: e.memzero(sall[:, h * 96:(h + 1) * 96]))
        for r in range(8):
            pr = r % 2
            p.E("y", lambda e, h=h, r=r, pr=pr, ag_t=ag_t,
                 v=last_ag_reader[pr]: e.dma_start(
                out=agld[pr][:, :],
                in_=agout[48 * r:48 * (r + 1), h * 144:(h + 1) * 144]),
                wait=[(scc, ag_t), (svx, last_ag_reader[pr])],
                inc=sbl[pr], inck=16)
            ld_t = sbl[pr].n
            if r % 4 == 0:
                p.E("v", lambda e, pr=pr, ld_t=ld_t: e.tensor_copy(
                    bbc[0:48, 0:96], agld[pr][:, 48:144]),
                    wait=[(sbl[pr], ld_t)], inc=svx)
            else:
                p.E("t", lambda e, pr=pr, ld_t=ld_t, v=svx.n: e.matmul(
                    ps[2][0:48, 0:96], agld[pr][:, 0:48], bbc[0:48, 0:96],
                    start=True, stop=True),
                    wait=[(sbl[pr], ld_t), (svx, svx.n)], inc=smm)
                cm_t = smm.n
                p.E("v", lambda e, pr=pr, cm_t=cm_t: e.tensor_tensor(
                    bbc[0:48, 0:96], ps[2][0:48, 0:96], agld[pr][:, 48:144],
                    ALU.add), wait=[(smm, cm_t)], inc=svx)
            last_ag_reader[pr] = svx.n
            if DBG and h == 0:
                p.E("v", lambda e, r=r: e.tensor_copy(
                    dbg_cp[:, r * 96:(r + 1) * 96], bbc[0:48, 0:96]))
            p.E("v", lambda e, h=h, r=r: e.scalar_tensor_tensor(
                sall[:, h * 96:(h + 1) * 96], bbc[0:48, 0:96], msel[:, r:r + 1],
                sall[:, h * 96:(h + 1) * 96], ALU.mult, ALU.add), inc=svx)
            last_ag_reader[pr] = svx.n
    comp_t = svx.n
    if DBG:
        dbg_cpo = p.outp("dbg_cpo", (48, 8 * 96))
        p.E("y", lambda e, v=comp_t: e.dma_start(out=dbg_cpo[:, :], in_=dbg_cp[:, :]),
            wait=[(svx, comp_t)], inc=sst, inck=16)

    last_bl = {0: 0, 1: 0}
    prev_bm = 0
    sp_drain = [(ssp[0], last_sp[0]), (ssp[1], last_sp[1])]
    if DBG:
        dbg_sin = p.outp("dbg_sin", (48, NH * 96))
        sdbg = p.sem("sdbg")
        p.E("y", lambda e, v=comp_t: e.dma_start(out=dbg_sin[:, :], in_=sall[:, :]),
            wait=[(svx, comp_t)], inc=sdbg, inck=16)
        sp_drain = sp_drain + [(sdbg, sdbg.n)]
    for h in range(NH):
        for c in range(NC):
            pb = (h * NC + c) % 2
            wld = [(sbl[pb], last_bl[pb]), (svx, comp_t)]
            if h == 0 and c == 0:
                wld = wld + sp_drain
            p.E("y", lambda e, h=h, c=c, pb=pb, wld=wld: e.dma_start(
                out=gql[pb][0:48, :],
                in_=gq_d[:, (h * NC + c) * 128:(h * NC + c + 1) * 128]),
                wait=wld, inc=sbl[pb], inck=16)
            p.E("y", lambda e, h=h, c=c, pb=pb: e.dma_start(
                out=tbl[pb][:, :],
                in_=tb_d[:, (h * NC + c) * 144:(h * NC + c + 1) * 144]),
                inc=sbl[pb], inck=16)
            p.E("y", lambda e, h=h, c=c, pb=pb: e.dma_start(
                out=olb[pb][:, :],
                in_=oloc_d[c * 128:(c + 1) * 128, h * 96:(h + 1) * 96]),
                inc=sbl[pb], inck=16)
            ld_t = sbl[pb].n
            p.E("v", lambda e, h=h, v=svx.n, w=prev_bm: e.tensor_copy(
                sbf[:, :], sall[:, h * 96:(h + 1) * 96]),
                wait=[(svx, svx.n), (smm, prev_bm)], inc=svx)
            sbf_t = svx.n
            p.E("t", lambda e, pb=pb, ld_t=ld_t, sbf_t=sbf_t: e.matmul(
                ps[0][:, 0:96], gql[pb][0:48, :], sbf[:, :],
                start=True, stop=True),
                wait=[(sbl[pb], ld_t), (svx, sbf_t)])
            p.E("t", lambda e, h=h, pb=pb: e.matmul(
                ps[1][0:48, 0:96], tbl[pb][:, 0:48], sall[:, h * 96:(h + 1) * 96],
                start=True, stop=True), inc=smm)
            bm_t = smm.n
            prev_bm = bm_t
            p.E("v", lambda e, pb=pb, bm_t=bm_t, v=sst.n: e.tensor_tensor(
                oob[pb][:, :], ps[0][:, 0:96], olb[pb][:, :], ALU.add),
                wait=[(smm, bm_t), (sst, sst.n if (h * NC + c) >= 2 else 0)])
            p.E("v", lambda e, h=h, pb=pb: e.tensor_tensor(
                sall[:, h * 96:(h + 1) * 96], ps[1][0:48, 0:96],
                tbl[pb][:, 48:144], ALU.add), inc=svx)
            st_t = svx.n
            p.E("y", lambda e, h=h, c=c, pb=pb, st_t=st_t: e.dma_start(
                out=o_out[c * 128:(c + 1) * 128, h * 96:(h + 1) * 96],
                in_=oob[pb][:, :]),
                wait=[(svx, st_t)], inc=sst, inck=16)
            last_bl[pb] = sbl[pb].n
    fin = sst.n
    p.q["y"].append(lambda e: e.wait_ge(sst.h, fin))
    return p


def _delta_consts(NH=H):
    j = np.arange(128)
    mstr = np.where(j[:, None] > j[None, :], 0.0, -1.0e5).astype(np.float32)
    mstrT = np.where(j[:, None] < j[None, :], 0.0, -1.0e5).astype(np.float32)
    minct = np.where(j[:, None] <= j[None, :], 0.0, -1.0e5).astype(np.float32)
    rst = np.ones((NH, TPC), np.float32)
    rst[:, 0::128] = 0.0
    return {
        "cMSTR": mstr, "cMSTRT": mstrT, "cMINCT": minct,
        "cIDB": np.eye(128).astype(BF),
        "cRST": rst,
    }


def _delta_msel(c):
    b, k = c // 4, c % 4
    m = np.zeros((48, 8), np.float32)
    if k > 0:
        m[:, 4 * b + k - 1] = 1.0
    return m



def build_attn_post(p=None, io=None, prefix="", xc0=0):
    if p is None:
        p = P("attnpost")
    p.stage(prefix)
    NT = TPC
    NIT = 5632 // 128

    x_in = p.io_inp(io, "x", (D, NT + xc0))
    o_in = p.io_inp(io, "oD", (NT, 3072), BF16)
    gate_in = p.io_inp(io, "gateF", (3072, NT), BF16)
    gnw_in = p.inp("gnwB", (128, 3072))
    wo_in = p.inp("Wo", (3072, D))
    rms2_in = p.inp("rms2P", (128, NKT))
    w1_in = p.inp("W1", (D, 5632))
    w3_in = p.inp("W3", (D, 5632))
    w2_in = p.inp("W2", (5632, D))
    ones_in = p.inp("cONESC", (128, 1))
    onesr_in = p.inp("cONESR", (1, 128))
    cmisc_in = p.inp("cMISC", (128, 8))
    idf_in = p.inp("cIDF", (128, 128))
    xo_out = p.io_out(io, "xo", (D, NT))

    ogfm_d = p.dram("ogfmD", (3072, NT), BF16)

    xtr = [p.sbuf(f"xtr{i}", (128, NT)) for i in range(2)]
    x2b = p.sbuf("x2b", (128, NT))
    rows = p.sbuf("rows", (1, 2100))
    xob = p.sbuf("xob", (128, NKT * NT))
    ob = [p.sbuf(f"ob{i}", (128, 3072), BF16) for i in range(2)]
    ogb = p.sbuf("ogb", (128, 3072))
    osq = ogb
    ssb = p.sbuf("ssb", (128, 32))
    gnw = p.sbuf("gnw_sb", (128, 3072))
    gblk = [p.sbuf(f"gblk{i}", (128, 128), BF16) for i in range(2)]
    ogsb = [p.sbuf(f"ogsb{i}", (128, 128), BF16) for i in range(2)]
    wob = p.sbuf("wob", (128, 4 * D), BF16)
    ogt = p.sbuf("ogt", (128, 4 * NT), BF16)
    ffb = p.sbuf("ffb", (128, NKT * NT), BF16)
    w13 = [p.sbuf(f"w13_{i}", (128, 2 * NKT * 128), BF16) for i in range(2)]
    w2t = [p.sbuf(f"w2t{i}", (128, D), BF16) for i in range(2)]
    actb = p.sbuf("actb", (128, NT), BF16)
    tmpv = p.sbuf("tmpv", (128, 512))
    ones_c = p.sbuf("onesc_sb", (128, 1))
    ones_r = p.sbuf("onesr_sb", (1, 128))
    cmisc = p.sbuf("cmisc_sb", (128, 8))
    idf = p.sbuf("idf_sb", (128, 128))
    rms2 = p.sbuf("rms2_sb", (128, NKT))

    ps = [p.psum(f"ps{i}") for i in range(8)]

    sx = p.sem("sx")
    sxa = [p.sem("sxa0"), p.sem("sxa1")]
    sol = [p.sem("sol0"), p.sem("sol1")]
    sgb = [p.sem("sgb0"), p.sem("sgb1")]
    sog = [p.sem("sog0"), p.sem("sog1")]
    swl = p.sem("swl")
    sgl = p.sem("sgl")
    smm = p.sem("smm")
    svx = p.sem("svx")
    srw = p.sem("srw")
    sst = p.sem("sst")

    for dst, src_ in ((ones_c, ones_in), (ones_r, onesr_in), (cmisc, cmisc_in),
                      (idf, idf_in), (gnw, gnw_in), (rms2, rms2_in)):
        p.E("y", lambda e, dst=dst, src_=src_: e.dma_start(out=dst[:], in_=src_[:, :]),
            inc=sx, inck=16)
    all_ld = sx.n

    RSTD, R2 = 0, 1050
    cg2 = [(0, 512), (512, 1024)]

    xa_t = {}
    for kt in range(NKT):
        xb = xtr[kt % 2]
        sxp = sxa[kt % 2]
        w = [(smm, xa_t[kt - 2])] if kt >= 2 else []
        p.E("y", lambda e, kt=kt, xb=xb: e.dma_start(
            out=xb[:], in_=x_in[kt * 128:(kt + 1) * 128, xc0:xc0 + NT]),
            wait=w, inc=sxp, inck=16)
        ld_t = sxp.n
        wsq = [(sxp, ld_t)]
        if kt >= 1:
            wsq.append((smm, xa_t[kt - 1]))
        p.E("s", lambda e, xb=xb: e.activation(x2b[:], xb[:], AF.Square),
            wait=wsq, inc=srw)
        sq_t = srw.n
        for cg, (c0, c1) in enumerate(cg2):
            p.E("t", lambda e, kt=kt, c0=c0, c1=c1, xb=xb, ld_t=ld_t, sxp=sxp, cg=cg: e.matmul(
                ps[cg][0:1, 0:c1 - c0], ones_c[:, 0:1], xb[:, c0:c1],
                start=(kt == 0), stop=(kt == NKT - 1)),
                wait=[(sxp, ld_t), (sx, all_ld)] if cg == 0 else [])
            p.E("t", lambda e, kt=kt, c0=c0, c1=c1, sq_t=sq_t, cg=cg: e.matmul(
                ps[2 + cg][0:1, 0:c1 - c0], ones_c[:, 0:1], x2b[:, c0:c1],
                start=(kt == 0), stop=(kt == NKT - 1)),
                wait=[(srw, sq_t)], inc=smm if cg == 1 else None)
        xa_t[kt] = smm.n
    stats_done = smm.n
    for cg, (c0, c1) in enumerate(cg2):
        n = c1 - c0
        p.E("v", lambda e, c0=c0, n=n, cg=cg: e.tensor_scalar(
            rows[0:1, R2 + c0:R2 + c0 + n], ps[cg][0:1, 0:n], -1.0 / D, None,
            ALU.mult), wait=[(smm, stats_done)] if cg == 0 else [])
        p.E("v", lambda e, c0=c0, n=n, cg=cg: e.tensor_scalar(
            rows[0:1, RSTD + c0:RSTD + c0 + n], ps[2 + cg][0:1, 0:n], 1.0 / D,
            None, ALU.mult))
        p.E("v", lambda e, c0=c0, n=n: e.tensor_tensor(
            tmpv[0:1, 0:n], rows[0:1, R2 + c0:R2 + c0 + n],
            rows[0:1, R2 + c0:R2 + c0 + n], ALU.mult))
        p.E("v", lambda e, c0=c0, n=n: e.tensor_tensor(
            rows[0:1, RSTD + c0:RSTD + c0 + n], rows[0:1, RSTD + c0:RSTD + c0 + n],
            tmpv[0:1, 0:n], ALU.subtract), inc=svx if cg == 1 else None)
    var_t = svx.n
    p.E("s", lambda e, var_t=var_t: e.activation(
        rows[0:1, RSTD:RSTD + NT], rows[0:1, RSTD:RSTD + NT], AF.Ln,
        bias=cmisc[0:1, 0:1]), wait=[(svx, var_t)], inc=srw)
    p.E("s", lambda e: e.activation(
        rows[0:1, RSTD:RSTD + NT], rows[0:1, RSTD:RSTD + NT], AF.Exp,
        scale=-0.5), inc=srw)
    rstd_t = srw.n
    p.E("v", lambda e, rstd_t=rstd_t: e.tensor_tensor(
        rows[0:1, R2:R2 + NT], rows[0:1, R2:R2 + NT],
        rows[0:1, RSTD:RSTD + NT], ALU.mult), wait=[(srw, rstd_t)], inc=svx)
    rows_done = svx.n

    for cg, (c0, c1) in enumerate(cg2):
        n = c1 - c0
        p.E("t", lambda e, c0=c0, n=n, cg=cg, v=rows_done: e.matmul(
            ps[4 + cg][:, 0:n], ones_r[0:1, :], rows[0:1, RSTD + c0:RSTD + c0 + n],
            start=True, stop=True), wait=[(svx, rows_done)] if cg == 0 else [])
        p.E("t", lambda e, c0=c0, n=n, cg=cg: e.matmul(
            ps[6 + cg][:, 0:n], ones_r[0:1, :], rows[0:1, R2 + c0:R2 + c0 + n],
            start=True, stop=True), inc=smm)
    bc_t = smm.n
    ap_t = {}
    for kt in range(NKT):
        xb = xtr[kt % 2]
        sxp = sxa[kt % 2]
        w = [(svx, ap_t[kt - 2])] if kt >= 2 else [(smm, xa_t[NKT - 1])]
        p.E("y", lambda e, kt=kt, xb=xb: e.dma_start(
            out=xb[:], in_=x_in[kt * 128:(kt + 1) * 128, xc0:xc0 + NT]),
            wait=w, inc=sxp, inck=16)
        ld_t = sxp.n
        for cg, (c0, c1) in enumerate(cg2):
            n = c1 - c0
            p.E("v", lambda e, kt=kt, c0=c0, n=n, cg=cg, xb=xb, ld_t=ld_t,
                 sxp=sxp, bc_t=bc_t: e.tensor_tensor(
                tmpv[:, 0:n], xb[:, c0:c0 + n], ps[4 + cg][:, 0:n], ALU.mult),
                wait=[(sxp, ld_t), (smm, bc_t)])
            p.E("v", lambda e, kt=kt, c0=c0, n=n, cg=cg: e.tensor_tensor(
                xob[:, kt * NT + c0:kt * NT + c0 + n], tmpv[:, 0:n],
                ps[6 + cg][:, 0:n], ALU.add), inc=svx)
        ap_t[kt] = svx.n
    ln_done = svx.n

    tr_ev = 0
    ob_rd = {0: 0, 1: 0}
    gb_rd = {0: 0, 1: 0}
    for c in range(8):
        par = c % 2
        p.E("y", lambda e, c=c, par=par, v=ob_rd[par]: e.dma_start(
            out=ob[par][:, :], in_=o_in[c * 128:(c + 1) * 128, :]),
            wait=[(svx, ob_rd[par])], inc=sol[par], inck=16)
        ol_t = sol[par].n
        p.E("s", lambda e, par=par, ol_t=ol_t: e.activation(
            osq[:, :], ob[par][:, :], AF.Square),
            wait=[(sol[par], ol_t)], inc=srw)
        osq_t = srw.n
        for h in range(H):
            p.E("v", lambda e, h=h, osq_t=osq_t: e.reduce_sum(
                ssb[:, h:h + 1], osq[:, h * 96:(h + 1) * 96],
                axis=mybir.AxisListType.X),
                wait=[(srw, osq_t)] if h == 0 else [])
        p.E("v", lambda e: e.tensor_scalar(
            ssb[:, :], ssb[:, :], 1.0 / 96.0, None, ALU.mult), inc=svx)
        ss_t = svx.n
        p.E("s", lambda e, ss_t=ss_t: e.activation(
            ssb[:, :], ssb[:, :], AF.Ln, bias=cmisc[:, 0:1]),
            wait=[(svx, ss_t)], inc=srw)
        p.E("s", lambda e: e.activation(
            ssb[:, :], ssb[:, :], AF.Exp, scale=-0.5), inc=srw)
        rs_t = srw.n
        for h in range(H):
            p.E("v", lambda e, h=h, par=par, rs_t=rs_t: e.tensor_scalar(
                ogb[:, h * 96:(h + 1) * 96], ob[par][:, h * 96:(h + 1) * 96],
                ssb[:, h:h + 1], None, ALU.mult),
                wait=[(rs_t and (srw, rs_t))] if h == 0 else [])
        p.E("v", lambda e: e.tensor_tensor(
            ogb[:, :], ogb[:, :], gnw[:, :], ALU.mult), inc=svx)
        og_t = svx.n
        ob_rd[par] = og_t
        for ft in range(24):
            fp = ft % 2
            p.E("y", lambda e, c=c, ft=ft, fp=fp, v=gb_rd[fp]: e.dma_start(
                out=gblk[fp][:, :],
                in_=gate_in[ft * 128:(ft + 1) * 128, c * 128:(c + 1) * 128]),
                wait=[(svx, gb_rd[fp])], inc=sgb[fp], inck=16)
            gl_t = sgb[fp].n
            p.E("s", lambda e, fp=fp, gl_t=gl_t: e.activation(
                gblk[fp][:, :], gblk[fp][:, :], AF.Silu),
                wait=[(sgb[fp], gl_t)], inc=srw)
            sil_t = srw.n
            pb = ps[ft % 2]
            p.E("t", lambda e, ft=ft, pb=pb, og_t=og_t, tr_ev=tr_ev: e.transpose(
                pb[:, 0:128], ogb[:, ft * 128:(ft + 1) * 128], idf[:, :]),
                wait=[(svx, og_t), (srw, tr_ev)], inc=smm)
            tr_t = smm.n
            p.E("s", lambda e, ft=ft, fp=fp, pb=pb, tr_t=tr_t, sil_t=sil_t,
                 v=sog[fp].n: e.activation(
                ogsb[fp][:, :], pb[:, 0:128], AF.Identity),
                wait=[(smm, tr_t), (srw, sil_t), (sog[fp], sog[fp].n)], inc=srw)
            p.E("v", lambda e, fp=fp, v=srw.n: e.tensor_tensor(
                ogsb[fp][:, :], ogsb[fp][:, :], gblk[fp][:, :], ALU.mult),
                wait=[(srw, srw.n)], inc=svx)
            ogs_t = svx.n
            gb_rd[fp] = ogs_t
            tr_ev = srw.n
            p.E("y", lambda e, c=c, ft=ft, fp=fp, ogs_t=ogs_t: e.dma_start(
                out=ogfm_d[ft * 128:(ft + 1) * 128, c * 128:(c + 1) * 128],
                in_=ogsb[fp][:, :]),
                wait=[(svx, ogs_t)], inc=sog[fp], inck=16)
        ogdma = max(sog[0].n, sog[1].n)
    og_done = [(sog[0], sog[0].n), (sog[1], sog[1].n)]

    evac_t = {b: ln_done for b in range(6)}
    wo_grp = 0
    for ktg in range(6):
        wld = og_done if ktg == 0 else []
        if ktg >= 1:
            wld = wld + [(svx, wo_grp)]
        p.E("g", lambda e, ktg=ktg, wld=wld: e.dma_start(
            out=wob[:, :].rearrange("p (k c) -> p k c", k=4),
            in_=wo_in[ktg * 512:(ktg + 1) * 512, :].rearrange(
                "(k p) c -> p k c", p=128)),
            wait=wld, inc=sgl, inck=16)
        p.E("y", lambda e, ktg=ktg, wld=wld: e.dma_start(
            out=ogt[:, :].rearrange("p (k c) -> p k c", k=4),
            in_=ogfm_d[ktg * 512:(ktg + 1) * 512, :].rearrange(
                "(k p) c -> p k c", p=128)),
            wait=wld, inc=swl, inck=16)
        wo_ld = (sgl.n, swl.n)
        for mt in range(NKT):
            for half in range(2):
                bank = (mt * 2 + half) % 6
                wmm = [(sgl, wo_ld[0]), (swl, wo_ld[1]),
                       (svx, evac_t[bank])]
                for k4 in range(4):
                    p.E("t", lambda e, mt=mt, half=half, k4=k4, bank=bank,
                         wmm=wmm: e.matmul(
                        ps[bank][:, 0:512],
                        wob[:, k4 * D + mt * 128:k4 * D + (mt + 1) * 128],
                        ogt[:, k4 * NT + half * 512:k4 * NT + (half + 1) * 512],
                        start=(k4 == 0), stop=(k4 == 3)),
                        wait=wmm if k4 == 0 else [],
                        inc=smm if k4 == 3 else None)
                mm_t = smm.n
                p.E("v", lambda e, mt=mt, half=half, bank=bank, mm_t=mm_t: e.tensor_tensor(
                    xob[:, mt * NT + half * 512:mt * NT + (half + 1) * 512],
                    xob[:, mt * NT + half * 512:mt * NT + (half + 1) * 512],
                    ps[bank][:, 0:512], ALU.add),
                    wait=[(smm, mm_t)], inc=svx)
                evac_t[bank] = svx.n
        wo_grp = svx.n
    h2_done = svx.n

    p4_mm = {}
    for kt in range(NKT):
        wsq = [(svx, h2_done)] if kt == 0 else [(smm, p4_mm[kt - 1])]
        p.E("s", lambda e, kt=kt, wsq=wsq: e.activation(
            x2b[:, :], xob[:, kt * NT:(kt + 1) * NT], AF.Square),
            wait=wsq, inc=srw)
        sq_t = srw.n
        for cg in range(2):
            p.E("t", lambda e, kt=kt, cg=cg, sq_t=sq_t: e.matmul(
                ps[cg][0:1, 0:512], ones_c[:, 0:1],
                x2b[:, cg * 512:(cg + 1) * 512],
                start=(kt == 0), stop=(kt == NKT - 1)),
                wait=[(srw, sq_t)], inc=smm if cg == 1 else None)
        p4_mm[kt] = smm.n
    ms_done = smm.n
    for cg in range(2):
        p.E("v", lambda e, cg=cg, v=ms_done: e.tensor_scalar(
            rows[0:1, RSTD + cg * 512:RSTD + (cg + 1) * 512],
            ps[cg][0:1, 0:512], 1.0 / D, None, ALU.mult),
            wait=[(smm, ms_done)] if cg == 0 else [], inc=svx)
    ms_t = svx.n
    p.E("s", lambda e, ms_t=ms_t: e.activation(
        rows[0:1, RSTD:RSTD + NT], rows[0:1, RSTD:RSTD + NT], AF.Ln,
        bias=cmisc[0:1, 0:1]), wait=[(svx, ms_t)], inc=srw)
    p.E("s", lambda e: e.activation(
        rows[0:1, RSTD:RSTD + NT], rows[0:1, RSTD:RSTD + NT], AF.Exp,
        scale=-0.5), inc=srw)
    rst2_t = srw.n
    for cg in range(2):
        p.E("t", lambda e, cg=cg, rst2_t=rst2_t: e.matmul(
            ps[4 + cg][:, 0:512], ones_r[0:1, :],
            rows[0:1, RSTD + cg * 512:RSTD + (cg + 1) * 512],
            start=True, stop=True),
            wait=[(srw, rst2_t)] if cg == 0 else [], inc=smm)
    bc2_t = smm.n
    for kt in range(NKT):
        for cg in range(2):
            p.E("v", lambda e, kt=kt, cg=cg, bc2_t=bc2_t: e.tensor_tensor(
                tmpv[:, 0:512], xob[:, kt * NT + cg * 512:kt * NT + (cg + 1) * 512],
                ps[4 + cg][:, 0:512], ALU.mult),
                wait=[(smm, bc2_t)] if kt == 0 and cg == 0 else [])
            p.E("v", lambda e, kt=kt, cg=cg: e.tensor_scalar(
                ffb[:, kt * NT + cg * 512:kt * NT + (cg + 1) * 512],
                tmpv[:, 0:512], rms2[:, kt:kt + 1], None, ALU.mult),
                inc=svx)
    ff_done = svx.n

    w2ev = {}
    ps45_sv = 0
    ps45_sr = 0
    last_w13 = {0: 0, 1: 0}
    last_w2 = {0: 0, 1: 0}
    act_rd = 0
    for it in range(NIT):
        wp = it % 2
        p.E("g", lambda e, it=it, wp=wp, v=last_w13[wp]: e.dma_start(
            out=w13[wp][:, 0:NKT * 128].rearrange("p (k c) -> p k c", k=NKT),
            in_=w1_in[:, it * 128:(it + 1) * 128].rearrange(
                "(k p) c -> p k c", p=128)),
            wait=[(smm, last_w13[wp])], inc=sgl, inck=16)
        p.E("g", lambda e, it=it, wp=wp: e.dma_start(
            out=w13[wp][:, NKT * 128:2 * NKT * 128].rearrange(
                "p (k c) -> p k c", k=NKT),
            in_=w3_in[:, it * 128:(it + 1) * 128].rearrange(
                "(k p) c -> p k c", p=128)),
            inc=sgl, inck=16)
        p.E("g", lambda e, it=it, wp=wp, v=last_w2[wp]: e.dma_start(
            out=w2t[wp][:, :], in_=w2_in[it * 128:(it + 1) * 128, :]),
            wait=[(smm, last_w2[wp])], inc=sgl, inck=16)
        w13_t = sgl.n
        for half in range(2):
            wmm = [(sgl, w13_t), (svx, max(ff_done, act_rd, ps45_sv)),
                   (srw, ps45_sr)]
            for kt in range(NKT):
                p.E("t", lambda e, it=it, wp=wp, kt=kt, half=half, wmm=wmm: e.matmul(
                    ps[4][:, 0:512], w13[wp][:, kt * 128:(kt + 1) * 128],
                    ffb[:, kt * NT + half * 512:kt * NT + (half + 1) * 512],
                    start=(kt == 0), stop=(kt == NKT - 1)),
                    wait=wmm if kt == 0 else [])
                p.E("t", lambda e, it=it, wp=wp, kt=kt, half=half: e.matmul(
                    ps[5][:, 0:512],
                    w13[wp][:, NKT * 128 + kt * 128:NKT * 128 + (kt + 1) * 128],
                    ffb[:, kt * NT + half * 512:kt * NT + (half + 1) * 512],
                    start=(kt == 0), stop=(kt == NKT - 1)),
                    inc=smm if kt == NKT - 1 else None)
            g13_t = smm.n
            p.E("s", lambda e, g13_t=g13_t: e.activation(
                tmpv[:, 0:512], ps[4][:, 0:512], AF.Silu),
                wait=[(smm, g13_t)], inc=srw)
            sil2_t = srw.n
            p.E("v", lambda e, half=half, sil2_t=sil2_t: e.tensor_tensor(
                actb[:, half * 512:(half + 1) * 512], tmpv[:, 0:512],
                ps[5][:, 0:512], ALU.mult),
                wait=[(srw, sil2_t)], inc=svx)
            ps45_sv = svx.n
            ps45_sr = srw.n
        act_t = svx.n
        last_w13[wp] = smm.n
        for mt in range(NKT):
            for half in range(2):
                bank = (0, 1, 2, 3, 6, 7)[(mt * 2 + half) % 6]
                wmm = [(sgl, w13_t), (svx, act_t), (svx, w2ev.get(bank, 0))]
                p.E("t", lambda e, it=it, wp=wp, mt=mt, half=half, bank=bank,
                     wmm=wmm: e.matmul(
                    ps[bank][:, 0:512], w2t[wp][:, mt * 128:(mt + 1) * 128],
                    actb[:, half * 512:(half + 1) * 512],
                    start=True, stop=True), wait=wmm, inc=smm)
                m2_t = smm.n
                p.E("v", lambda e, mt=mt, half=half, bank=bank, m2_t=m2_t: e.tensor_tensor(
                    xob[:, mt * NT + half * 512:mt * NT + (half + 1) * 512],
                    xob[:, mt * NT + half * 512:mt * NT + (half + 1) * 512],
                    ps[bank][:, 0:512], ALU.add),
                    wait=[(smm, m2_t)], inc=svx)
                w2ev[bank] = svx.n
        act_rd = svx.n
        last_w2[wp] = smm.n
    sw_done = svx.n

    for kt in range(NKT):
        xb = xtr[kt % 2]
        sxp = sxa[kt % 2]
        w = [(svx, sw_done)] if kt < 2 else [(sst, sst.n)]
        p.E("y", lambda e, kt=kt, xb=xb, w=w: e.dma_start(
            out=xb[:], in_=x_in[kt * 128:(kt + 1) * 128, xc0:xc0 + NT]),
            wait=w, inc=sxp, inck=16)
        ld_t = sxp.n
        p.E("v", lambda e, kt=kt, xb=xb, ld_t=ld_t, sxp=sxp: e.tensor_tensor(
            xob[:, kt * NT:(kt + 1) * NT], xob[:, kt * NT:(kt + 1) * NT],
            xb[:, :], ALU.add), wait=[(sxp, ld_t)], inc=svx)
        fin_t = svx.n
        p.E("y", lambda e, kt=kt, fin_t=fin_t: e.dma_start(
            out=xo_out[kt * 128:(kt + 1) * 128, :],
            in_=xob[:, kt * NT:(kt + 1) * NT]),
            wait=[(svx, fin_t)], inc=sst, inck=16)
    fin = sst.n
    p.q["y"].append(lambda e: e.wait_ge(sst.h, fin))
    return p


def attn_post_weights(inputs):
    gnw = np.asarray(inputs["L0_attn_gnorm_w"], np.float32)
    def coltile(v, nt):
        return np.ascontiguousarray(
            v.reshape(nt, 128, -1).transpose(1, 0, 2).reshape(128, -1))
    return {
        "gnwB": np.ascontiguousarray(
            np.broadcast_to(np.tile(gnw, 32)[None, :], (128, 3072)).copy()),
        "Wo": np.ascontiguousarray(np.asarray(inputs["L0_attn_Wo"], np.float32)),
        "rms2P": coltile(np.asarray(inputs["L0_attn_rms2_w"], np.float32), NKT),
        "W1": np.ascontiguousarray(np.asarray(inputs["L0_attn_W1"], np.float32)),
        "W3": np.ascontiguousarray(np.asarray(inputs["L0_attn_W3"], np.float32)),
        "W2": np.ascontiguousarray(np.asarray(inputs["L0_attn_W2"], np.float32)),
    }



def _fm_slices(x, halo=HALO):
    out = []
    xt = np.ascontiguousarray(np.asarray(x, np.float32))
    for c in range(NCORES):
        b = c // 4
        t0 = (c % 4) * TPC
        lo = t0 - halo
        sl = xt[b, max(lo, 0):t0 + TPC, :].T
        if lo < 0:
            sl = np.concatenate([np.zeros((D, -lo), np.float32), sl], 1)
        out.append(np.ascontiguousarray(sl))
    return out


def _coltiles(*vs):
    outs = []
    for v in vs:
        outs.append(np.ascontiguousarray(np.asarray(v, np.float32).reshape(NKT, 128).T))
    return np.ascontiguousarray(np.concatenate(outs, 1))


_cm = np.zeros((128, 8), np.float32)
_cm[:, 0] = LN_EPS
_cm[:, 1] = 1.0
_cm[:, 2] = 1e-6
_cm[:, 3] = 1e-38
_CONSTS = {
    "cONESC": np.ones((128, 1), np.float32),
    "cONESR": np.ones((1, 128), np.float32),
    "cMISC": _cm,
    "cEYE16": np.eye(16, dtype=np.float32),
    "cIDF": np.eye(128, dtype=np.float32),
    "cJ": np.tile(np.arange(TPC, dtype=np.float32)[None, :], (128, 1)),
    "cONESW": np.ones((128, TPC), np.float32),
}


def run_sign_ffn(x_fm_slices, wT, gamma, beta, bias, alpha, width):
    p = build_sign_ffn(width)
    nc = p.finish()
    pars = _coltiles(gamma, beta, bias, alpha)
    in_maps = []
    for c in range(NCORES):
        m = {"x": x_fm_slices[c], "wT": wT, "pars_in": pars}
        m.update(_CONSTS)
        in_maps.append(m)
    res = run_bass_kernel_spmd(nc, in_maps, core_ids=list(range(NCORES)))
    return [r["xo"] for r in res.results]


_PROG_CACHE = {}


def _prog(name, builder):
    if name not in _PROG_CACHE:
        p = builder()
        _PROG_CACHE[name] = p.finish()
    return _PROG_CACHE[name]


def build_mega(dbg=False):
    p = P("mega")
    p.init_barrier()
    mk = p.outp if dbg else p.dram
    x1_d = mk("x1D", (D, W))
    qf_d = mk("qfD", (2048, TPC), BF16)
    kf_d = mk("kfD", (2048, TPC), BF16)
    vf_d = mk("vfD", (3072, TPC), BF16)
    gate_d = mk("gateD", (3072, TPC), BF16)
    g_d = mk("gD", (H, TPC))
    b_d = mk("bD", (H, TPC))
    oo_d = mk("ooD", (TPC, 3072), BF16)
    x2_d = mk("x2D", (D, TPC))
    x3_d = mk("x3D", (D, TPC))
    x4_d = mk("x4D", (D, TPC))
    x5_d = mk("x5D", (D, TPC))
    out_ap = p.outp("out", (D, TPC))

    p.stage_marks = [("init", {k: len(v) for k, v in p.q.items()})]

    def mark(nm):
        p.stage_marks.append((nm, {k: len(v) for k, v in p.q.items()}))
    build_sign_ffn(W, p=p, io={"xo": x1_d}, prefix="s1_")
    mark("s1")
    p.barrier()
    build_attn_pre(p=p, io={"x": x1_d, "qf": qf_d, "kf": kf_d, "vf": vf_d,
                            "gate": gate_d, "gout": g_d, "bout": b_d},
                   prefix="ap_")
    mark("ap")
    p.barrier()
    build_delta(H, p=p, io={"qf": qf_d, "kf": kf_d, "vf": vf_d, "gout": g_d,
                            "bout": b_d, "oout": oo_d}, prefix="dl_")
    mark("dl")
    p.barrier()
    build_attn_post(p=p, io={"x": x1_d, "oD": oo_d, "gateF": gate_d,
                             "xo": x2_d}, prefix="po_", xc0=HALO)
    mark("po")
    p.barrier()
    build_sign_ffn(TPC, p=p, io={"x": x2_d, "xo": x3_d}, prefix="s2_")
    mark("s2")
    p.barrier()
    build_moe(p=p, io={"x": x3_d, "xo": x4_d}, prefix="m1_")
    mark("m1")
    p.barrier()
    build_lru(p=p, io={"x": x4_d, "xo": x5_d}, prefix="lr_")
    mark("lr")
    p.barrier()
    build_moe(p=p, io={"x": x5_d, "xo": out_ap}, prefix="m2_")
    mark("m2")
    return p


def _mega_inputs(inputs):
    x0 = np.asarray(inputs["L0_attn_x"], np.float32)
    slices = _fm_slices(x0, halo=HALO)
    wT = np.ascontiguousarray(np.asarray(inputs["L0_ffn_weight"], np.float32).T)
    pars = _coltiles(inputs["L0_ffn_gamma"], inputs["L0_ffn_beta"],
                     inputs["L0_ffn_bias"], inputs["L0_ffn_alpha"])
    shared = {"wT": wT, "pars_in": pars}
    shared.update(attn_pre_weights(inputs))
    shared.update(_delta_consts(H))
    shared.update(attn_post_weights(inputs))
    shared.update(moe_weights(inputs))
    shared.update(lru_weights(inputs))
    shared.update(_CONSTS)
    maps = []
    for c in range(NCORES):
        hm = np.ones((128, 4), np.float32)
        if c % 4 == 0:
            hm[:, 0:3] = 0.0
        m = dict(shared)
        m["x"] = slices[c]
        m["hmask"] = hm
        m["mSELD"] = _delta_msel(c)
        m["mSEL"] = _lru_msel(c)
        maps.append(m)
    return maps


def kernel_fused(**inputs):
    nc = _prog("mega", build_mega)
    maps = _mega_inputs(inputs)
    res = run_bass_kernel_spmd(nc, maps, core_ids=list(range(NCORES)))
    out = np.empty((B, T, D), np.float32)
    for c in range(NCORES):
        b, t0 = c // 4, (c % 4) * TPC
        out[b, t0:t0 + TPC, :] = np.asarray(res.results[c]["out"]).T
    return out


def _launch(nc, in_maps, outs):
    res = run_bass_kernel_spmd(nc, in_maps, core_ids=list(range(NCORES)))
    return [{o: np.asarray(r[o]) for o in outs} for r in res.results]


def kernel(**inputs):
    return kernel_fused(**inputs)


def kernel_unfused(**inputs):
    x0 = np.asarray(inputs["L0_attn_x"], np.float32)

    nc1 = _prog("sign_w", lambda: build_sign_ffn(W))
    wT = np.ascontiguousarray(np.asarray(inputs["L0_ffn_weight"], np.float32).T)
    pars = _coltiles(inputs["L0_ffn_gamma"], inputs["L0_ffn_beta"],
                     inputs["L0_ffn_bias"], inputs["L0_ffn_alpha"])
    slices = _fm_slices(x0, halo=HALO)
    maps = []
    for c in range(NCORES):
        m = {"x": slices[c], "wT": wT, "pars_in": pars}
        m.update(_CONSTS)
        maps.append(m)
    x1s = [r["xo"] for r in _launch(nc1, maps, ["xo"])]

    nc2 = _prog("attnpre", build_attn_pre)
    wpre = attn_pre_weights(inputs)
    maps = []
    for c in range(NCORES):
        hm = np.ones((128, 4), np.float32)
        if c % 4 == 0:
            hm[:, 0:3] = 0.0
        m = {"x": x1s[c], "hmask": hm}
        m.update(wpre)
        m.update(_CONSTS)
        maps.append(m)
    pre = _launch(nc2, maps, ["qf", "kf", "vf", "gate", "gout", "bout"])

    nc3 = _prog("delta", build_delta)
    dc = _delta_consts(H)
    maps = []
    for c in range(NCORES):
        m = {"qf": pre[c]["qf"], "kf": pre[c]["kf"], "vf": pre[c]["vf"],
             "gout": pre[c]["gout"], "bout": pre[c]["bout"],
             "mSELD": _delta_msel(c)}
        m.update(dc)
        for k in ("cIDF", "cONESR", "cMISC"):
            m[k] = _CONSTS[k]
        maps.append(m)
    oo = [r["oout"] for r in _launch(nc3, maps, ["oout"])]

    nc4 = _prog("attnpost", build_attn_post)
    wpost = attn_post_weights(inputs)
    maps = []
    for c in range(NCORES):
        m = {"x": np.ascontiguousarray(x1s[c][:, HALO:]), "oD": oo[c],
             "gateF": pre[c]["gate"]}
        m.update(wpost)
        m.update(_CONSTS)
        maps.append(m)
    x2s = [r["xo"] for r in _launch(nc4, maps, ["xo"])]

    nc5 = _prog("sign_n", lambda: build_sign_ffn(TPC))
    maps = []
    for c in range(NCORES):
        m = {"x": x2s[c], "wT": wT, "pars_in": pars}
        m.update(_CONSTS)
        maps.append(m)
    x3s = [r["xo"] for r in _launch(nc5, maps, ["xo"])]

    nc6 = _prog("moe", build_moe)
    wmoe = moe_weights(inputs)
    maps = []
    for c in range(NCORES):
        m = {"x": x3s[c]}
        m.update(wmoe)
        m.update(_CONSTS)
        maps.append(m)
    x4s = [r["xo"] for r in _launch(nc6, maps, ["xo"])]

    nc7 = _prog("lru", build_lru)
    wlru = lru_weights(inputs)
    maps = []
    for c in range(NCORES):
        m = {"x": x4s[c], "mSEL": _lru_msel(c)}
        m.update(wlru)
        m.update(_CONSTS)
        maps.append(m)
    x5s = [r["xo"] for r in _launch(nc7, maps, ["xo"])]

    maps = []
    for c in range(NCORES):
        m = {"x": x5s[c]}
        m.update(wmoe)
        m.update(_CONSTS)
        maps.append(m)
    x6s = [r["xo"] for r in _launch(nc6, maps, ["xo"])]

    out = np.empty((B, T, D), np.float32)
    for c in range(NCORES):
        b, t0 = c // 4, (c % 4) * TPC
        out[b, t0:t0 + TPC, :] = x6s[c].T
    return out
